# version 06
# baseline (speedup 1.0000x reference)
import numpy as np
import ml_dtypes
from contextlib import ExitStack

import concourse.bass as bass
import concourse.mybir as mybir
from concourse.bass_utils import run_bass_kernel_spmd

F32 = mybir.dt.float32
BF16 = mybir.dt.bfloat16
AF = mybir.ActivationFunctionType
ALU = mybir.AluOpType
AXX = mybir.AxisListType.X

D = 2048
NT = 4096
NO = 2048
NCH = D // 128
NTL = NT // 128
EPS_LN = 1e-5
EPS_RMS = 1e-6

SAFE_CHAIN = True
ATT0_IT_LIMIT = None
ATT0_NO_POST = False
FFN0_STAGE = 3
ATT1_IT_LIMIT = None
ATT1_STAGE = 2


class PhysSem:
    def __init__(self, nc, es, name):
        self.h = es.enter_context(nc.semaphore(name))
        self.total = 0


class Sem:

    def __init__(self, phys):
        self.p = phys
        self.base = phys.total
        self.v = 0

    def inc(self, instr, n=1):
        instr.then_inc(self.p.h, n)
        self.p.total += n
        self.v += n
        return instr

    def wait(self, eng, v=None):
        eng.wait_ge(self.p.h, self.base + (self.v if v is None else v))


class KB:

    def __init__(self, dbg=(), phases=None):
        self.es = ExitStack()
        self.nc = bass.Bass("TRN2", target_bir_lowering=False)
        self.dbg = set(dbg)
        self.phases = phases
        nc = self.nc
        self.te, self.ve, self.se = nc.tensor, nc.vector, nc.scalar
        self.ge, self.sy = nc.gpsimd, nc.sync
        self.sems = {}
        self._phys = {}
        self.drams = {}
        self.inputs = set()
        self._chain_n = 0
        self.ps = self.es.enter_context(nc.psum_tensor("ps_arena", [128, 4096], F32))
        self._consts = {}
        self._const_pool = self.es.enter_context(
            nc.sbuf_tensor("const_pool", [128, 64], F32))
        self._const_n = 0
        init = self.sem("init")
        self.ones_col = self.es.enter_context(nc.sbuf_tensor("ones_col", [128, 1], BF16))
        self.ones_row = self.es.enter_context(nc.sbuf_tensor("ones_row", [1, 128], BF16))
        self.ve.memset(self.ones_col[:], 1.0)
        init.inc(self.ve.memset(self.ones_row[:], 1.0))

    def constf(self, val, rows=128):
        key = (float(val), rows)
        if key not in self._consts:
            assert not getattr(self, "_init_done", False), \
                "register consts before wait_init()"
            col = self._const_n
            self._const_n += 1
            ap = self._const_pool[0:rows, col:col + 1]
            init = self.sem("init")
            init.inc(self.ve.memset(self._const_pool[:, col:col + 1], float(val)))
            self._consts[key] = ap
        return self._consts[key]

    def wait_init(self):
        self._init_done = True
        init = self.sem("init")
        for eng in (self.te, self.ve, self.se, self.ge, self.sy):
            init.wait(eng)

    def sem(self, name):
        if name in self.sems:
            return self.sems[name]
        if name.startswith("chain") or name == "init":
            role = name
        else:
            role = name.split("_", 1)[-1] if "_" in name else name
        if role not in self._phys:
            self._phys[role] = PhysSem(self.nc, self.es, role)
        self.sems[name] = Sem(self._phys[role])
        return self.sems[name]

    def bank(self, b, nb=1, rows=128, row0=0):
        return self.ps[row0:row0 + rows, 512 * b:512 * b + 512 * nb]

    def chain(self, eng_name, instr):
        if SAFE_CHAIN:
            s = self.sem(f"chain_{eng_name}")
            s.inc(instr)
            eng = getattr(self, eng_name)
            s.wait(eng)
        return instr

    def din(self, name, shape, dtype=F32):
        if name not in self.drams:
            self.inputs.add(name)
            self.drams[name] = self.nc.dram_tensor(
                name, list(shape), dtype, kind="ExternalInput").ap()
        return self.drams[name]

    def dout(self, name, shape, dtype=F32):
        if name not in self.drams:
            self.drams[name] = self.nc.dram_tensor(
                name, list(shape), dtype, kind="ExternalOutput").ap()
        return self.drams[name]

    def dram(self, name, shape, dtype=BF16):
        if name not in self.drams:
            if name in self.dbg:
                return self.dout(name, shape, dtype)
            self.drams[name] = self.nc.dram_tensor(
                name, list(shape), dtype, kind="Internal").ap()
        return self.drams[name]

    def want(self, phase):
        return self.phases is None or phase in self.phases

    def barrier(self):
        self.nc.all_engine_barrier()

    def close(self):
        self.es.close()


def emit_ln(k: KB, tag, x_ap, xn_ap):
    nc, te, ve, se, sy, ge = k.nc, k.te, k.ve, k.se, k.sy, k.ge
    ld = k.sem(f"{tag}_ld")
    va = k.sem(f"{tag}_va")
    sa = k.sem(f"{tag}_sa")
    pa = k.sem(f"{tag}_pa")
    ev = k.sem(f"{tag}_ev")
    bc = k.sem(f"{tag}_bc")
    ld2 = k.sem(f"{tag}_ld2")
    vb = k.sem(f"{tag}_vb")
    st = k.sem(f"{tag}_st")
    with (
        nc.sbuf_tensor(f"{tag}_xc", [128, 2 * NT], F32) as xc,
        nc.sbuf_tensor(f"{tag}_xb", [128, 2 * NT], BF16) as xb,
        nc.sbuf_tensor(f"{tag}_xq", [128, 2 * NT], BF16) as xq,
        nc.sbuf_tensor(f"{tag}_s0", [1, NT], F32) as sums_h,
        nc.sbuf_tensor(f"{tag}_s1", [1, NT], F32) as sumsq_h,
        nc.sbuf_tensor(f"{tag}_s2", [1, NT], F32) as mrow_h,
        nc.sbuf_tensor(f"{tag}_s3", [1, NT], F32) as v4_h,
        nc.sbuf_tensor(f"{tag}_mB", [128, NT], F32) as mB,
        nc.sbuf_tensor(f"{tag}_rB", [128, NT], F32) as rB,
        nc.sbuf_tensor(f"{tag}_t1", [128, NT], F32) as t1,
        nc.sbuf_tensor(f"{tag}_xnb", [128, 2 * NT], BF16) as xnb,
    ):
        sums, sumsq, mrow, v4 = sums_h[:], sumsq_h[:], mrow_h[:], v4_h[:]

        for c in range(NCH):
            buf = c % 2
            sl = slice(buf * NT, (buf + 1) * NT)
            if c >= 2:
                va.wait(sy, v=c - 1)
                sa.wait(sy, v=c - 1)
            ld.inc(sy.dma_start(out=xc[:, sl], in_=x_ap[c * 128:(c + 1) * 128, :]), 16)
            ld.wait(ve)
            if c >= 2:
                pa.wait(ve, v=c - 1)
            va.inc(ve.tensor_copy(xb[:, sl], xc[:, sl]))
            ld.wait(se)
            if c >= 2:
                pa.wait(se, v=c - 1)
            sa.inc(se.activation(xq[:, sl], xc[:, sl], AF.Square))
            va.wait(te)
            sa.wait(te)
            last = None
            for j in range(8):
                last = te.matmul(k.bank(j, rows=1, row0=0), k.ones_col[:],
                                 xb[:, buf * NT + j * 512: buf * NT + (j + 1) * 512],
                                 start=(c == 0), stop=(c == NCH - 1))
            for j in range(8):
                last = te.matmul(k.bank(j, rows=1, row0=32), k.ones_col[:],
                                 xq[:, buf * NT + j * 512: buf * NT + (j + 1) * 512],
                                 start=(c == 0), stop=(c == NCH - 1))
            pa.inc(last)

        pa.wait(ve)
        k.chain("ve", ve.tensor_copy(sums, k.ps[0:1, 0:4096]))
        k.chain("ve", ve.tensor_copy(sumsq, k.ps[32:33, 0:4096]))
        k.chain("ve", ve.tensor_scalar(mrow, sums, 1.0 / D, None, op0=ALU.mult))
        ev.inc(ve.tensor_scalar(v4, sumsq, 1.0 / D, None, op0=ALU.mult))
        ev.wait(se)
        sq = k.sem(f"{tag}_sq")
        sq.inc(se.activation(sums, mrow, AF.Square))
        sq.wait(ve)
        ev.inc(ve.tensor_tensor(sumsq, v4, sums, ALU.subtract))
        ev.wait(se)
        sq.inc(se.activation(v4, sumsq, AF.Sqrt, bias=k.constf(EPS_LN, rows=1)))
        sq.wait(ve)
        k.chain("ve", ve.reciprocal(sums, v4))
        mrow_bf = xb[0:1, 0:NT]
        rrow_bf = xb[0:1, NT:2 * NT]
        k.chain("ve", ve.tensor_copy(mrow_bf, mrow))
        ev.inc(ve.tensor_copy(rrow_bf, sums))
        evb = k.sem(f"{tag}_evb")
        ev.wait(te)
        for r in range(2):
            src = mrow_bf if r == 0 else rrow_bf
            dst = mB if r == 0 else rB
            if r == 1:
                evb.wait(te, v=8)
            mmw = k.sem(f"{tag}_bb")
            for j in range(8):
                mmw.inc(te.matmul(k.bank(j), k.ones_row[:],
                                  src[:, j * 512:(j + 1) * 512],
                                  start=True, stop=True))
            mmw.wait(ve)
            for j in range(8):
                evb.inc(ve.tensor_copy(dst[:, j * 512:(j + 1) * 512], k.bank(j)))
        evb.wait(sy)
        evb.wait(ve)
        for c in range(NCH):
            buf = c % 2
            sl = slice(buf * NT, (buf + 1) * NT)
            if c >= 2:
                vb.wait(sy, v=c - 1)
            ld2.inc(sy.dma_start(out=xc[:, sl], in_=x_ap[c * 128:(c + 1) * 128, :]), 16)
            ld2.wait(ve)
            if c >= 2:
                st.wait(ve, v=16 * (c - 1))
            k.chain("ve", ve.tensor_tensor(t1[:], xc[:, sl], mB[:], ALU.subtract))
            vb.inc(ve.tensor_tensor(xnb[:, sl], t1[:], rB[:], ALU.mult))
            vb.wait(sy)
            st.inc(sy.dma_start(out=xn_ap[c * 128:(c + 1) * 128, :], in_=xnb[:, sl]), 16)
        st.wait(sy)


def emit_qkv0(k: KB, tag, xn_ap, wq_ap, wk_ap, wv_ap, q0_ap, k0_ap, v0_ap,
              bnd_ap):
    nc, te, ve, se, sy, ge = k.nc, k.te, k.ve, k.se, k.sy, k.ge
    xl = k.sem(f"{tag}_xl")
    wl = k.sem(f"{tag}_wl")
    smm = k.sem(f"{tag}_mm")
    sev = k.sem(f"{tag}_ev")
    ssq = k.sem(f"{tag}_sq")
    snm = k.sem(f"{tag}_nm")
    smx = k.sem(f"{tag}_mx")
    sst = k.sem(f"{tag}_st")
    with (
        nc.sbuf_tensor(f"{tag}_xn", [128, NCH * NT], BF16) as xn_sb,
        nc.sbuf_tensor(f"{tag}_wb", [128, 2 * NCH * 512], BF16) as wb,
        nc.sbuf_tensor(f"{tag}_evt", [128, 2 * 512], BF16) as evt,
        nc.sbuf_tensor(f"{tag}_sqt", [128, 2 * 512], BF16) as sqt,
        nc.sbuf_tensor(f"{tag}_mxr", [1, 512], F32) as mxr,
    ):
        for c in range(NCH):
            xl.inc(sy.dma_start(out=xn_sb[:, c * NT:(c + 1) * NT],
                                in_=xn_ap[c * 128:(c + 1) * 128, :]), 16)
        xl.wait(te)

        it = 0
        nqk_blocks = [(wq_ap, q0_ap, fb) for fb in range(8)] + \
                     [(wk_ap, k0_ap, fb) for fb in range(8)]
        for bi, (w_ap, dst_ap, fb) in enumerate(nqk_blocks):
            wbuf = bi % 2
            if bi >= 2:
                smm.wait(ge, v=(bi - 1) * 32)
            src = w_ap[:, fb * 512:(fb + 1) * 512]
            src = src.rearrange("(dc p) f -> p dc f", p=128)
            dst3 = wb[:, wbuf * NCH * 512:(wbuf + 1) * NCH * 512].rearrange(
                "p (dc f) -> p dc f", dc=NCH)
            wl.inc(ge.dma_start(out=dst3, in_=src), 16)
            for fc in range(4):
                for tb in range(8):
                    bank = it % 2
                    ebuf = it % 2
                    wl.wait(te, v=(bi + 1) * 16)
                    if it >= 2:
                        sev.wait(te, v=it - 1)
                    last = None
                    for dc in range(NCH):
                        lhsT = wb[:, wbuf * NCH * 512 + dc * 512 + fc * 128:
                                  wbuf * NCH * 512 + dc * 512 + fc * 128 + 128]
                        rhs = xn_sb[:, dc * NT + tb * 512: dc * NT + (tb + 1) * 512]
                        last = te.matmul(k.bank(bank), lhsT, rhs,
                                         start=(dc == 0), stop=(dc == NCH - 1))
                    smm.inc(last)
                    smm.wait(se, v=it + 1)
                    if it >= 2:
                        sst.wait(se, v=16 * (it - 1))
                        ssq.wait(se, v=it - 1)
                    sev.inc(se.copy(evt[:, ebuf * 512:(ebuf + 1) * 512], k.bank(bank)))
                    sev.wait(ve, v=it + 1)
                    if it >= 2:
                        snm.wait(ve, v=it - 1)
                    ssq.inc(ve.tensor_tensor(sqt[:, ebuf * 512:(ebuf + 1) * 512],
                                             evt[:, ebuf * 512:(ebuf + 1) * 512],
                                             evt[:, ebuf * 512:(ebuf + 1) * 512],
                                             ALU.mult))
                    ssq.wait(te, v=it + 1)
                    if it >= 2:
                        smx.wait(te, v=it - 1)
                    snm.inc(te.matmul(k.bank(2 + bank, rows=1), k.ones_col[:],
                                      sqt[:, ebuf * 512:(ebuf + 1) * 512],
                                      start=True, stop=True))
                    snm.wait(ve, v=it + 1)
                    chunk_id = bi * 4 + fc
                    smx.inc(ve.reduce_max(mxr[:, chunk_id * 8 + tb:
                                              chunk_id * 8 + tb + 1],
                                          k.bank(2 + bank, rows=1), axis=AXX))
                    sev.wait(sy, v=it + 1)
                    sst.inc(sy.dma_start(
                        out=dst_ap[fb * 512 + fc * 128: fb * 512 + (fc + 1) * 128,
                                   tb * 512:(tb + 1) * 512],
                        in_=evt[:, ebuf * 512:(ebuf + 1) * 512]), 16)
                    it += 1
        smx.wait(sy)
        sst.inc(sy.dma_start(out=bnd_ap, in_=mxr[:]), 16)

        st_snap = {}
        for vb in range(4):
            wbuf = vb % 2
            if vb >= 2:
                smm.wait(ge, v=len(nqk_blocks) * 32 + (vb - 1) * 32)
            else:
                smm.wait(ge, v=len(nqk_blocks) * 32)
            src = wv_ap[:, vb * 512:(vb + 1) * 512]
            src = src.rearrange("(dc p) f -> p dc f", p=128)
            dst3 = wb[:, wbuf * NCH * 512:(wbuf + 1) * NCH * 512].rearrange(
                "p (dc f) -> p dc f", dc=NCH)
            wl.inc(ge.dma_start(out=dst3, in_=src), 16)
            for tt in range(NTL):
                vit = vb * NTL + tt
                bank = it % 2
                ebuf = it % 2
                wl.wait(te, v=(len(nqk_blocks) + vb + 1) * 16)
                if it >= 2:
                    sev.wait(te, v=it - 1)
                last = None
                for dc in range(NCH):
                    lhsT = xn_sb[:, dc * NT + tt * 128: dc * NT + (tt + 1) * 128]
                    rhs = wb[:, wbuf * NCH * 512 + dc * 512:
                             wbuf * NCH * 512 + (dc + 1) * 512]
                    last = te.matmul(k.bank(bank), lhsT, rhs,
                                     start=(dc == 0), stop=(dc == NCH - 1))
                smm.inc(last)
                smm.wait(se, v=it + 1)
                if vit >= 2:
                    sst.wait(se, v=st_snap[vit - 2])
                else:
                    sst.wait(se, v=512 * 16 + 16)
                sev.inc(se.copy(evt[:, ebuf * 512:(ebuf + 1) * 512], k.bank(bank)))
                sev.wait(sy, v=it + 1)
                for hh in range(4):
                    h = vb * 4 + hh
                    sst.inc(sy.dma_start(
                        out=v0_ap[h * 128:(h + 1) * 128, tt * 128:(tt + 1) * 128],
                        in_=evt[:, ebuf * 512 + hh * 128:
                                ebuf * 512 + (hh + 1) * 128]), 16)
                st_snap[vit] = sst.v
                it += 1
        sst.wait(sy)


def emit_att0(k: KB, tag, q0_ap, k0_ap, v0_ap, bnd_ap, lam_ap, s0t_ap,
              factor_ap, rms_ap):
    nc, te, ve, se, sy, ge = k.nc, k.te, k.ve, k.se, k.sy, k.ge
    NH, SB = 16, 8
    NIT = NH * SB
    ld = k.sem(f"{tag}_ld")
    lq = k.sem(f"{tag}_lq")
    pmm = k.sem(f"{tag}_pmm")
    aex = k.sem(f"{tag}_aex")
    psu = k.sem(f"{tag}_psu")
    vsr = k.sem(f"{tag}_vsr")
    pb12 = k.sem(f"{tag}_pb12")
    aev = k.sem(f"{tag}_aev")
    vdf = k.sem(f"{tag}_vdf")
    ae3 = k.sem(f"{tag}_ae3")
    pav = k.sem(f"{tag}_pav")
    ps3 = k.sem(f"{tag}_ps3")
    vs3 = k.sem(f"{tag}_vs3")
    pb3 = k.sem(f"{tag}_pb3")
    rdm = k.sem(f"{tag}_rdm")
    vout = k.sem(f"{tag}_vout")
    vsq = k.sem(f"{tag}_vsq")
    prm = k.sem(f"{tag}_prm")
    vrm = k.sem(f"{tag}_vrm")
    sst = k.sem(f"{tag}_sst")
    vpr = k.sem(f"{tag}_vpr")
    asr = k.sem(f"{tag}_asr")
    snap = {}

    def put(name, i, sem):
        snap[(name, i)] = sem.v

    def at(name, i):
        return snap[(name, i)]

    from contextlib import ExitStack as _ES
    _les = _ES()

    def _sb(name, shape, dt):
        return _les.enter_context(nc.sbuf_tensor(name, shape, dt))

    E1 = _sb(f"{tag}_E1", [128, 2 * 16384], BF16)
    E2 = _sb(f"{tag}_E2", [128, 2 * 16384], BF16)
    Wq = _sb(f"{tag}_Wq", [128, 4096], BF16)
    K1 = _sb(f"{tag}_K1", [128, NT], BF16)
    K2 = _sb(f"{tag}_K2", [128, NT], BF16)
    Vt = _sb(f"{tag}_V", [128, NT], BF16)
    qt = _sb(f"{tag}_q", [128, 4 * 512], BF16)
    Rs = _sb(f"{tag}_Rs", [128, 2 * 1536], BF16)
    ob = _sb(f"{tag}_ob", [128, 2 * 512], BF16)
    sq2b = _sb(f"{tag}_sq2", [128, 2 * 512], BF16)
    sr = _sb(f"{tag}_sr", [1, 512], F32)[:]
    rr = _sb(f"{tag}_rr", [1, 512], F32)[:]
    rb = _sb(f"{tag}_rb", [1, 2 * 1024], BF16)
    rb2f = _sb(f"{tag}_r2f", [1, 512], F32)[:]
    s3r = _sb(f"{tag}_s3", [1, 2 * 512], F32)
    b3r = _sb(f"{tag}_b3", [1, 2 * 512], BF16)
    st1 = _sb(f"{tag}_st1", [1, 512], F32)[:]
    pr = _sb(f"{tag}_pr", [1, 640], F32)[:]
    prb = _sb(f"{tag}_prb", [1, 64], BF16)[:]
    biasB = _sb(f"{tag}_bias", [128, 64], F32)[:]

    bnd = pr[:, 0:512]
    n2 = pr[:, 512:576]
    nrm = pr[:, 576:640]
    lq.inc(sy.dma_start(out=bnd, in_=bnd_ap), 16)
    lam_sb = biasB[0:1, 48:64]
    lq.inc(sy.dma_start(out=lam_sb, in_=lam_ap), 16)
    lq.wait(ve)
    for cid in range(64):
        k.chain("ve", ve.reduce_max(n2[:, cid:cid + 1],
                                    bnd[:, cid * 8:(cid + 1) * 8], axis=AXX))
    vpr.inc(ve.tensor_copy(nrm, n2))
    vpr.wait(se)
    aprep = k.sem(f"{tag}_aprep")
    aprep.inc(se.activation(nrm, n2, AF.Sqrt))
    aprep.wait(ve)
    k.chain("ve", ve.tensor_tensor(pr[:, 0:32], nrm[:, 0:32], nrm[:, 32:64],
                                   ALU.mult))
    k.chain("ve", ve.tensor_scalar(pr[:, 0:32], pr[:, 0:32],
                                   -1.0 / float(np.sqrt(128.0)), None,
                                   op0=ALU.mult))
    k.chain("ve", ve.tensor_scalar(pr[:, 64:80], lam_sb, -1.0, None,
                                   op0=ALU.mult))
    vpr.inc(ve.tensor_copy(pr[:, 80:96], pr[:, 64:80]))
    vpr.wait(se)
    aprep.inc(se.activation(pr[:, 64:80], pr[:, 80:96], AF.Relu))
    aprep.wait(ve)
    k.chain("ve", ve.tensor_scalar(pr[:, 32:48], pr[:, 64:80], -1.0, -1.0,
                                   op0=ALU.mult, op1=ALU.add))
    mlap = biasB[0:1, 62:63]
    k.chain("ve", ve.reduce_sum(pr[:, 96:97], lam_sb, axis=AXX))
    k.chain("ve", ve.tensor_scalar(mlap, pr[:, 96:97], -1.0 / 16.0,
                                   1.0, op0=ALU.mult, op1=ALU.add))
    vpr.inc(ve.tensor_copy(prb[:, 0:48], pr[:, 0:48]))
    vpr.wait(te)
    pprep = k.sem(f"{tag}_pprep")
    pprep.inc(te.matmul(k.bank(0)[:, 0:48], k.ones_row[:], prb[:, 0:48],
                        start=True, stop=True))
    pprep.wait(ve)
    vpr.inc(ve.tensor_copy(biasB[:, 0:48], k.bank(0)[:, 0:48]))
    vpr.inc(ve.memset(st1, 0.0))
    vpr.wait(ge)
    for sb_ in range(8):
        rdm.inc(ge.dma_start(out=rms_ap[0:1, sb_ * 512:(sb_ + 1) * 512],
                             in_=st1), 16)
    put("rdm", -1, rdm)
    rdm.wait(ve)
    vpr.wait(te)
    vpr.wait(se)
    vpr.wait(sy)

    def emit_e3(j):
        h = j // SB
        P = j % 2
        eP = 16384 * P
        vdf.wait(se, v=at("vdf", j))
        lastq = None
        for q_ in range(4):
            qs = slice(eP + q_ * 4096, eP + (q_ + 1) * 4096)
            lastq = se.activation(E1[:, qs], E2[:, qs], AF.Exp,
                                  bias=biasB[:, 32 + h:33 + h])
            if q_ < 3:
                k.chain("se", lastq)
        ae3.inc(lastq)
        put("ae3", j, ae3)

    def emit_post(j):
        h, sb = j // SB, j % SB
        P = j % 2
        eP = 16384 * P
        AVB = 4 + P
        RB = 6 + P
        ae3.wait(te, v=at("ae3", j))
        if j >= 2:
            vout.wait(te, v=at("vout", j - 2))
        ld.wait(te, v=at("ldv", j // SB))
        last = None
        for tt in range(NTL):
            last = te.matmul(k.bank(AVB), Vt[:, tt * 128:(tt + 1) * 128],
                             E1[:, eP + tt * 512:eP + (tt + 1) * 512],
                             start=(tt == 0), stop=(tt == NTL - 1))
        pav.inc(last)
        put("pav", j, pav)
        aev.wait(te, v=at("aevR2", j))
        last = None
        for tt in range(NTL):
            last = te.matmul(k.bank(RB, rows=1, row0=64), k.ones_col[:],
                             E1[:, eP + tt * 512:eP + (tt + 1) * 512],
                             start=(tt == 0), stop=(tt == NTL - 1))
        ps3.inc(last)
        put("ps3", j, ps3)
        ps3.wait(se, v=at("ps3", j))
        k.chain("se", se.copy(s3r[:, P * 512:(P + 1) * 512],
                              k.ps[64:65, 512 * RB:512 * RB + 512]))
        k.chain("se", se.activation(rr, s3r[:, P * 512:(P + 1) * 512], AF.Ln))
        vs3.inc(se.activation(b3r[:, P * 512:(P + 1) * 512], rr, AF.Exp,
                              scale=-1.0))
        put("vs3", j, vs3)
        vs3.wait(te, v=at("vs3", j))
        pb3.inc(te.matmul(k.bank(RB), k.ones_row[:],
                          b3r[:, P * 512:(P + 1) * 512],
                          start=True, stop=True))
        put("pb3", j, pb3)
        pb3.wait(se, v=at("pb3", j))
        aev.inc(se.copy(Rs[:, P * 1536 + 1024:P * 1536 + 1536], k.bank(RB)))
        put("aevR3", j, aev)
        pav.wait(ve, v=at("pav", j))
        aev.wait(ve, v=at("aevR3", j))
        if j >= 2:
            sst.wait(ve, v=at("sst", j - 2))
        vout.inc(ve.tensor_tensor(ob[:, 512 * P:512 * (P + 1)], k.bank(AVB),
                                  Rs[:, P * 1536 + 1024:P * 1536 + 1536],
                                  ALU.mult))
        put("vout", j, vout)
        if j >= 2:
            prm.wait(ve, v=at("prm", j - 2))
        vsq.inc(ve.tensor_tensor(sq2b[:, 512 * P:512 * (P + 1)],
                                 ob[:, 512 * P:512 * (P + 1)],
                                 ob[:, 512 * P:512 * (P + 1)], ALU.mult))
        put("vsq", j, vsq)
        vsq.wait(te, v=at("vsq", j))
        aev.wait(te, v=at("aevR3", j))
        prm.inc(te.matmul(k.bank(RB, rows=1, row0=32), k.ones_col[:],
                          sq2b[:, 512 * P:512 * (P + 1)],
                          start=True, stop=True))
        put("prm", j, prm)
        prm.wait(ve, v=at("prm", j))
        rdm.wait(ve, v=at("rdm", j - 1))
        vrm.inc(ve.tensor_copy(st1, k.ps[32:33, 512 * RB:512 * RB + 512]))
        put("vrm", j, vrm)
        vrm.wait(ge, v=at("vrm", j))
        rdm.inc(ge.dma_start(out=rms_ap[0:1, sb * 512:(sb + 1) * 512],
                             in_=st1, accum_op=ALU.add), 16)
        put("rdm", j, rdm)
        vout.wait(sy, v=at("vout", j))
        sst.inc(sy.dma_start(
            out=s0t_ap[h * 128:(h + 1) * 128, sb * 512:(sb + 1) * 512],
            in_=ob[:, 512 * P:512 * (P + 1)]), 16)
        put("sst", j, sst)

    NITL = min(NIT, ATT0_IT_LIMIT) if ATT0_IT_LIMIT else NIT
    for i in range(NITL):
        h, sb = i // SB, i % SB
        P = i % 2
        eP = 16384 * P
        RB = 6 + P
        if sb == 0:
            if h > 0:
                pmm.wait(sy, v=at("pmm", i - 1))
            ld.inc(sy.dma_start(out=K1[:],
                                in_=k0_ap[h * 256:h * 256 + 128, :]), 16)
            ld.inc(sy.dma_start(out=K2[:],
                                in_=k0_ap[h * 256 + 128:h * 256 + 256, :]), 16)
            if h == 0:
                ld.inc(sy.dma_start(out=Vt[:], in_=v0_ap[0:128, :]), 16)
                put("ldv", 0, ld)
        if i >= 2:
            pmm.wait(sy, v=at("pmm", i - 1))
        lq.inc(sy.dma_start(
            out=qt[:, P * 1024:P * 1024 + 512],
            in_=q0_ap[h * 256:h * 256 + 128, sb * 512:(sb + 1) * 512]), 16)
        lq.inc(sy.dma_start(
            out=qt[:, P * 1024 + 512:P * 1024 + 1024],
            in_=q0_ap[h * 256 + 128:h * 256 + 256, sb * 512:(sb + 1) * 512]),
            16)

        ld.wait(te)
        lq.wait(te)
        for u in range(32):
            sc = u % 2
            g = u // 2
            reg = u % 2
            Ksb = K1 if sc == 0 else K2
            qv = qt[:, P * 1024 + sc * 512: P * 1024 + (sc + 1) * 512]
            wv_ = 32 * i + u - 1
            if wv_ > 0:
                aex.wait(te, v=wv_)
            last = None
            for tj in range(2):
                tt = g * 2 + tj
                last = te.matmul(k.bank(2 * reg + tj),
                                 Ksb[:, tt * 128:(tt + 1) * 128], qv,
                                 start=True, stop=True)
            pmm.inc(last)
        put("pmm", i, pmm)

        if i >= 1 and not ATT0_NO_POST:
            emit_e3(i - 1)
        hc1 = h * 2
        for u in range(32):
            sc = u % 2
            g = u // 2
            reg = u % 2
            pmm.wait(se, v=at("pmm", i) - 32 + u + 1)
            dstE = E1 if sc == 0 else E2
            if i >= 2 and u < 2:
                vdf.wait(se, v=at("vdf", i - 2))
                psu.wait(se, v=at("psu", i - 2))
                if ("ps3", i - 2) in snap:
                    pav.wait(se, v=at("pav", i - 2))
                    ps3.wait(se, v=at("ps3", i - 2))
            aex.inc(se.activation(
                dstE[:, eP + g * 1024: eP + (g + 1) * 1024],
                k.bank(2 * reg, nb=2), AF.Exp,
                bias=biasB[:, hc1 + sc:hc1 + sc + 1],
                scale=float(1.0 / np.sqrt(128.0))))
        put("aexe", i, aex)

        if i >= 1 and not ATT0_NO_POST:
            emit_post(i - 1)
        if sb == 0 and h > 0:
            if ("pav", i - 1) in snap:
                pav.wait(sy, v=at("pav", i - 1))
            ld.inc(sy.dma_start(out=Vt[:],
                                in_=v0_ap[h * 128:(h + 1) * 128, :]), 16)
            put("ldv", h, ld)

        aex.wait(te)
        if i >= 2:
            vrm.wait(te, v=at("vrm", i - 2))
        first = True
        last = None
        for tt in range(NTL):
            last = te.matmul(k.bank(RB, rows=1), k.ones_col[:],
                             E1[:, eP + tt * 512:eP + (tt + 1) * 512],
                             start=first, stop=False)
            first = False
            last = te.matmul(k.bank(RB, rows=1, row0=32), k.ones_col[:],
                             E2[:, eP + tt * 512:eP + (tt + 1) * 512],
                             start=False, stop=(tt == NTL - 1))
        psu.inc(last)
        put("psu", i, psu)
        psu.wait(se, v=at("psu", i))
        k.chain("se", se.copy(sr, k.ps[0:1, 512 * RB:512 * RB + 512]))
        k.chain("se", se.activation(rr, sr, AF.Ln))
        k.chain("se", se.activation(rb[0:1, P * 1024:P * 1024 + 512], rr,
                                    AF.Exp, scale=-1.0))
        k.chain("se", se.copy(sr, k.ps[32:33, 512 * RB:512 * RB + 512]))
        k.chain("se", se.activation(rr, sr, AF.Ln))
        asr.inc(se.activation(rb2f, rr, AF.Exp, scale=-1.0))
        put("asr", i, asr)
        asr.wait(ve, v=at("asr", i))
        vsr.inc(ve.tensor_scalar(rb[0:1, P * 1024 + 512:P * 1024 + 1024], rb2f,
                                 lam_sb[:, h:h + 1], None, op0=ALU.mult))
        put("vsr", i, vsr)
        vsr.wait(te, v=at("vsr", i))
        pb12.inc(te.matmul(k.bank(RB), k.ones_row[:],
                           rb[0:1, P * 1024:P * 1024 + 512],
                           start=True, stop=True))
        pb12.wait(se)
        aev.inc(se.copy(Rs[:, P * 1536:P * 1536 + 512], k.bank(RB)))
        put("aevR1", i, aev)
        aev.wait(te, v=at("aevR1", i))
        pb12.inc(te.matmul(k.bank(RB), k.ones_row[:],
                           rb[0:1, P * 1024 + 512:P * 1024 + 1024],
                           start=True, stop=True))
        pb12.wait(se)
        aev.inc(se.copy(Rs[:, P * 1536 + 512:P * 1536 + 1024], k.bank(RB)))
        put("aevR2", i, aev)
        put("pb12", i, pb12)

        aev.wait(ve, v=at("aevR2", i))
        R1v = Rs[:, P * 1536:P * 1536 + 512]
        R2v = Rs[:, P * 1536 + 512:P * 1536 + 1024]
        last = None
        for q_ in range(4):
            for tt in range(q_ * 8, (q_ + 1) * 8):
                sl = slice(eP + tt * 512, eP + (tt + 1) * 512)
                wsl = slice((tt - q_ * 8) * 512, (tt - q_ * 8 + 1) * 512)
                k.chain("ve", ve.tensor_tensor(Wq[:, wsl], E1[:, sl], R1v,
                                               ALU.mult))
            for tt in range(q_ * 8, (q_ + 1) * 8):
                sl = slice(eP + tt * 512, eP + (tt + 1) * 512)
                k.chain("ve", ve.tensor_tensor(E1[:, sl], E2[:, sl], R2v,
                                               ALU.mult))
            for tt in range(q_ * 8, (q_ + 1) * 8):
                sl = slice(eP + tt * 512, eP + (tt + 1) * 512)
                wsl = slice((tt - q_ * 8) * 512, (tt - q_ * 8 + 1) * 512)
                last = ve.tensor_tensor(E2[:, sl], Wq[:, wsl], E1[:, sl],
                                        ALU.subtract)
                if not (q_ == 3 and tt == (q_ + 1) * 8 - 1):
                    k.chain("ve", last)
        vdf.inc(last)
        put("vdf", i, vdf)

    if not ATT0_NO_POST:
        emit_e3(NITL - 1)
        emit_post(NITL - 1)

    rdm.wait(sy)
    lf = k.sem(f"{tag}_lf")
    vvf = k.sem(f"{tag}_vvf")
    afc = k.sem(f"{tag}_afc")
    for cch in range(8):
        csl = slice(cch * 512, (cch + 1) * 512)
        if cch >= 1:
            sst.wait(sy, v=at("fst", cch - 1))
        lf.inc(sy.dma_start(out=rr, in_=rms_ap[0:1, csl]), 16)
        lf.wait(ve)
        k.chain("ve", ve.tensor_scalar(sr, rr, 1.0 / D, None, op0=ALU.mult))
        vvf.inc(ve.tensor_copy(rr, sr))
        vvf.wait(se)
        afc.inc(se.activation(sr, rr, AF.Sqrt, bias=k.constf(EPS_RMS, rows=1)))
        afc.wait(ve)
        k.chain("ve", ve.reciprocal(rr, sr))
        vvf.inc(ve.tensor_scalar(sr, rr, mlap, None, op0=ALU.mult))
        vvf.wait(sy)
        sst.inc(sy.dma_start(out=factor_ap[0:1, csl], in_=sr), 16)
        snap[("fst", cch)] = sst.v
    sst.wait(sy)
    _les.close()


def emit_wo0(k: KB, tag, s0t_ap, factor_ap, wo_ap, ns_ap, s0_ap, n_tok=NT,
             d_out=D, use_ns=True, use_factor=True):
    nc, te, ve, se, sy, ge = k.nc, k.te, k.ve, k.se, k.sy, k.ge
    wl = k.sem(f"{tag}_wl")
    pl = k.sem(f"{tag}_pl")
    fl = k.sem(f"{tag}_fl")
    vsc = k.sem(f"{tag}_vsc")
    pmm = k.sem(f"{tag}_pmm")
    aev = k.sem(f"{tag}_aev")
    pfb = k.sem(f"{tag}_pfb")
    sst = k.sem(f"{tag}_sst")
    NOC = d_out // 128
    NMAC = n_tok // 2048
    snap = {}
    with (
        nc.sbuf_tensor(f"{tag}_wo", [128, NCH * d_out], BF16) as wo_sb,
        nc.sbuf_tensor(f"{tag}_pan", [128, NCH * 2048], BF16) as pan,
        nc.sbuf_tensor(f"{tag}_sc", [128, 2 * NCH * 512], BF16) as scl,
        nc.sbuf_tensor(f"{tag}_fb", [128, 2048], BF16) as fb,
        nc.sbuf_tensor(f"{tag}_fr", [1, 2048], F32) as fr_h,
        nc.sbuf_tensor(f"{tag}_frb", [1, 2048], BF16) as frb_h,
        nc.sbuf_tensor(f"{tag}_ns", [128, 16], F32) as ns_sb,
        nc.sbuf_tensor(f"{tag}_ev", [128, 2 * 512], BF16) as evt,
    ):
        fr, frb = fr_h[:], frb_h[:]
        for fc in range(NCH):
            wl.inc(ge.dma_start(out=wo_sb[:, fc * d_out:(fc + 1) * d_out],
                                in_=wo_ap[fc * 128:(fc + 1) * 128, :]), 16)
        if use_ns:
            wl.inc(ge.dma_start(out=ns_sb[:], in_=ns_ap), 16)
        nit = 0
        noc_it = 0
        for mac in range(NMAC):
            msl = slice(mac * 2048, (mac + 1) * 2048)
            if mac > 0:
                pmm.wait(sy, v=snap[("pmm_mac", mac - 1)])
            for fc in range(NCH):
                pl.inc(sy.dma_start(out=pan[:, fc * 2048:(fc + 1) * 2048],
                                    in_=s0t_ap[fc * 128:(fc + 1) * 128, msl]), 16)
            if use_factor:
                fl.inc(sy.dma_start(out=fr, in_=factor_ap[0:1, msl]), 16)
                fl.wait(ve)
                if mac > 0:
                    vsc.wait(ve, v=snap[("vsc_mac", mac - 1)])
                k.chain("ve", ve.tensor_copy(frb, fr))
                vsc.inc(ve.tensor_copy(frb[:, 0:1], fr[:, 0:1]))
                vsc.wait(te)
                lastb = None
                for j in range(4):
                    lastb = te.matmul(k.bank(4 + j), k.ones_row[:],
                                      frb[:, j * 512:(j + 1) * 512],
                                      start=True, stop=True)
                pfb.inc(lastb)
                pfb.wait(ve)
                for j in range(4):
                    k.chain("ve", ve.tensor_copy(fb[:, j * 512:(j + 1) * 512],
                                                 k.bank(4 + j)))
            for nb in range(4):
                nbs = slice(nb * 512, (nb + 1) * 512)
                sb_ = nit % 2
                ssl0 = sb_ * NCH * 512
                pl.wait(ve)
                if nit >= 2:
                    pmm.wait(ve, v=snap[("pmm_nit", nit - 2)])
                last = None
                for fc in range(NCH):
                    src = pan[:, fc * 2048 + nb * 512: fc * 2048 + (nb + 1) * 512]
                    dst = scl[:, ssl0 + fc * 512:ssl0 + (fc + 1) * 512]
                    if use_ns:
                        k.chain("ve", ve.tensor_scalar(
                            dst, src, ns_sb[:, fc:fc + 1], None, op0=ALU.mult))
                        if use_factor:
                            last = ve.tensor_tensor(dst, dst, fb[:, nbs], ALU.mult)
                    elif use_factor:
                        last = ve.tensor_tensor(dst, src, fb[:, nbs], ALU.mult)
                    else:
                        last = ve.tensor_copy(dst, src)
                    if fc < NCH - 1:
                        k.chain("ve", last)
                vsc.inc(last)
                snap[("vsc_nit", nit)] = vsc.v
                vsc.wait(te)
                wl.wait(te)
                for oc in range(NOC):
                    bank = noc_it % 2
                    if noc_it >= 2:
                        aev.wait(te, v=noc_it - 1)
                    last = None
                    for fc in range(NCH):
                        last = te.matmul(
                            k.bank(bank),
                            wo_sb[:, fc * d_out + oc * 128:
                                  fc * d_out + (oc + 1) * 128],
                            scl[:, ssl0 + fc * 512:ssl0 + (fc + 1) * 512],
                            start=(fc == 0), stop=(fc == NCH - 1))
                    pmm.inc(last)
                    pmm.wait(se)
                    if noc_it >= 2:
                        sst.wait(se, v=16 * (noc_it - 1))
                    aev.inc(se.copy(evt[:, bank * 512:(bank + 1) * 512],
                                    k.bank(bank)))
                    aev.wait(sy)
                    sst.inc(sy.dma_start(
                        out=s0_ap[oc * 128:(oc + 1) * 128,
                                  mac * 2048 + nb * 512:
                                  mac * 2048 + (nb + 1) * 512],
                        in_=evt[:, bank * 512:(bank + 1) * 512]), 16)
                    noc_it += 1
                snap[("pmm_nit", nit)] = pmm.v
                nit += 1
            snap[("pmm_mac", mac)] = pmm.v
            snap[("vsc_mac", mac)] = vsc.v
        sst.wait(sy)


def emit_ffn0(k: KB, tag, xn_ap, wT_ap, a_ap, b_ap, rt_ap, rep_ap, id_ap,
              g0_ap):
    nc, te, ve, se, sy, ge = k.nc, k.te, k.ve, k.se, k.sy, k.ge
    wl = k.sem(f"{tag}_wl")
    pl = k.sem(f"{tag}_pl")
    prt = k.sem(f"{tag}_prt")
    vrt = k.sem(f"{tag}_vrt")
    art = k.sem(f"{tag}_art")
    ptr = k.sem(f"{tag}_ptr")
    aev = k.sem(f"{tag}_aev")
    prp = k.sem(f"{tag}_prp")
    pax = k.sem(f"{tag}_pax")
    vax = k.sem(f"{tag}_vax")
    pmm = k.sem(f"{tag}_pmm")
    asl = k.sem(f"{tag}_asl")
    sst = k.sem(f"{tag}_sst")
    snap = {}
    with (
        nc.sbuf_tensor(f"{tag}_xn", [128, NCH * 2048], BF16) as xp,
        nc.sbuf_tensor(f"{tag}_wt", [128, NCH * D], BF16) as wt,
        nc.sbuf_tensor(f"{tag}_A", [128, NCH * 128], BF16) as A_sb,
        nc.sbuf_tensor(f"{tag}_B", [128, D], BF16) as B_sb,
        nc.sbuf_tensor(f"{tag}_B2", [128, D], BF16) as B2,
        nc.sbuf_tensor(f"{tag}_rt", [128, NCH * 8], BF16) as rt_sb,
        nc.sbuf_tensor(f"{tag}_rep", [8, 128], BF16) as rep_sb,
        nc.sbuf_tensor(f"{tag}_id", [128, 128], BF16) as id_sb,
        nc.sbuf_tensor(f"{tag}_wfm", [8, 2048], BF16) as wfm,
        nc.sbuf_tensor(f"{tag}_wrp", [128, 2048], BF16) as wrp,
        nc.sbuf_tensor(f"{tag}_axw", [128, 2048], BF16) as axw,
        nc.sbuf_tensor(f"{tag}_r1", [128, 64], F32) as r1_h,
        nc.sbuf_tensor(f"{tag}_rb1", [128, 8], BF16) as rb1_h,
        nc.sbuf_tensor(f"{tag}_ev", [128, 2 * 512], BF16) as evt,
    ):
        r1 = r1_h[:]
        rb1 = rb1_h[:]
        for fc in range(NCH):
            wl.inc(ge.dma_start(out=wt[:, fc * D:(fc + 1) * D],
                                in_=wT_ap[fc * 128:(fc + 1) * 128, :]), 16)
        wl.inc(ge.dma_start(out=A_sb[:].rearrange("p (c f) -> p c f", c=NCH),
                            in_=a_ap.rearrange("(c p) f -> p c f", p=128)), 16)
        wl.inc(ge.dma_start(out=B_sb[:], in_=b_ap), 16)
        wl.inc(ge.dma_start(out=rt_sb[:].rearrange("p (c f) -> p c f", c=NCH),
                            in_=rt_ap.rearrange("(c p) f -> p c f", p=128)), 16)
        wl.inc(ge.dma_start(out=rep_sb[:], in_=rep_ap), 16)
        wl.inc(ge.dma_start(out=id_sb[:], in_=id_ap), 16)
        wl.wait(ve)
        vB2 = k.sem(f"{tag}_vB2")
        vB2.inc(ve.tensor_scalar(B2[:], B_sb[:], 2.0, None, op0=ALU.mult))
        vB2.wait(te)

        for mac in range(NT // 2048):
            msl = slice(mac * 2048, (mac + 1) * 2048)
            if mac > 0:
                if ("pmm_mac", mac - 1) in snap:
                    pmm.wait(sy, v=snap[("pmm_mac", mac - 1)])
                if ("pax_mac", mac - 1) in snap:
                    pax.wait(sy, v=snap[("pax_mac", mac - 1)])
                prt.wait(sy, v=snap[("prt_mac", mac - 1)] if
                         ("prt_mac", mac - 1) in snap else prt.v)
            for fc in range(NCH):
                pl.inc(sy.dma_start(out=xp[:, fc * 2048:(fc + 1) * 2048],
                                    in_=xn_ap[fc * 128:(fc + 1) * 128, msl]), 16)
            pl.wait(te)
            wl.wait(te)
            for tt in range(16):
                if mac + tt > 0:
                    prev = (mac, tt - 1) if tt else (mac - 1, 15)
                    vrt.wait(te, v=snap[("vrt", prev)])
                    art.wait(te, v=snap[("art", prev)])
                last = None
                for fc in range(NCH):
                    last = te.matmul(k.bank(0)[:, 0:8],
                                     xp[:, fc * 2048 + tt * 128:
                                        fc * 2048 + (tt + 1) * 128],
                                     rt_sb[:, fc * 8:(fc + 1) * 8],
                                     start=(fc == 0), stop=(fc == NCH - 1))
                prt.inc(last)
                prt.wait(ve)
                e_t = r1[:, 0:8]
                probs = r1[:, 8:16]
                eq1 = r1[:, 16:24]
                pm = r1[:, 24:32]
                eq2 = r1[:, 32:40]
                wsel = r1[:, 40:48]
                eq1s = r1[:, 56:64]
                mx = r1[:, 48:49]
                mxn = r1[:, 49:50]
                sm = r1[:, 50:51]
                rs = r1[:, 51:52]
                m1 = r1[:, 52:53]
                m2 = r1[:, 53:54]
                ws = r1[:, 54:55]
                rw = r1[:, 55:56]
                k.chain("ve", ve.reduce_max(mx, k.bank(0)[:, 0:8], axis=AXX))
                vrt.inc(ve.tensor_scalar(mxn, mx, -1.0, None, op0=ALU.mult))
                vrt.wait(se)
                art.inc(se.activation(e_t, k.bank(0)[:, 0:8], AF.Exp, bias=mxn,
                                      accum_out=sm))
                snap[("art", (mac, tt))] = art.v
                art.wait(ve)
                k.chain("ve", ve.reciprocal(rs, sm))
                k.chain("ve", ve.tensor_scalar(probs, e_t, rs, None, op0=ALU.mult))
                k.chain("ve", ve.reduce_max(m1, probs, axis=AXX))
                k.chain("ve", ve.tensor_scalar(eq1, probs, m1, None,
                                               op0=ALU.is_equal))
                k.chain("ve", ve.tensor_scalar(eq1s, eq1, -2.0, None, op0=ALU.mult))
                k.chain("ve", ve.tensor_tensor(pm, probs, eq1s, ALU.add))
                k.chain("ve", ve.reduce_max(m2, pm, axis=AXX))
                k.chain("ve", ve.tensor_scalar(eq2, probs, m2, None,
                                               op0=ALU.is_equal))
                k.chain("ve", ve.tensor_tensor(pm, eq1, eq2, ALU.add))
                k.chain("ve", ve.tensor_tensor(wsel, probs, pm, ALU.mult))
                k.chain("ve", ve.tensor_tensor(ws, m1, m2, ALU.add))
                k.chain("ve", ve.reciprocal(rw, ws))
                vrt.inc(ve.tensor_scalar(rb1, wsel, rw, None, op0=ALU.mult))
                snap[("vrt", (mac, tt))] = vrt.v
                vrt.wait(te)
                if mac + tt > 0:
                    aev.wait(te, v=snap[("aevt", (mac, tt - 1) if tt
                                         else (mac - 1, 15))])
                ptr.inc(te.matmul(k.bank(1, rows=8)[:, 0:128], rb1, id_sb[:],
                                  start=True, stop=True))
                ptr.wait(se)
                aev.inc(se.copy(wfm[:, tt * 128:(tt + 1) * 128],
                                k.bank(1, rows=8)[:, 0:128]))
                snap[("aevt", (mac, tt))] = aev.v
            snap[("prt_mac", mac)] = prt.v
            if FFN0_STAGE < 2:
                continue
            aev.wait(te)
            if mac > 0:
                vax.wait(se, v=snap[("vax_mac", mac - 1)])
            for nb in range(4):
                if nb > 0:
                    aev.wait(te)
                prp.inc(te.matmul(k.bank(2), rep_sb[:],
                                  wfm[:, nb * 512:(nb + 1) * 512],
                                  start=True, stop=True))
                prp.wait(se)
                aev.inc(se.copy(wrp[:, nb * 512:(nb + 1) * 512], k.bank(2)))
            aev.wait(ve)
            if mac > 0 and ("pmm_mac", mac - 1) in snap:
                pmm.wait(ve, v=snap[("pmm_mac", mac - 1)])
            for nb in range(4):
                vax.wait(te)
                last = None
                for fc in range(NCH):
                    last = te.matmul(k.bank(3), A_sb[:, fc * 128:(fc + 1) * 128],
                                     xp[:, fc * 2048 + nb * 512:
                                        fc * 2048 + (nb + 1) * 512],
                                     start=(fc == 0), stop=(fc == NCH - 1))
                pax.inc(last)
                pax.wait(ve)
                vax.inc(ve.tensor_tensor(axw[:, nb * 512:(nb + 1) * 512],
                                         k.bank(3), wrp[:, nb * 512:(nb + 1) * 512],
                                         ALU.mult))
            snap[("vax_mac", mac)] = vax.v
            snap[("pax_mac", mac)] = pax.v
            if FFN0_STAGE < 3:
                continue
            vax.wait(te)
            vB2.wait(te)
            bit = mac * NCH * 4
            for oc in range(NCH):
                for nb in range(4):
                    bank = 4 + bit % 2
                    if bit >= 2:
                        asl.wait(te, v=snap[("aslG", bit - 2)])
                    last = None
                    for fc in range(NCH):
                        last = te.matmul(
                            k.bank(bank),
                            wt[:, fc * D + oc * 128: fc * D + (oc + 1) * 128],
                            xp[:, fc * 2048 + nb * 512:
                               fc * 2048 + (nb + 1) * 512],
                            start=(fc == 0), stop=False)
                    last = te.matmul(k.bank(bank), B2[:, oc * 128:(oc + 1) * 128],
                                     axw[:, nb * 512:(nb + 1) * 512],
                                     start=False, stop=True)
                    pmm.inc(last)
                    pmm.wait(se)
                    if bit >= 2:
                        sst.wait(se, v=snap[("sstG", bit - 2)])
                    asl.inc(se.activation(evt[:, (bit % 2) * 512:
                                              (bit % 2 + 1) * 512],
                                          k.bank(bank), AF.Silu))
                    snap[("aslG", bit)] = asl.v
                    asl.wait(sy)
                    sst.inc(sy.dma_start(
                        out=g0_ap[oc * 128:(oc + 1) * 128,
                                  mac * 2048 + nb * 512:
                                  mac * 2048 + (nb + 1) * 512],
                        in_=evt[:, (bit % 2) * 512:(bit % 2 + 1) * 512]), 16)
                    snap[("sstG", bit)] = sst.v
                    bit += 1
            snap[("pmm_mac", mac)] = pmm.v
        sst.wait(sy)


def emit_combine(k: KB, tag, x_ap, s_ap, g_ap, xout_ap, n_tok=NT,
                 g_silu=False):
    nc, te, ve, se, sy, ge = k.nc, k.te, k.ve, k.se, k.sy, k.ge
    ld = k.sem(f"{tag}_ld")
    vc = k.sem(f"{tag}_vc")
    agl = k.sem(f"{tag}_agl")
    sst = k.sem(f"{tag}_sst")
    gdt = F32 if g_silu else BF16
    with (
        nc.sbuf_tensor(f"{tag}_x", [128, 2 * n_tok], F32) as xc,
        nc.sbuf_tensor(f"{tag}_s", [128, 2 * n_tok], BF16) as sc,
        nc.sbuf_tensor(f"{tag}_g", [128, 2 * n_tok], gdt) as gc,
        nc.sbuf_tensor(f"{tag}_gs", [128, n_tok], BF16) as gs,
        nc.sbuf_tensor(f"{tag}_sg", [128, n_tok], BF16) as sg,
        nc.sbuf_tensor(f"{tag}_o", [128, 2 * n_tok], F32) as oc_,
    ):
        for c in range(NCH):
            buf = c % 2
            sl = slice(buf * n_tok, (buf + 1) * n_tok)
            if c >= 2:
                vc.wait(sy, v=c - 1)
                if g_silu:
                    agl.wait(sy, v=c - 1)
            ld.inc(sy.dma_start(out=xc[:, sl],
                                in_=x_ap[c * 128:(c + 1) * 128, :]), 16)
            ld.inc(sy.dma_start(out=sc[:, sl],
                                in_=s_ap[c * 128:(c + 1) * 128, :]), 16)
            ld.inc(sy.dma_start(out=gc[:, sl],
                                in_=g_ap[c * 128:(c + 1) * 128, :]), 16)
            if g_silu:
                ld.wait(se)
                if c >= 1:
                    vc.wait(se, v=c)
                agl.inc(se.activation(gs[:], gc[:, sl], AF.Silu))
                agl.wait(ve)
            ld.wait(ve)
            if c >= 2:
                sst.wait(ve, v=16 * (c - 1))
            gsrc = gs[:] if g_silu else gc[:, sl]
            k.chain("ve", ve.tensor_tensor(sg[:], sc[:, sl], gsrc, ALU.mult))
            vc.inc(ve.tensor_tensor(oc_[:, sl], xc[:, sl], sg[:], ALU.add))
            vc.wait(sy)
            sst.inc(sy.dma_start(out=xout_ap[c * 128:(c + 1) * 128, :],
                                 in_=oc_[:, sl]), 16)
        sst.wait(sy)


def emit_qkv1(k: KB, tag, xn_ap, wq_ap, wk_ap, wv_ap, cos_ap, sin_ap, psw_ap,
              q1_ap, k1_ap, v1_ap, bnd_ap):
    nc, te, ve, se, sy, ge = k.nc, k.te, k.ve, k.se, k.sy, k.ge
    xl = k.sem(f"{tag}_xl")
    wl = k.sem(f"{tag}_wl")
    smm = k.sem(f"{tag}_mm")
    sev = k.sem(f"{tag}_ev")
    psw = k.sem(f"{tag}_sw")
    vro = k.sem(f"{tag}_ro")
    pnm = k.sem(f"{tag}_nm")
    vmx = k.sem(f"{tag}_mx")
    vsq = k.sem(f"{tag}_sq")
    sst = k.sem(f"{tag}_st")
    snap = {}
    with (
        nc.sbuf_tensor(f"{tag}_xn", [128, NCH * NT], BF16) as xn_sb,
        nc.sbuf_tensor(f"{tag}_wb", [128, 2 * NCH * 512], BF16) as wb,
        nc.sbuf_tensor(f"{tag}_cos", [128, NT], F32) as cos_sb,
        nc.sbuf_tensor(f"{tag}_sin", [128, NT], F32) as sin_sb,
        nc.sbuf_tensor(f"{tag}_psw", [128, 128], BF16) as psw_sb,
        nc.sbuf_tensor(f"{tag}_ho", [128, 2], BF16) as ho_sb,
        nc.sbuf_tensor(f"{tag}_qe", [128, 2 * 512], BF16) as qe,
        nc.sbuf_tensor(f"{tag}_u1", [128, 512], F32) as u1_h,
        nc.sbuf_tensor(f"{tag}_u2", [128, 512], F32) as u2_h,
        nc.sbuf_tensor(f"{tag}_rq", [128, 2 * 512], BF16) as rq,
        nc.sbuf_tensor(f"{tag}_sq2", [128, 2 * 512], BF16) as sq2,
        nc.sbuf_tensor(f"{tag}_mxr", [2, 512], F32) as mxr_h,
    ):
        u1, u2, mxr = u1_h[:], u2_h[:], mxr_h[:]
        init2 = k.sem(f"{tag}_in2")
        k.ve.memset(ho_sb[:], 0.0)
        k.ve.memset(ho_sb[0:64, 0:1], 1.0)
        init2.inc(k.ve.memset(ho_sb[64:128, 1:2], 1.0))
        init2.wait(te)
        for c in range(NCH):
            xl.inc(sy.dma_start(out=xn_sb[:, c * NT:(c + 1) * NT],
                                in_=xn_ap[c * 128:(c + 1) * 128, :]), 16)
        xl.inc(sy.dma_start(out=cos_sb[:], in_=cos_ap), 16)
        xl.inc(sy.dma_start(out=sin_sb[:], in_=sin_ap), 16)
        xl.inc(ge.dma_start(out=psw_sb[:], in_=psw_ap), 16)
        xl.wait(te)

        blocks = [("q", wq_ap, q1_ap, 4, 4, 0), ("k", wk_ap, k1_ap, 1, 8, 64)]
        it = 0
        for name, w_ap, dst_ap, nfb, ntb, nslot0 in blocks:
            for fb in range(nfb):
                bi = it
                wbuf = (len(snap) * 0 + fb + (0 if name == "q" else 4)) % 2
                if ("wld", wbuf) in snap:
                    smm.wait(ge, v=snap[("wld", wbuf)])
                src = w_ap[:, fb * 512:(fb + 1) * 512]
                src = src.rearrange("(dc p) f -> p dc f", p=128)
                dst3 = wb[:, wbuf * NCH * 512:(wbuf + 1) * NCH * 512].rearrange(
                    "p (dc f) -> p dc f", dc=NCH)
                wl.inc(ge.dma_start(out=dst3, in_=src), 16)
                wseen = wl.v
                for fc in range(4):
                    for tb in range(ntb):
                        buf = it % 2
                        bank = it % 2
                        wl.wait(te, v=wseen)
                        if it >= 2:
                            sev.wait(te, v=snap[("sev", it - 2)])
                        last = None
                        for dc in range(NCH):
                            lhsT = wb[:, wbuf * NCH * 512 + dc * 512 + fc * 128:
                                      wbuf * NCH * 512 + dc * 512 + (fc + 1) * 128]
                            rhs = xn_sb[:, dc * NT + tb * 512:
                                        dc * NT + (tb + 1) * 512]
                            last = te.matmul(k.bank(bank), lhsT, rhs,
                                             start=(dc == 0), stop=(dc == NCH - 1))
                        smm.inc(last)
                        snap[("wld", wbuf)] = smm.v
                        smm.wait(se, v=smm.v)
                        if it >= 2:
                            psw.wait(se, v=snap[("psw", it - 2)])
                            k.sem("chain_ve").wait(se, v=snap[("vroU2", it - 2)])
                        sev.inc(se.copy(qe[:, buf * 512:(buf + 1) * 512],
                                        k.bank(bank)))
                        snap[("sev", it)] = sev.v
                        sev.wait(te, v=snap[("sev", it)])
                        if it >= 2:
                            k.sem("chain_ve").wait(te, v=snap[("vroU2", it - 2)])
                        psw.inc(te.matmul(k.bank(2 + bank), psw_sb[:],
                                          qe[:, buf * 512:(buf + 1) * 512],
                                          start=True, stop=True))
                        snap[("psw", it)] = psw.v
                        csl = slice(tb * 512, (tb + 1) * 512)
                        psw.wait(ve, v=snap[("psw", it)])
                        k.chain("ve", ve.tensor_tensor(
                            u1, qe[:, buf * 512:(buf + 1) * 512],
                            cos_sb[:, csl], ALU.mult))
                        k.chain("ve", ve.tensor_tensor(u2, k.bank(2 + bank),
                                                       sin_sb[:, csl], ALU.mult))
                        chv = k.sem("chain_ve")
                        snap[("vroU2", it)] = chv.v
                        if it >= 2:
                            sst.wait(ve, v=snap[("sstq", it - 2)])
                        k.chain("ve", ve.tensor_tensor(
                            rq[:, buf * 512:(buf + 1) * 512], u1, u2, ALU.add))
                        snap[("vro", it)] = chv.v
                        if it >= 1:
                            pnm.wait(ve, v=snap[("pnm", it - 1)])
                        vsq.inc(ve.tensor_tensor(
                            sq2[:, buf * 512:(buf + 1) * 512],
                            rq[:, buf * 512:(buf + 1) * 512],
                            rq[:, buf * 512:(buf + 1) * 512], ALU.mult))
                        vsq.wait(te, v=vsq.v)
                        if it >= 2:
                            vmx.wait(te, v=snap[("vmx", it - 2)])
                        pnm.inc(te.matmul(k.bank(4 + bank, rows=2), ho_sb[:],
                                          sq2[:, buf * 512:(buf + 1) * 512],
                                          start=True, stop=True))
                        snap[("pnm", it)] = pnm.v
                        pnm.wait(ve, v=snap[("pnm", it)])
                        slot = nslot0 + (fb * 4 + fc) * ntb + tb
                        vmx.inc(ve.reduce_max(mxr[:, slot:slot + 1],
                                              k.bank(4 + bank, rows=2), axis=AXX))
                        snap[("vmx", it)] = vmx.v
                        k.sem("chain_ve").wait(sy, v=snap[("vro", it)])
                        sst.inc(sy.dma_start(
                            out=dst_ap[fb * 512 + fc * 128:
                                       fb * 512 + (fc + 1) * 128,
                                       tb * 512:(tb + 1) * 512],
                            in_=rq[:, buf * 512:(buf + 1) * 512]), 16)
                        snap[("sstq", it)] = sst.v
                        it += 1
        vmx.wait(sy)
        sst.inc(sy.dma_start(out=bnd_ap, in_=mxr), 16)

        wbuf = 0
        smm.wait(ge, v=smm.v)
        src = wv_ap.rearrange("(dc p) f -> p dc f", p=128)
        dst3 = wb[:, 0:NCH * 512].rearrange("p (dc f) -> p dc f", dc=NCH)
        wl.inc(ge.dma_start(out=dst3, in_=src), 16)
        st_v = {}
        for tt in range(NTL):
            bank = tt % 2
            buf = tt % 2
            wl.wait(te)
            if tt >= 2:
                sev.wait(te, v=snap[("sevv", tt - 2)])
            last = None
            for dc in range(NCH):
                lhsT = xn_sb[:, dc * NT + tt * 128: dc * NT + (tt + 1) * 128]
                rhs = wb[:, dc * 512:(dc + 1) * 512]
                last = te.matmul(k.bank(bank), lhsT, rhs,
                                 start=(dc == 0), stop=(dc == NCH - 1))
            smm.inc(last)
            smm.wait(se, v=smm.v)
            if tt >= 2:
                sst.wait(se, v=st_v[tt - 2])
            sev.inc(se.copy(qe[:, buf * 512:(buf + 1) * 512], k.bank(bank)))
            snap[("sevv", tt)] = sev.v
            sev.wait(sy, v=snap[("sevv", tt)])
            for g in range(8):
                sst.inc(sy.dma_start(
                    out=v1_ap[g * 128:(g + 1) * 128, tt * 64:(tt + 1) * 64],
                    in_=qe[:, buf * 512 + g * 64: buf * 512 + (g + 1) * 64]), 16)
            st_v[tt] = sst.v
        sst.wait(sy)


def emit_att1(k: KB, tag, q1_ap, k1_ap, v1_ap, bnd_ap, cm_ap, a1_ap):
    nc, te, ve, se, sy, ge = k.nc, k.te, k.ve, k.se, k.sy, k.ge
    ld = k.sem(f"{tag}_ld")
    lq = k.sem(f"{tag}_lq")
    pmm = k.sem(f"{tag}_pmm")
    aex = k.sem(f"{tag}_aex")
    vmk = k.sem(f"{tag}_vmk")
    pav = k.sem(f"{tag}_pav")
    vs3 = k.sem(f"{tag}_vs3")
    pb3 = k.sem(f"{tag}_pb3")
    aev = k.sem(f"{tag}_aev")
    vout = k.sem(f"{tag}_vout")
    vms = k.sem(f"{tag}_vms")
    sst = k.sem(f"{tag}_sst")
    vpr = k.sem(f"{tag}_vpr")
    snap = {}

    def put(n, i, sem):
        snap[(n, i)] = sem.v

    def at(n, i):
        return snap[(n, i)]

    from contextlib import ExitStack as _ES
    _les = _ES()

    def _sb(name, shape, dt):
        return _les.enter_context(nc.sbuf_tensor(name, shape, dt))

    E1 = _sb(f"{tag}_E1", [128, 2 * 16384], BF16)
    MB = _sb(f"{tag}_MB", [128, 2 * 4096], BF16)
    Kp = _sb(f"{tag}_K", [128, NT], BF16)
    V65 = _sb(f"{tag}_V", [128, 2 * 32 * 65], BF16)
    qt = _sb(f"{tag}_q", [128, 2 * 512], BF16)
    CM = _sb(f"{tag}_cm", [128, 4 * 8 * 512], BF16)
    Rs = _sb(f"{tag}_Rs", [128, 512], BF16)
    ob = _sb(f"{tag}_ob", [128, 512], BF16)
    s3r = _sb(f"{tag}_s3", [1, 512], F32)[:]
    r3r = _sb(f"{tag}_r3", [1, 512], F32)[:]
    b3r = _sb(f"{tag}_b3", [1, 512], BF16)[:]
    pr = _sb(f"{tag}_pr", [1, 1280], F32)[:]
    prb = _sb(f"{tag}_prb", [1, 64], BF16)[:]
    biasB = _sb(f"{tag}_bias", [128, 32], F32)[:]

    bnd = pr[:, 0:1024]
    lq.inc(sy.dma_start(out=pr[:, 0:512], in_=bnd_ap[0:1, :]), 16)
    lq.inc(sy.dma_start(out=pr[:, 512:1024], in_=bnd_ap[1:2, :]), 16)
    lq.inc(ge.dma_start(out=CM[:], in_=cm_ap), 16)
    lq.wait(ve)
    bq = pr[:, 1024 + 0:1024 + 32]
    bk = pr[:, 1024 + 32:1024 + 40]
    scr40 = pr[:, 1064:1104]
    bk4 = pr[:, 1104:1136]
    prod = pr[:, 1136:1168]
    for h in range(32):
        col = (h % 2) * 512 + (h // 2) * 4
        k.chain("ve", ve.reduce_max(bq[:, h:h + 1], pr[:, col:col + 4], axis=AXX))
    for g in range(8):
        col = (g % 2) * 512 + 64 + (g // 2) * 8
        k.chain("ve", ve.reduce_max(bk[:, g:g + 1], pr[:, col:col + 8], axis=AXX))
    vpr.inc(ve.tensor_copy(scr40, pr[:, 1024:1064]))
    vpr.wait(se)
    apr = k.sem(f"{tag}_apr")
    apr.inc(se.activation(pr[:, 1024:1064], scr40, AF.Sqrt))
    apr.wait(ve)
    bk4v = bk4.rearrange("p (a b) -> p a b", b=4)
    for j in range(4):
        k.chain("ve", ve.tensor_copy(bk4v[:, :, j:j + 1],
                                     bk.rearrange("p (a b) -> p a b", b=1)))
    k.chain("ve", ve.tensor_tensor(prod, bq, bk4, ALU.mult))
    vpr.inc(ve.tensor_scalar(prb[:, 0:32], prod, -1.0 / 8.0, None, op0=ALU.mult))
    vpr.wait(te)
    ppr = k.sem(f"{tag}_ppr")
    ppr.inc(te.matmul(k.bank(0)[:, 0:32], k.ones_row[:], prb[:, 0:32],
                      start=True, stop=True))
    ppr.wait(ve)
    vpr.inc(ve.tensor_copy(biasB, k.bank(0)[:, 0:32]))
    vpr.wait(te)
    vpr.wait(se)
    vpr.wait(sy)
    vpr.wait(ge)
    if ATT1_STAGE == 0:
        _les.close()
        return

    def emit_post(j):
        jp, sb, hq8 = j // 32, (j // 8) % 4, j % 8
        gi = hq8 // 4
        hq = jp * 8 + hq8
        P = j % 2
        eP = 16384 * P
        mP = 4096 * P
        nt = 8 * sb + 8
        aex.wait(te, v=at("aexe", j))
        vmk.wait(te, v=at("vmk", j))
        if j >= 1:
            vout.wait(te, v=at("vout", j - 1))
        ld.wait(te, v=at("ldv", j // 32))
        last = None
        for jj in range(nt):
            own = jj < nt // 2
            c65 = jj if own else 16 + (jj - nt // 2)
            near = (jj >= nt // 2 - 4 and own) or (jj >= nt - 4)
            if near:
                m = (jj - (nt // 2 - 4)) if own else (4 + jj - (nt - 4))
                rhs = MB[:, mP + m * 512: mP + (m + 1) * 512]
            else:
                rhs = E1[:, eP + jj * 512: eP + (jj + 1) * 512]
            last = te.matmul(k.bank(6)[0:65, :],
                             V65[:, gi * 2080 + c65 * 65: gi * 2080 + (c65 + 1) * 65],
                             rhs, start=(jj == 0), stop=(jj == nt - 1))
        pav.inc(last)
        put("pav", j, pav)
        pav.wait(ve, v=at("pav", j))
        if j >= 1:
            pb3.wait(ve, v=at("pb3", j - 1))
        k.chain("ve", ve.tensor_copy(s3r, k.ps[64:65, 512 * 6:512 * 7]))
        k.chain("ve", ve.reciprocal(r3r, s3r))
        vs3.inc(ve.tensor_copy(b3r, r3r))
        put("vs3", j, vs3)
        vs3.wait(te, v=at("vs3", j))
        if j >= 1:
            aev.wait(te, v=at("aevR3", j - 1))
        pb3.inc(te.matmul(k.bank(7, rows=64), k.ones_row[0:1, 0:64], b3r,
                          start=True, stop=True))
        put("pb3", j, pb3)
        pb3.wait(se, v=at("pb3", j))
        aev.inc(se.copy(Rs[0:64, :], k.bank(7, rows=64)))
        put("aevR3", j, aev)
        aev.wait(ve, v=at("aevR3", j))
        if j >= 1:
            sst.wait(ve, v=at("sst", j - 1))
        vout.inc(ve.tensor_tensor(ob[0:64, :], k.bank(6)[0:64, :], Rs[0:64, :],
                                  ALU.mult))
        put("vout", j, vout)
        vout.wait(sy, v=at("vout", j))
        sst.inc(sy.dma_start(
            out=a1_ap[hq * 64:(hq + 1) * 64, sb * 512:(sb + 1) * 512],
            in_=ob[0:64, :]), 16)
        put("sst", j, sst)

    NIT = min(128, ATT1_IT_LIMIT) if ATT1_IT_LIMIT else 128
    for i in range(NIT):
        jp, sb, hq8 = i // 32, (i // 8) % 4, i % 8
        gi = hq8 // 4
        hq = jp * 8 + hq8
        P = i % 2
        eP = 16384 * P
        mP = 4096 * P
        nt = 8 * sb + 8
        if sb == 0 and hq8 == 0:
            if jp > 0:
                pmm.wait(sy, v=at("pmm", i - 1))
            ld.inc(sy.dma_start(out=Kp[:],
                                in_=k1_ap[jp * 128:(jp + 1) * 128, :]), 16)
        if i >= 2:
            pmm.wait(sy, v=at("pmm", i - 1))
        lq.inc(sy.dma_start(
            out=qt[gi * 64:(gi + 1) * 64, P * 512:(P + 1) * 512],
            in_=q1_ap[hq * 64:(hq + 1) * 64, sb * 512:(sb + 1) * 512]), 16)

        ld.wait(te)
        lq.wait(te)
        nun = nt // 2
        gu0 = snap.get(("gun", i - 1), 0)
        for u in range(nun):
            G = gu0 + u
            reg = G % 3
            wv_ = G - 2
            if wv_ > 0:
                aex.wait(te, v=wv_)
            last = None
            for tj in range(2):
                jj = u * 2 + tj
                own = jj < nt // 2
                col = jj * 128 if own else 2048 + (jj - nt // 2) * 128
                last = te.matmul(
                    k.bank(2 * reg + tj),
                    Kp[gi * 64:(gi + 1) * 64, col:col + 128],
                    qt[gi * 64:(gi + 1) * 64, P * 512:(P + 1) * 512],
                    start=True, stop=True)
            pmm.inc(last)
        put("pmm", i, pmm)
        snap[("gun", i)] = gu0 + nun

        for u in range(nun):
            G = gu0 + u
            reg = G % 3
            pmm.wait(se, v=at("pmm", i) - nun + u + 1)
            if i >= 2 and u == 0:
                if ("pav", i - 2) in snap:
                    pav.wait(se, v=at("pav", i - 2))
                vmk.wait(se, v=at("vmk", i - 2))
            aex.inc(se.activation(
                E1[:, eP + u * 1024: eP + (u + 1) * 1024],
                k.bank(2 * reg, nb=2), AF.Exp,
                bias=biasB[:, hq:hq + 1], scale=float(1.0 / 8.0)))
        put("aexe", i, aex)

        if i >= 1 and ATT1_STAGE >= 2:
            emit_post(i - 1)
        if sb == 0 and hq8 == 0:
            if jp > 0 and ("pav", i - 1) in snap:
                pav.wait(ve, v=at("pav", i - 1))
            vms.inc(ve.memset(V65[:], 1.0))
            vms.wait(sy)
            for gi2 in range(2):
                g = jp * 2 + gi2
                dst = V65[:, gi2 * 2080:(gi2 + 1) * 2080].rearrange(
                    "p (c d) -> p c d", d=65)[:, :, 0:64]
                ld.inc(sy.dma_start(out=dst, in_=v1_ap[g * 128:(g + 1) * 128, :]
                                    .rearrange("p (c d) -> p c d", d=64)), 16)
            put("ldv", jp, ld)

        aex.wait(ve, v=at("aexe", i))
        if i >= 2 and ("pav", i - 2) in snap:
            pav.wait(ve, v=at("pav", i - 2))
        for m in range(8):
            own = m < 4
            jj = (nt // 2 - 4 + m) if own else (nt - 8 + m)
            last = ve.tensor_tensor(
                MB[:, mP + m * 512: mP + (m + 1) * 512],
                E1[:, eP + jj * 512: eP + (jj + 1) * 512],
                CM[:, (sb * 8 + m) * 512:(sb * 8 + m + 1) * 512], ALU.mult)
            if m < 7:
                k.chain("ve", last)
        vmk.inc(last)
        put("vmk", i, vmk)

    if ATT1_STAGE >= 2:
        emit_post(NIT - 1)
    sst.wait(sy)
    _les.close()


def emit_ffn1(k: KB, tag, xn_ap, rt_ap, wsg_ap, wgu_ap, wsd_ap, wegu_ap,
              wed_ap, id_ap, g1p_ap, w1fm_ap):
    nc, te, ve, se, sy, ge = k.nc, k.te, k.ve, k.se, k.sy, k.ge
    wl = k.sem(f"{tag}_wl")
    pl = k.sem(f"{tag}_pl")
    prt = k.sem(f"{tag}_prt")
    vrt = k.sem(f"{tag}_vrt")
    art = k.sem(f"{tag}_art")
    ptr = k.sem(f"{tag}_ptr")
    aev = k.sem(f"{tag}_aev")
    psg = k.sem(f"{tag}_psg")
    pgu = k.sem(f"{tag}_pgu")
    ahg = k.sem(f"{tag}_ahg")
    vh = k.sem(f"{tag}_vh")
    pdn = k.sem(f"{tag}_pdn")
    adn = k.sem(f"{tag}_adn")
    sst = k.sem(f"{tag}_sst")
    rdm = k.sem(f"{tag}_rdm")
    lfm = k.sem(f"{tag}_lfm")
    vsg = k.sem(f"{tag}_vsg")
    asg = k.sem(f"{tag}_asg")
    snap = {}
    from contextlib import ExitStack as _ES
    _les = _ES()

    def _sb(name, shape, dt):
        return _les.enter_context(nc.sbuf_tensor(name, shape, dt))

    xp = _sb(f"{tag}_xn", [128, NCH * 2048], BF16)
    hbuf = _sb(f"{tag}_h", [128, 45056], BF16)
    wg = _sb(f"{tag}_wg", [128, 2 * NCH * 128], BF16)
    wu = _sb(f"{tag}_wu", [128, 2 * NCH * 128], BF16)
    wd = _sb(f"{tag}_wd", [128, 2 * 5632], BF16)
    rt_sb = _sb(f"{tag}_rt", [128, NCH * 8], BF16)
    wsg_sb = _sb(f"{tag}_ws", [128, NCH], BF16)
    id_sb = _sb(f"{tag}_id", [128, 128], BF16)
    wfm = _sb(f"{tag}_wfm", [8, 128], BF16)
    sgB = _sb(f"{tag}_sgB", [128, 2048], BF16)
    ceB = sgB
    scr = _sb(f"{tag}_scr", [128, 2048], BF16)
    r1 = _sb(f"{tag}_r1", [128, 64], F32)[:]
    rb1 = _sb(f"{tag}_rb1", [128, 8], BF16)[:]
    srow = _sb(f"{tag}_srow", [1, 512], F32)[:]
    srb = _sb(f"{tag}_srb", [1, 512], BF16)[:]
    evf = _sb(f"{tag}_evf", [128, 2 * 512], F32)

    for fc in range(NCH):
        pl.inc(sy.dma_start(out=xp[:, fc * 2048:(fc + 1) * 2048],
                            in_=xn_ap[fc * 128:(fc + 1) * 128, 0:2048]), 16)
    wl.inc(ge.dma_start(out=rt_sb[:].rearrange("p (c f) -> p c f", c=NCH),
                        in_=rt_ap.rearrange("(c p) f -> p c f", p=128)), 16)
    wl.inc(ge.dma_start(out=wsg_sb[:], in_=wsg_ap), 16)
    wl.inc(ge.dma_start(out=id_sb[:], in_=id_ap), 16)
    wl.wait(te)
    pl.wait(te)

    for tt in range(16):
        if tt > 0:
            vrt.wait(te, v=snap[("vrt", tt - 1)])
            art.wait(te, v=snap[("art", tt - 1)])
        last = None
        for fc in range(NCH):
            last = te.matmul(k.bank(0)[:, 0:8],
                             xp[:, fc * 2048 + tt * 128:
                                fc * 2048 + (tt + 1) * 128],
                             rt_sb[:, fc * 8:(fc + 1) * 8],
                             start=(fc == 0), stop=(fc == NCH - 1))
        prt.inc(last)
        prt.wait(ve)
        e_t = r1[:, 0:8]
        probs = r1[:, 8:16]
        eq1 = r1[:, 16:24]
        pm = r1[:, 24:32]
        eq2 = r1[:, 32:40]
        eq1s = r1[:, 56:64]
        mx = r1[:, 48:49]
        mxn = r1[:, 49:50]
        sm = r1[:, 50:51]
        rs = r1[:, 51:52]
        m1 = r1[:, 52:53]
        m2 = r1[:, 53:54]
        k.chain("ve", ve.reduce_max(mx, k.bank(0)[:, 0:8], axis=AXX))
        vrt.inc(ve.tensor_scalar(mxn, mx, -1.0, None, op0=ALU.mult))
        vrt.wait(se)
        art.inc(se.activation(e_t, k.bank(0)[:, 0:8], AF.Exp, bias=mxn,
                              accum_out=sm))
        snap[("art", tt)] = art.v
        art.wait(ve)
        k.chain("ve", ve.reciprocal(rs, sm))
        k.chain("ve", ve.tensor_scalar(probs, e_t, rs, None, op0=ALU.mult))
        k.chain("ve", ve.reduce_max(m1, probs, axis=AXX))
        k.chain("ve", ve.tensor_scalar(eq1, probs, m1, None, op0=ALU.is_equal))
        k.chain("ve", ve.tensor_scalar(eq1s, eq1, -2.0, None, op0=ALU.mult))
        k.chain("ve", ve.tensor_tensor(pm, probs, eq1s, ALU.add))
        k.chain("ve", ve.reduce_max(m2, pm, axis=AXX))
        k.chain("ve", ve.tensor_scalar(eq2, probs, m2, None, op0=ALU.is_equal))
        k.chain("ve", ve.tensor_tensor(pm, eq1, eq2, ALU.add))
        vrt.inc(ve.tensor_tensor(rb1, probs, pm, ALU.mult))
        snap[("vrt", tt)] = vrt.v
        vrt.wait(te)
        if tt > 0:
            aev.wait(te, v=snap[("aevt", tt - 1)])
        ptr.inc(te.matmul(k.bank(1, rows=8)[:, 0:128], rb1, id_sb[:],
                          start=True, stop=True))
        ptr.wait(se)
        if tt > 0:
            sst.wait(se, v=snap[("wfmst", tt - 1)])
        aev.inc(se.copy(wfm[:], k.bank(1, rows=8)[:, 0:128]))
        snap[("aevt", tt)] = aev.v
        aev.wait(sy)
        sst.inc(sy.dma_start(out=w1fm_ap[:, tt * 128:(tt + 1) * 128],
                             in_=wfm[:]), 16)
        snap[("wfmst", tt)] = sst.v

    pbj = k.sem(f"{tag}_pbj")
    for j in range(4):
        if j > 0:
            vsg.wait(te, v=snap[("vsgj", j - 1)])
        last = None
        for fc in range(NCH):
            last = te.matmul(k.bank(2, rows=1), wsg_sb[:, fc:fc + 1],
                             xp[:, fc * 2048 + j * 512:fc * 2048 + j * 512 + 512],
                             start=(fc == 0), stop=(fc == NCH - 1))
        psg.inc(last)
        psg.wait(ve)
        if j > 0:
            asg.wait(ve, v=snap[("asgj", j - 1)])
        vsg.inc(ve.tensor_copy(srow, k.bank(2, rows=1)))
        snap[("vsgj", j)] = vsg.v
        vsg.wait(se)
        if j > 0:
            pbj.wait(se, v=snap[("pbjj", j - 1)])
        asg.inc(se.activation(srb, srow, AF.Sigmoid))
        snap[("asgj", j)] = asg.v
        asg.wait(te)
        if j > 0:
            aev.wait(te, v=snap[("aevj", j - 1)])
        pbj.inc(te.matmul(k.bank(3), k.ones_row[:], srb,
                          start=True, stop=True))
        snap[("pbjj", j)] = pbj.v
        pbj.wait(se)
        aev.inc(se.copy(sgB[:, j * 512:(j + 1) * 512], k.bank(3)))
        snap[("aevj", j)] = aev.v
    aev.wait(ve)
    aev.wait(te)

    state = {"git": 0, "dit": 0}

    def mlp_half(gu_ap, up_off, nip, dn_ap, scale_sb, r, accum, ekey):
        git0 = state["git"]
        tsl = slice(r * 1024, (r + 1) * 1024)
        for ip in range(nip):
            git = state["git"]
            wbuf = git % 2
            if git >= 2:
                pgu.wait(ge, v=snap[("pgu", git - 2)])
            for wdst, col0 in ((wg, ip * 128), (wu, up_off + ip * 128)):
                src = gu_ap[:, col0:col0 + 128]
                wl.inc(ge.dma_start(
                    out=wdst[:, wbuf * NCH * 128:(wbuf + 1) * NCH * 128]
                    .rearrange("p (c f) -> p c f", c=NCH),
                    in_=src.rearrange("(c p) f -> p c f", p=128)), 16)
            wseen = wl.v
            wl.wait(te, v=wseen)
            if git >= 1:
                vh.wait(te, v=snap[("vh", git - 1)])
                ahg.wait(te, v=snap[("ahg", git - 1)])
            if ip == 0 and ("hwar", ekey) in snap:
                pdn.wait(ve, v=snap[("hwar", ekey)])
            for bank0, wsrc in ((0, wg), (2, wu)):
                last = None
                for th in range(2):
                    for fc in range(NCH):
                        last = te.matmul(
                            k.bank(bank0 + th),
                            wsrc[:, wbuf * NCH * 128 + fc * 128:
                                 wbuf * NCH * 128 + (fc + 1) * 128],
                            xp[:, fc * 2048 + r * 1024 + th * 512:
                               fc * 2048 + r * 1024 + (th + 1) * 512],
                            start=(fc == 0), stop=(fc == NCH - 1))
                pgu.inc(last)
            snap[("pgu", git)] = pgu.v
            pgu.wait(se, v=snap[("pgu", git)])
            hsl = slice(ip * 1024, (ip + 1) * 1024)
            ahg.inc(se.activation(scr[:, 0:1024], k.bank(0, nb=2), AF.Silu))
            snap[("ahg", git)] = ahg.v
            ahg.wait(ve)
            k.chain("ve", ve.tensor_tensor(scr[:, 1024:2048], scr[:, 0:1024],
                                           k.bank(2, nb=2), ALU.mult))
            vh.inc(ve.tensor_tensor(hbuf[:, hsl], scr[:, 1024:2048],
                                    scale_sb[:, tsl], ALU.mult))
            snap[("vh", git)] = vh.v
            state["git"] = git + 1
        vh.wait(te)

        def wd_load(oc):
            dwb = oc % 2
            if ("wdw", dwb) in snap:
                pdn.wait(ge, v=snap[("wdw", dwb)])
            wl.inc(ge.dma_start(
                out=wd[:, dwb * 5632: dwb * 5632 + nip * 128]
                .rearrange("p (c f) -> p c f", c=nip),
                in_=dn_ap[:, oc * 128:(oc + 1) * 128]
                .rearrange("(c p) f -> p c f", p=128)), 16)
            snap[("wds", oc)] = wl.v

        wd_load(0)
        for oc in range(NCH):
            dwb = oc % 2
            if oc + 1 < NCH:
                wd_load(oc + 1)
            wl.wait(te, v=snap[("wds", oc)])
            for th in range(2):
                dit = state["dit"]
                bank = 4 + dit % 2
                if dit >= 2:
                    adn.wait(te, v=snap[("adn", dit - 2)])
                last = None
                for ip in range(nip):
                    last = te.matmul(
                        k.bank(bank),
                        wd[:, dwb * 5632 + ip * 128:dwb * 5632 + (ip + 1) * 128],
                        hbuf[:, ip * 1024 + th * 512: ip * 1024 + th * 512 + 512],
                        start=(ip == 0), stop=(ip == nip - 1))
                pdn.inc(last)
                snap[("wdw", dwb)] = pdn.v
                pdn.wait(se, v=pdn.v)
                if dit >= 2:
                    sst.wait(se, v=snap[("dst", dit - 2)])
                    rdm.wait(se, v=snap[("drm", dit - 2)])
                adn.inc(se.copy(evf[:, (dit % 2) * 512:(dit % 2 + 1) * 512],
                                k.bank(bank)))
                snap[("adn", dit)] = adn.v
                reg = (oc, r * 2 + th)
                dst = g1p_ap[oc * 128:(oc + 1) * 128,
                             r * 1024 + th * 512: r * 1024 + (th + 1) * 512]
                srcv = evf[:, (dit % 2) * 512:(dit % 2 + 1) * 512]
                if accum:
                    adn.wait(ge, v=snap[("adn", dit)])
                    if ("g1w", reg) in snap:
                        rdm.wait(ge, v=snap[("g1w", reg)])
                    rdm.inc(ge.dma_start(out=dst, in_=srcv, accum_op=ALU.add), 16)
                    snap[("g1w", reg)] = rdm.v
                    snap[("drm", dit)] = rdm.v
                    snap[("dst", dit)] = sst.v
                else:
                    adn.wait(sy, v=snap[("adn", dit)])
                    sst.inc(sy.dma_start(out=dst, in_=srcv), 16)
                    snap[("dst", dit)] = sst.v
                    snap[("drm", dit)] = rdm.v
                state["dit"] = dit + 1
        snap[("hwar", ekey + 1)] = pdn.v

    snap[("hwar", 0)] = 0
    ek = 0
    for r in range(2):
        mlp_half(wgu_ap, 5632, 44, wsd_ap, sgB, r, False, ek)
        ek += 1
    sst.wait(ge)
    for e in range(8):
        vh.wait(te)
        aev.wait(te)
        for j in range(4):
            pbj.wait(sy)
            lfm.inc(sy.dma_start(out=srb, in_=w1fm_ap[e:e + 1,
                                                      j * 512:(j + 1) * 512]), 16)
            lfm.wait(te)
            pbj.inc(te.matmul(k.bank(3), k.ones_row[:], srb,
                              start=True, stop=True))
            pbj.wait(se)
            aev.inc(se.copy(ceB[:, j * 512:(j + 1) * 512], k.bank(3)))
            aev.wait(te)
        aev.wait(ve)
        for r in range(2):
            mlp_half(wegu_ap[e * 2048:(e + 1) * 2048, :], 1408, 11,
                     wed_ap[e * 1408:(e + 1) * 1408, :], ceB, r, True, ek)
            ek += 1
    rdm.wait(ge)
    sst.wait(sy)
    _les.close()


def _host_perm(parity):
    own = np.arange(parity, NTL, 2)
    other = np.arange(1 - parity, NTL, 2)
    tiles = np.concatenate([own, other])
    return (tiles[:, None] * 128 + np.arange(128)[None, :]).reshape(-1)


def build_program(dbg=(), phases=None):
    k = KB(dbg=dbg, phases=phases)
    k.constf(EPS_LN, rows=1)
    k.constf(EPS_RMS, rows=1)
    k.wait_init()
    x0 = k.din("x0", [D, NT], F32)
    if k.want("ln0"):
        xn = k.dram("xn0", [D, NT], BF16)
        emit_ln(k, "ln0", x0, xn)
        k.barrier()
    if k.want("qkv0"):
        xn = k.dram("xn0", [D, NT], BF16)
        wq = k.din("wq0", [D, 4096], F32)
        wk = k.din("wk0", [D, 4096], F32)
        wv = k.din("wv0", [D, D], F32)
        q0 = k.dram("q0", [4096, NT], BF16)
        k0 = k.dram("k0", [4096, NT], BF16)
        v0 = k.dram("v0", [D, NT], BF16)
        bnd = k.dram("bnd0", [1, 512], F32)
        emit_qkv0(k, "qkv0", xn, wq, wk, wv, q0, k0, v0, bnd)
        k.barrier()
    if k.want("att0"):
        q0 = k.dram("q0", [4096, NT], BF16)
        k0 = k.dram("k0", [4096, NT], BF16)
        v0 = k.dram("v0", [D, NT], BF16)
        bnd = k.dram("bnd0", [1, 512], F32)
        lam = k.din("lam0", [1, 16], F32)
        s0t = k.dram("s0t", [D, NT], BF16)
        factor = k.dram("factor0", [1, NT], F32)
        rms0 = k.dram("rms0", [1, NT], F32)
        emit_att0(k, "att0", q0, k0, v0, bnd, lam, s0t, factor, rms0)
        k.barrier()
    if k.want("wo0"):
        s0t = k.dram("s0t", [D, NT], BF16)
        factor = k.dram("factor0", [1, NT], F32)
        wo = k.din("wo0w", [D, D], F32)
        ns = k.din("ns0", [128, 16], F32)
        s0 = k.dram("s0", [D, NT], BF16)
        emit_wo0(k, "wo0", s0t, factor, wo, ns, s0)
        k.barrier()
    if k.want("ffn0"):
        xn = k.dram("xn0", [D, NT], BF16)
        wT = k.din("w0ffnT", [D, D], F32)
        aL = k.din("a0l", [D, 128], F32)
        bL = k.din("b0l", [128, D], F32)
        rt = k.din("rt0", [D, 8], F32)
        rep = k.din("rep", [8, 128], F32)
        ident = k.din("ident", [128, 128], F32)
        g0 = k.dram("g0", [D, NT], BF16)
        emit_ffn0(k, "ffn0", xn, wT, aL, bL, rt, rep, ident, g0)
        k.barrier()
    if k.want("cmb0"):
        x0_ = k.din("x0", [D, NT], F32)
        s0 = k.dram("s0", [D, NT], BF16)
        g0 = k.dram("g0", [D, NT], BF16)
        x1 = k.dram("x1", [D, NT], F32)
        emit_combine(k, "cmb0", x0_, s0, g0, x1)
        k.barrier()
    if k.want("ln1"):
        x1 = k.dram("x1", [D, NT], F32)
        xn1 = k.dram("xn1", [D, NT], BF16)
        emit_ln(k, "ln1", x1, xn1)
        k.barrier()
    if k.want("qkv1"):
        xn1 = k.dram("xn1", [D, NT], BF16)
        wq1 = k.din("wq1T", [D, D], F32)
        wk1 = k.din("wk1T", [D, 512], F32)
        wv1 = k.din("wv1T", [D, 512], F32)
        cosT = k.din("cos1", [128, NT], F32)
        sinT = k.din("sin1", [128, NT], F32)
        pswi = k.din("psw", [128, 128], F32)
        q1 = k.dram("q1", [D, NO], BF16)
        k1 = k.dram("k1", [512, NT], BF16)
        v1 = k.dram("v1", [1024, 2048], BF16)
        bnd1 = k.dram("bnd1", [2, 512], F32)
        emit_qkv1(k, "qkv1", xn1, wq1, wk1, wv1, cosT, sinT, pswi,
                  q1, k1, v1, bnd1)
        k.barrier()
    if k.want("att1"):
        q1 = k.dram("q1", [D, NO], BF16)
        k1 = k.dram("k1", [512, NT], BF16)
        v1 = k.dram("v1", [1024, 2048], BF16)
        bnd1 = k.dram("bnd1", [2, 512], F32)
        cm = k.din("cmask", [128, 4 * 8 * 512], F32)
        a1 = k.dram("a1", [D, NO], BF16)
        emit_att1(k, "att1", q1, k1, v1, bnd1, cm, a1)
        k.barrier()
    if k.want("wo1"):
        a1 = k.dram("a1", [D, NO], BF16)
        wo1 = k.din("wo1T", [D, D], F32)
        s1 = k.dram("s1", [D, NO], BF16)
        emit_wo0(k, "wo1", a1, None, wo1, None, s1, n_tok=NO,
                 use_ns=False, use_factor=False)
        k.barrier()
    if k.want("ffn1"):
        xn1 = k.dram("xn1", [D, NT], BF16)
        rt1 = k.din("rt1T", [D, 8], F32)
        wsg1 = k.din("wsg1T", [128, NCH], F32)
        wgu1 = k.din("wgu1", [D, 11264], F32)
        wsd1 = k.din("wsd1", [5632, D], F32)
        wegu1 = k.din("wegu1", [8 * D, 2816], F32)
        wed1 = k.din("wed1", [8 * 1408, D], F32)
        ident = k.din("ident", [128, 128], F32)
        g1p = k.dram("g1p", [D, NO], F32)
        w1fm = k.dram("w1fm", [8, 2048], BF16)
        emit_ffn1(k, "ffn1", xn1, rt1, wsg1, wgu1, wsd1, wegu1, wed1, ident,
                  g1p, w1fm)
        k.barrier()
    if k.want("cmb1"):
        x1 = k.dram("x1", [D, NT], F32)
        s1 = k.dram("s1", [D, NO], BF16)
        g1p = k.dram("g1p", [D, NO], F32)
        x2 = k.dout("out", [D, NO], F32)
        emit_combine(k, "cmb1", x1[:, 0:NO], s1, g1p, x2, n_tok=NO,
                     g_silu=True)
        k.barrier()
    return k


def _stage_weights(inputs):
    f32 = np.float32
    A = np.asarray(inputs["L0_ffn_lora_A"], f32)
    B = np.asarray(inputs["L0_ffn_lora_B"], f32)
    rep = np.zeros((8, 128), f32)
    for e in range(8):
        rep[e, e * 16:(e + 1) * 16] = 1.0
    psw = np.zeros((128, 128), f32)
    psw[np.arange(128), np.arange(128) ^ 1] = 1.0
    return {
        "wq0": np.asarray(inputs["L0_attn_Wq"], f32),
        "wk0": np.asarray(inputs["L0_attn_Wk"], f32),
        "wv0": np.asarray(inputs["L0_attn_Wv"], f32),
        "wo0w": np.asarray(inputs["L0_attn_Wo"], f32),
        "ns0": np.ascontiguousarray(
            np.asarray(inputs["L0_attn_norm_scale"], f32).reshape(16, 128).T),
        "lam0": np.asarray(inputs["L0_attn_lam"], f32).reshape(1, 16),
        "w0ffnT": np.ascontiguousarray(np.asarray(inputs["L0_ffn_weight"], f32).T),
        "a0l": np.ascontiguousarray(np.transpose(A, (2, 0, 1)).reshape(2048, 128)),
        "b0l": np.ascontiguousarray(np.transpose(B, (0, 2, 1)).reshape(128, 2048)),
        "rt0": np.ascontiguousarray(np.asarray(inputs["L0_ffn_router_w"], f32).T),
        "rep": rep,
        "ident": np.eye(128, dtype=f32),
        "psw": psw,
        "wq1T": np.ascontiguousarray(np.asarray(inputs["L1_attn_wq"], f32).T),
        "wk1T": np.ascontiguousarray(np.asarray(inputs["L1_attn_wk"], f32).T),
        "wv1T": np.ascontiguousarray(np.asarray(inputs["L1_attn_wv"], f32).T),
        "wo1T": np.ascontiguousarray(np.asarray(inputs["L1_attn_wo"], f32).T),
        "rt1T": np.ascontiguousarray(np.asarray(inputs["L1_ffn_W_router"], f32).T),
        "wsg1T": np.ascontiguousarray(
            np.asarray(inputs["L1_ffn_W_sgate"], f32).reshape(16, 128).T),
        "wgu1": np.asarray(inputs["L1_ffn_Ws_gate_up"], f32),
        "wsd1": np.asarray(inputs["L1_ffn_Ws_down"], f32),
        "wegu1": np.asarray(inputs["L1_ffn_We_gate_up"], f32).reshape(8 * 2048, 2816),
        "wed1": np.asarray(inputs["L1_ffn_We_down"], f32).reshape(8 * 1408, 2048),
    }


def _stage_percore(inputs, core):
    f32 = np.float32
    p, parity = core // 2, core % 2
    perm = _host_perm(parity)
    x = np.asarray(inputs["L0_attn_x"][p], f32)
    st = {"x0": np.ascontiguousarray(x.T[:, perm])}
    pos = perm.astype(np.float64)
    r = np.arange(128)
    theta = 10000.0 ** (-(2.0 * ((r % 64) // 2)) / 64.0)
    ang = pos[None, :] * theta[:, None]
    st["cos1"] = np.cos(ang).astype(f32)
    sgn = np.where(r % 2 == 0, -1.0, 1.0)
    st["sin1"] = (sgn[:, None] * np.sin(ang)).astype(f32)
    own_t = np.arange(parity, 32, 2)
    oth_t = np.arange(1 - parity, 32, 2)
    cm = np.zeros((4, 8, 128, 512), f32)
    for sb in range(4):
        for m in range(8):
            own = m < 4
            j = 4 * sb + (m if own else m - 4)
            keyg = (own_t if own else oth_t)[j]
            kpos = keyg * 128 + np.arange(128)
            for c in range(4):
                qg = own_t[4 * sb + c]
                qpos = qg * 128 + np.arange(128)
                cm[sb, m, :, c * 128:(c + 1) * 128] = (
                    kpos[:, None] <= qpos[None, :]).astype(f32)
    st["cmask"] = np.ascontiguousarray(
        cm.reshape(32, 128, 512).transpose(1, 0, 2).reshape(128, 32 * 512))
    return st, perm


def kernel(**inputs):
    k = build_program()
    names = set(k.inputs)
    w = _stage_weights(inputs)
    in_maps = []
    perms = []
    for core in range(8):
        st, perm = _stage_percore(inputs, core)
        st.update(w)
        in_maps.append({n: st[n] for n in names})
        perms.append(perm)
    res = run_bass_kernel_spmd(k.nc, in_maps, core_ids=list(range(8)))
    k.close()
    out = np.zeros((4, NT, D), np.float32)
    for core in range(8):
        p = core // 2
        o = np.asarray(res.results[core]["out"], np.float32)
        out[p, perms[core][:NO], :] = o.T
    return out


# version 07
# speedup vs baseline: 1.1406x; 1.1406x over previous
import numpy as np
import ml_dtypes
from contextlib import ExitStack

import concourse.bass as bass
import concourse.mybir as mybir
from concourse.bass_utils import run_bass_kernel_spmd

F32 = mybir.dt.float32
BF16 = mybir.dt.bfloat16
AF = mybir.ActivationFunctionType
ALU = mybir.AluOpType
AXX = mybir.AxisListType.X

D = 2048
NT = 4096
NO = 2048
NCH = D // 128
NTL = NT // 128
EPS_LN = 1e-5
EPS_RMS = 1e-6

SAFE_CHAIN = True
ATT0_IT_LIMIT = None
ATT0_NO_POST = False
FFN0_STAGE = 3
ATT1_IT_LIMIT = None
ATT1_STAGE = 2


class PhysSem:
    def __init__(self, nc, es, name):
        self.h = es.enter_context(nc.semaphore(name))
        self.total = 0


class Sem:

    def __init__(self, phys):
        self.p = phys
        self.base = phys.total
        self.v = 0

    def inc(self, instr, n=1):
        instr.then_inc(self.p.h, n)
        self.p.total += n
        self.v += n
        return instr

    def wait(self, eng, v=None):
        eng.wait_ge(self.p.h, self.base + (self.v if v is None else v))


class KB:

    def __init__(self, dbg=(), phases=None):
        self.es = ExitStack()
        self.nc = bass.Bass("TRN2", target_bir_lowering=False)
        self.dbg = set(dbg)
        self.phases = phases
        nc = self.nc
        self.te, self.ve, self.se = nc.tensor, nc.vector, nc.scalar
        self.ge, self.sy = nc.gpsimd, nc.sync
        self.sems = {}
        self._phys = {}
        self.drams = {}
        self.inputs = set()
        self._chain_n = 0
        self.ps = self.es.enter_context(nc.psum_tensor("ps_arena", [128, 4096], F32))
        self._consts = {}
        self._const_pool = self.es.enter_context(
            nc.sbuf_tensor("const_pool", [128, 64], F32))
        self._const_n = 0
        init = self.sem("init")
        self.ones_col = self.es.enter_context(nc.sbuf_tensor("ones_col", [128, 1], BF16))
        self.ones_row = self.es.enter_context(nc.sbuf_tensor("ones_row", [1, 128], BF16))
        self.ve.memset(self.ones_col[:], 1.0)
        init.inc(self.ve.memset(self.ones_row[:], 1.0))

    def constf(self, val, rows=128):
        key = (float(val), rows)
        if key not in self._consts:
            assert not getattr(self, "_init_done", False), \
                "register consts before wait_init()"
            col = self._const_n
            self._const_n += 1
            ap = self._const_pool[0:rows, col:col + 1]
            init = self.sem("init")
            init.inc(self.ve.memset(self._const_pool[:, col:col + 1], float(val)))
            self._consts[key] = ap
        return self._consts[key]

    def wait_init(self):
        self._init_done = True
        init = self.sem("init")
        for eng in (self.te, self.ve, self.se, self.ge, self.sy):
            init.wait(eng)

    def sem(self, name):
        if name in self.sems:
            return self.sems[name]
        if name.startswith("chain") or name == "init":
            role = name
        else:
            role = name.split("_", 1)[-1] if "_" in name else name
        if role not in self._phys:
            self._phys[role] = PhysSem(self.nc, self.es, role)
        self.sems[name] = Sem(self._phys[role])
        return self.sems[name]

    def bank(self, b, nb=1, rows=128, row0=0):
        return self.ps[row0:row0 + rows, 512 * b:512 * b + 512 * nb]

    def chain(self, eng_name, instr):
        if SAFE_CHAIN:
            s = self.sem(f"chain_{eng_name}")
            s.inc(instr)
            eng = getattr(self, eng_name)
            s.wait(eng)
        return instr

    def din(self, name, shape, dtype=F32):
        if name not in self.drams:
            self.inputs.add(name)
            self.drams[name] = self.nc.dram_tensor(
                name, list(shape), dtype, kind="ExternalInput").ap()
        return self.drams[name]

    def dout(self, name, shape, dtype=F32):
        if name not in self.drams:
            self.drams[name] = self.nc.dram_tensor(
                name, list(shape), dtype, kind="ExternalOutput").ap()
        return self.drams[name]

    def dram(self, name, shape, dtype=BF16):
        if name not in self.drams:
            if name in self.dbg:
                return self.dout(name, shape, dtype)
            self.drams[name] = self.nc.dram_tensor(
                name, list(shape), dtype, kind="Internal").ap()
        return self.drams[name]

    def want(self, phase):
        return self.phases is None or phase in self.phases

    def barrier(self):
        self.nc.all_engine_barrier()

    def close(self):
        self.es.close()


def emit_ln(k: KB, tag, x_ap, xn_ap):
    nc, te, ve, se, sy, ge = k.nc, k.te, k.ve, k.se, k.sy, k.ge
    ld = k.sem(f"{tag}_ld")
    va = k.sem(f"{tag}_va")
    sa = k.sem(f"{tag}_sa")
    pa = k.sem(f"{tag}_pa")
    ev = k.sem(f"{tag}_ev")
    bc = k.sem(f"{tag}_bc")
    ld2 = k.sem(f"{tag}_ld2")
    vb = k.sem(f"{tag}_vb")
    st = k.sem(f"{tag}_st")
    with (
        nc.sbuf_tensor(f"{tag}_xc", [128, 2 * NT], F32) as xc,
        nc.sbuf_tensor(f"{tag}_xb", [128, 2 * NT], BF16) as xb,
        nc.sbuf_tensor(f"{tag}_xq", [128, 2 * NT], BF16) as xq,
        nc.sbuf_tensor(f"{tag}_s0", [1, NT], F32) as sums_h,
        nc.sbuf_tensor(f"{tag}_s1", [1, NT], F32) as sumsq_h,
        nc.sbuf_tensor(f"{tag}_s2", [1, NT], F32) as mrow_h,
        nc.sbuf_tensor(f"{tag}_s3", [1, NT], F32) as v4_h,
        nc.sbuf_tensor(f"{tag}_mB", [128, NT], F32) as mB,
        nc.sbuf_tensor(f"{tag}_rB", [128, NT], F32) as rB,
        nc.sbuf_tensor(f"{tag}_t1", [128, NT], F32) as t1,
        nc.sbuf_tensor(f"{tag}_xnb", [128, 2 * NT], BF16) as xnb,
    ):
        sums, sumsq, mrow, v4 = sums_h[:], sumsq_h[:], mrow_h[:], v4_h[:]

        for c in range(NCH):
            buf = c % 2
            sl = slice(buf * NT, (buf + 1) * NT)
            if c >= 2:
                va.wait(sy, v=c - 1)
                sa.wait(sy, v=c - 1)
            ld.inc(sy.dma_start(out=xc[:, sl], in_=x_ap[c * 128:(c + 1) * 128, :]), 16)
            ld.wait(ve)
            if c >= 2:
                pa.wait(ve, v=c - 1)
            va.inc(ve.tensor_copy(xb[:, sl], xc[:, sl]))
            ld.wait(se)
            if c >= 2:
                pa.wait(se, v=c - 1)
            sa.inc(se.activation(xq[:, sl], xc[:, sl], AF.Square))
            va.wait(te)
            sa.wait(te)
            last = None
            for j in range(8):
                last = te.matmul(k.bank(j, rows=1, row0=0), k.ones_col[:],
                                 xb[:, buf * NT + j * 512: buf * NT + (j + 1) * 512],
                                 start=(c == 0), stop=(c == NCH - 1))
            for j in range(8):
                last = te.matmul(k.bank(j, rows=1, row0=32), k.ones_col[:],
                                 xq[:, buf * NT + j * 512: buf * NT + (j + 1) * 512],
                                 start=(c == 0), stop=(c == NCH - 1))
            pa.inc(last)

        pa.wait(ve)
        k.chain("ve", ve.tensor_copy(sums, k.ps[0:1, 0:4096]))
        k.chain("ve", ve.tensor_copy(sumsq, k.ps[32:33, 0:4096]))
        k.chain("ve", ve.tensor_scalar(mrow, sums, 1.0 / D, None, op0=ALU.mult))
        ev.inc(ve.tensor_scalar(v4, sumsq, 1.0 / D, None, op0=ALU.mult))
        ev.wait(se)
        sq = k.sem(f"{tag}_sq")
        sq.inc(se.activation(sums, mrow, AF.Square))
        sq.wait(ve)
        ev.inc(ve.tensor_tensor(sumsq, v4, sums, ALU.subtract))
        ev.wait(se)
        sq.inc(se.activation(v4, sumsq, AF.Sqrt, bias=k.constf(EPS_LN, rows=1)))
        sq.wait(ve)
        k.chain("ve", ve.reciprocal(sums, v4))
        mrow_bf = xb[0:1, 0:NT]
        rrow_bf = xb[0:1, NT:2 * NT]
        k.chain("ve", ve.tensor_copy(mrow_bf, mrow))
        ev.inc(ve.tensor_copy(rrow_bf, sums))
        evb = k.sem(f"{tag}_evb")
        ev.wait(te)
        for r in range(2):
            src = mrow_bf if r == 0 else rrow_bf
            dst = mB if r == 0 else rB
            if r == 1:
                evb.wait(te, v=8)
            mmw = k.sem(f"{tag}_bb")
            for j in range(8):
                mmw.inc(te.matmul(k.bank(j), k.ones_row[:],
                                  src[:, j * 512:(j + 1) * 512],
                                  start=True, stop=True))
            mmw.wait(ve)
            for j in range(8):
                evb.inc(ve.tensor_copy(dst[:, j * 512:(j + 1) * 512], k.bank(j)))
        evb.wait(sy)
        evb.wait(ve)
        for c in range(NCH):
            buf = c % 2
            sl = slice(buf * NT, (buf + 1) * NT)
            if c >= 2:
                vb.wait(sy, v=c - 1)
            ld2.inc(sy.dma_start(out=xc[:, sl], in_=x_ap[c * 128:(c + 1) * 128, :]), 16)
            ld2.wait(ve)
            if c >= 2:
                st.wait(ve, v=16 * (c - 1))
            k.chain("ve", ve.tensor_tensor(t1[:], xc[:, sl], mB[:], ALU.subtract))
            vb.inc(ve.tensor_tensor(xnb[:, sl], t1[:], rB[:], ALU.mult))
            vb.wait(sy)
            st.inc(sy.dma_start(out=xn_ap[c * 128:(c + 1) * 128, :], in_=xnb[:, sl]), 16)
        st.wait(sy)


def emit_qkv0(k: KB, tag, xn_ap, wq_ap, wk_ap, wv_ap, q0_ap, k0_ap, v0_ap,
              bnd_ap):
    nc, te, ve, se, sy, ge = k.nc, k.te, k.ve, k.se, k.sy, k.ge
    xl = k.sem(f"{tag}_xl")
    wl = k.sem(f"{tag}_wl")
    smm = k.sem(f"{tag}_mm")
    sev = k.sem(f"{tag}_ev")
    ssq = k.sem(f"{tag}_sq")
    snm = k.sem(f"{tag}_nm")
    smx = k.sem(f"{tag}_mx")
    sst = k.sem(f"{tag}_st")
    with (
        nc.sbuf_tensor(f"{tag}_xn", [128, NCH * NT], BF16) as xn_sb,
        nc.sbuf_tensor(f"{tag}_wb", [128, 2 * NCH * 512], BF16) as wb,
        nc.sbuf_tensor(f"{tag}_evt", [128, 2 * 512], BF16) as evt,
        nc.sbuf_tensor(f"{tag}_sqt", [128, 2 * 512], BF16) as sqt,
        nc.sbuf_tensor(f"{tag}_mxr", [1, 512], F32) as mxr,
    ):
        for c in range(NCH):
            xl.inc(sy.dma_start(out=xn_sb[:, c * NT:(c + 1) * NT],
                                in_=xn_ap[c * 128:(c + 1) * 128, :]), 16)
        xl.wait(te)

        it = 0
        nqk_blocks = [(wq_ap, q0_ap, fb) for fb in range(8)] + \
                     [(wk_ap, k0_ap, fb) for fb in range(8)]
        for bi, (w_ap, dst_ap, fb) in enumerate(nqk_blocks):
            wbuf = bi % 2
            if bi >= 2:
                smm.wait(ge, v=(bi - 1) * 32)
            src = w_ap[:, fb * 512:(fb + 1) * 512]
            src = src.rearrange("(dc p) f -> p dc f", p=128)
            dst3 = wb[:, wbuf * NCH * 512:(wbuf + 1) * NCH * 512].rearrange(
                "p (dc f) -> p dc f", dc=NCH)
            wl.inc(ge.dma_start(out=dst3, in_=src), 16)
            for fc in range(4):
                for tb in range(8):
                    bank = it % 2
                    ebuf = it % 2
                    wl.wait(te, v=(bi + 1) * 16)
                    if it >= 2:
                        sev.wait(te, v=it - 1)
                    last = None
                    for dc in range(NCH):
                        lhsT = wb[:, wbuf * NCH * 512 + dc * 512 + fc * 128:
                                  wbuf * NCH * 512 + dc * 512 + fc * 128 + 128]
                        rhs = xn_sb[:, dc * NT + tb * 512: dc * NT + (tb + 1) * 512]
                        last = te.matmul(k.bank(bank), lhsT, rhs,
                                         start=(dc == 0), stop=(dc == NCH - 1))
                    smm.inc(last)
                    smm.wait(se, v=it + 1)
                    if it >= 2:
                        sst.wait(se, v=16 * (it - 1))
                        ssq.wait(se, v=it - 1)
                    sev.inc(se.copy(evt[:, ebuf * 512:(ebuf + 1) * 512], k.bank(bank)))
                    sev.wait(ve, v=it + 1)
                    if it >= 2:
                        snm.wait(ve, v=it - 1)
                    ssq.inc(ve.tensor_tensor(sqt[:, ebuf * 512:(ebuf + 1) * 512],
                                             evt[:, ebuf * 512:(ebuf + 1) * 512],
                                             evt[:, ebuf * 512:(ebuf + 1) * 512],
                                             ALU.mult))
                    ssq.wait(te, v=it + 1)
                    if it >= 2:
                        smx.wait(te, v=it - 1)
                    snm.inc(te.matmul(k.bank(2 + bank, rows=1), k.ones_col[:],
                                      sqt[:, ebuf * 512:(ebuf + 1) * 512],
                                      start=True, stop=True))
                    snm.wait(ve, v=it + 1)
                    chunk_id = bi * 4 + fc
                    smx.inc(ve.reduce_max(mxr[:, chunk_id * 8 + tb:
                                              chunk_id * 8 + tb + 1],
                                          k.bank(2 + bank, rows=1), axis=AXX))
                    sev.wait(sy, v=it + 1)
                    sst.inc(sy.dma_start(
                        out=dst_ap[fb * 512 + fc * 128: fb * 512 + (fc + 1) * 128,
                                   tb * 512:(tb + 1) * 512],
                        in_=evt[:, ebuf * 512:(ebuf + 1) * 512]), 16)
                    it += 1
        smx.wait(sy)
        sst.inc(sy.dma_start(out=bnd_ap, in_=mxr[:]), 16)

        st_snap = {}
        for vb in range(4):
            wbuf = vb % 2
            if vb >= 2:
                smm.wait(ge, v=len(nqk_blocks) * 32 + (vb - 1) * 32)
            else:
                smm.wait(ge, v=len(nqk_blocks) * 32)
            src = wv_ap[:, vb * 512:(vb + 1) * 512]
            src = src.rearrange("(dc p) f -> p dc f", p=128)
            dst3 = wb[:, wbuf * NCH * 512:(wbuf + 1) * NCH * 512].rearrange(
                "p (dc f) -> p dc f", dc=NCH)
            wl.inc(ge.dma_start(out=dst3, in_=src), 16)
            for tt in range(NTL):
                vit = vb * NTL + tt
                bank = it % 2
                ebuf = it % 2
                wl.wait(te, v=(len(nqk_blocks) + vb + 1) * 16)
                if it >= 2:
                    sev.wait(te, v=it - 1)
                last = None
                for dc in range(NCH):
                    lhsT = xn_sb[:, dc * NT + tt * 128: dc * NT + (tt + 1) * 128]
                    rhs = wb[:, wbuf * NCH * 512 + dc * 512:
                             wbuf * NCH * 512 + (dc + 1) * 512]
                    last = te.matmul(k.bank(bank), lhsT, rhs,
                                     start=(dc == 0), stop=(dc == NCH - 1))
                smm.inc(last)
                smm.wait(se, v=it + 1)
                if vit >= 2:
                    sst.wait(se, v=st_snap[vit - 2])
                else:
                    sst.wait(se, v=512 * 16 + 16)
                sev.inc(se.copy(evt[:, ebuf * 512:(ebuf + 1) * 512], k.bank(bank)))
                sev.wait(sy, v=it + 1)
                for hh in range(4):
                    h = vb * 4 + hh
                    sst.inc(sy.dma_start(
                        out=v0_ap[h * 128:(h + 1) * 128, tt * 128:(tt + 1) * 128],
                        in_=evt[:, ebuf * 512 + hh * 128:
                                ebuf * 512 + (hh + 1) * 128]), 16)
                st_snap[vit] = sst.v
                it += 1
        sst.wait(sy)


def emit_att0(k: KB, tag, q0_ap, k0_ap, v0_ap, bnd_ap, lam_ap, s0t_ap,
              factor_ap, rms_ap):
    nc, te, ve, se, sy, ge = k.nc, k.te, k.ve, k.se, k.sy, k.ge
    NH, SB = 16, 8
    NIT = NH * SB
    ld = k.sem(f"{tag}_ld")
    lq = k.sem(f"{tag}_lq")
    pmm = k.sem(f"{tag}_pmm")
    aex = k.sem(f"{tag}_aex")
    psu = k.sem(f"{tag}_psu")
    vsr = k.sem(f"{tag}_vsr")
    pb12 = k.sem(f"{tag}_pb12")
    aev = k.sem(f"{tag}_aev")
    vdf = k.sem(f"{tag}_vdf")
    ae3 = k.sem(f"{tag}_ae3")
    pav = k.sem(f"{tag}_pav")
    ps3 = k.sem(f"{tag}_ps3")
    vs3 = k.sem(f"{tag}_vs3")
    pb3 = k.sem(f"{tag}_pb3")
    rdm = k.sem(f"{tag}_rdm")
    vout = k.sem(f"{tag}_vout")
    vsq = k.sem(f"{tag}_vsq")
    prm = k.sem(f"{tag}_prm")
    vrm = k.sem(f"{tag}_vrm")
    sst = k.sem(f"{tag}_sst")
    vpr = k.sem(f"{tag}_vpr")
    asr = k.sem(f"{tag}_asr")
    snap = {}

    def put(name, i, sem):
        snap[(name, i)] = sem.v

    def at(name, i):
        return snap[(name, i)]

    from contextlib import ExitStack as _ES
    _les = _ES()

    def _sb(name, shape, dt):
        return _les.enter_context(nc.sbuf_tensor(name, shape, dt))

    E1 = _sb(f"{tag}_E1", [128, 2 * 16384], BF16)
    E2 = _sb(f"{tag}_E2", [128, 2 * 16384], BF16)
    Wq = _sb(f"{tag}_Wq", [128, 4096], BF16)
    K1 = _sb(f"{tag}_K1", [128, NT], BF16)
    K2 = _sb(f"{tag}_K2", [128, NT], BF16)
    Vt = _sb(f"{tag}_V", [128, NT], BF16)
    qt = _sb(f"{tag}_q", [128, 4 * 512], BF16)
    Rs = _sb(f"{tag}_Rs", [128, 2 * 1536], BF16)
    ob = _sb(f"{tag}_ob", [128, 2 * 512], BF16)
    sq2b = _sb(f"{tag}_sq2", [128, 2 * 512], BF16)
    sr = _sb(f"{tag}_sr", [1, 512], F32)[:]
    rr = _sb(f"{tag}_rr", [1, 512], F32)[:]
    rb = _sb(f"{tag}_rb", [1, 2 * 1024], BF16)
    rb2f = _sb(f"{tag}_r2f", [1, 512], F32)[:]
    s3r = _sb(f"{tag}_s3", [1, 2 * 512], F32)
    b3r = _sb(f"{tag}_b3", [1, 2 * 512], BF16)
    st1 = _sb(f"{tag}_st1", [1, 512], F32)[:]
    pr = _sb(f"{tag}_pr", [1, 640], F32)[:]
    prb = _sb(f"{tag}_prb", [1, 64], BF16)[:]
    biasB = _sb(f"{tag}_bias", [128, 64], F32)[:]

    bnd = pr[:, 0:512]
    n2 = pr[:, 512:576]
    nrm = pr[:, 576:640]
    lq.inc(sy.dma_start(out=bnd, in_=bnd_ap), 16)
    lam_sb = biasB[0:1, 48:64]
    lq.inc(sy.dma_start(out=lam_sb, in_=lam_ap), 16)
    lq.wait(ve)
    for cid in range(64):
        k.chain("ve", ve.reduce_max(n2[:, cid:cid + 1],
                                    bnd[:, cid * 8:(cid + 1) * 8], axis=AXX))
    vpr.inc(ve.tensor_copy(nrm, n2))
    vpr.wait(se)
    aprep = k.sem(f"{tag}_aprep")
    aprep.inc(se.activation(nrm, n2, AF.Sqrt))
    aprep.wait(ve)
    k.chain("ve", ve.tensor_tensor(pr[:, 0:32], nrm[:, 0:32], nrm[:, 32:64],
                                   ALU.mult))
    k.chain("ve", ve.tensor_scalar(pr[:, 0:32], pr[:, 0:32],
                                   -1.0 / float(np.sqrt(128.0)), None,
                                   op0=ALU.mult))
    k.chain("ve", ve.tensor_scalar(pr[:, 64:80], lam_sb, -1.0, None,
                                   op0=ALU.mult))
    vpr.inc(ve.tensor_copy(pr[:, 80:96], pr[:, 64:80]))
    vpr.wait(se)
    aprep.inc(se.activation(pr[:, 64:80], pr[:, 80:96], AF.Relu))
    aprep.wait(ve)
    k.chain("ve", ve.tensor_scalar(pr[:, 32:48], pr[:, 64:80], -1.0, -1.0,
                                   op0=ALU.mult, op1=ALU.add))
    mlap = biasB[0:1, 62:63]
    k.chain("ve", ve.reduce_sum(pr[:, 96:97], lam_sb, axis=AXX))
    k.chain("ve", ve.tensor_scalar(mlap, pr[:, 96:97], -1.0 / 16.0,
                                   1.0, op0=ALU.mult, op1=ALU.add))
    vpr.inc(ve.tensor_copy(prb[:, 0:48], pr[:, 0:48]))
    vpr.wait(te)
    pprep = k.sem(f"{tag}_pprep")
    pprep.inc(te.matmul(k.bank(0)[:, 0:48], k.ones_row[:], prb[:, 0:48],
                        start=True, stop=True))
    pprep.wait(ve)
    vpr.inc(ve.tensor_copy(biasB[:, 0:48], k.bank(0)[:, 0:48]))
    vpr.inc(ve.memset(st1, 0.0))
    vpr.wait(ge)
    for sb_ in range(8):
        rdm.inc(ge.dma_start(out=rms_ap[0:1, sb_ * 512:(sb_ + 1) * 512],
                             in_=st1), 16)
    put("rdm", -1, rdm)
    rdm.wait(ve)
    vpr.wait(te)
    vpr.wait(se)
    vpr.wait(sy)

    def emit_e3q(j, q_):
        h = j // SB
        P = j % 2
        eP = 16384 * P
        vdf.wait(se, v=at("vdfq", (j, q_)))
        qs = slice(eP + q_ * 4096, eP + (q_ + 1) * 4096)
        ae3.inc(se.activation(E1[:, qs], E2[:, qs], AF.Exp,
                              bias=biasB[:, 32 + h:33 + h]))
        put(("ae3q", q_), j, ae3)
        if q_ == 3:
            put("ae3", j, ae3)

    def emit_post(j):
        h, sb = j // SB, j % SB
        P = j % 2
        eP = 16384 * P
        AVB = 4 + P
        RB = 6 + P
        if j >= 2:
            vout.wait(te, v=at("vout", j - 2))
        ld.wait(te, v=at("ldv", j // SB))
        last = None
        for tt in range(NTL):
            if tt % 8 == 0:
                ae3.wait(te, v=at(("ae3q", tt // 8), j))
            last = te.matmul(k.bank(AVB), Vt[:, tt * 128:(tt + 1) * 128],
                             E1[:, eP + tt * 512:eP + (tt + 1) * 512],
                             start=(tt == 0), stop=(tt == NTL - 1))
        pav.inc(last)
        put("pav", j, pav)
        aev.wait(te, v=at("aevR2", j))
        last = None
        for tt in range(NTL):
            last = te.matmul(k.bank(RB, rows=1, row0=64), k.ones_col[:],
                             E1[:, eP + tt * 512:eP + (tt + 1) * 512],
                             start=(tt == 0), stop=(tt == NTL - 1))
        ps3.inc(last)
        put("ps3", j, ps3)
        ps3.wait(se, v=at("ps3", j))
        k.chain("se", se.copy(s3r[:, P * 512:(P + 1) * 512],
                              k.ps[64:65, 512 * RB:512 * RB + 512]))
        k.chain("se", se.activation(rr, s3r[:, P * 512:(P + 1) * 512], AF.Ln))
        vs3.inc(se.activation(b3r[:, P * 512:(P + 1) * 512], rr, AF.Exp,
                              scale=-1.0))
        put("vs3", j, vs3)
        vs3.wait(te, v=at("vs3", j))
        pb3.inc(te.matmul(k.bank(RB), k.ones_row[:],
                          b3r[:, P * 512:(P + 1) * 512],
                          start=True, stop=True))
        put("pb3", j, pb3)
        pb3.wait(se, v=at("pb3", j))
        aev.inc(se.copy(Rs[:, P * 1536 + 1024:P * 1536 + 1536], k.bank(RB)))
        put("aevR3", j, aev)
        pav.wait(ve, v=at("pav", j))
        aev.wait(ve, v=at("aevR3", j))
        if j >= 2:
            sst.wait(ve, v=at("sst", j - 2))
        vout.inc(ve.tensor_tensor(ob[:, 512 * P:512 * (P + 1)], k.bank(AVB),
                                  Rs[:, P * 1536 + 1024:P * 1536 + 1536],
                                  ALU.mult))
        put("vout", j, vout)
        if j >= 2:
            prm.wait(ve, v=at("prm", j - 2))
        vsq.inc(ve.tensor_tensor(sq2b[:, 512 * P:512 * (P + 1)],
                                 ob[:, 512 * P:512 * (P + 1)],
                                 ob[:, 512 * P:512 * (P + 1)], ALU.mult))
        put("vsq", j, vsq)
        vsq.wait(te, v=at("vsq", j))
        aev.wait(te, v=at("aevR3", j))
        prm.inc(te.matmul(k.bank(RB, rows=1, row0=32), k.ones_col[:],
                          sq2b[:, 512 * P:512 * (P + 1)],
                          start=True, stop=True))
        put("prm", j, prm)
        prm.wait(ve, v=at("prm", j))
        rdm.wait(ve, v=at("rdm", j - 1))
        vrm.inc(ve.tensor_copy(st1, k.ps[32:33, 512 * RB:512 * RB + 512]))
        put("vrm", j, vrm)
        vrm.wait(ge, v=at("vrm", j))
        rdm.inc(ge.dma_start(out=rms_ap[0:1, sb * 512:(sb + 1) * 512],
                             in_=st1, accum_op=ALU.add), 16)
        put("rdm", j, rdm)
        vout.wait(sy, v=at("vout", j))
        sst.inc(sy.dma_start(
            out=s0t_ap[h * 128:(h + 1) * 128, sb * 512:(sb + 1) * 512],
            in_=ob[:, 512 * P:512 * (P + 1)]), 16)
        put("sst", j, sst)

    NITL = min(NIT, ATT0_IT_LIMIT) if ATT0_IT_LIMIT else NIT
    for i in range(NITL):
        h, sb = i // SB, i % SB
        P = i % 2
        eP = 16384 * P
        RB = 6 + P
        if sb == 0:
            if h > 0:
                pmm.wait(sy, v=at("pmm", i - 1))
            ld.inc(sy.dma_start(out=K1[:],
                                in_=k0_ap[h * 256:h * 256 + 128, :]), 16)
            ld.inc(sy.dma_start(out=K2[:],
                                in_=k0_ap[h * 256 + 128:h * 256 + 256, :]), 16)
            if h == 0:
                ld.inc(sy.dma_start(out=Vt[:], in_=v0_ap[0:128, :]), 16)
                put("ldv", 0, ld)
        if i >= 2:
            pmm.wait(sy, v=at("pmm", i - 1))
        lq.inc(sy.dma_start(
            out=qt[:, P * 1024:P * 1024 + 512],
            in_=q0_ap[h * 256:h * 256 + 128, sb * 512:(sb + 1) * 512]), 16)
        lq.inc(sy.dma_start(
            out=qt[:, P * 1024 + 512:P * 1024 + 1024],
            in_=q0_ap[h * 256 + 128:h * 256 + 256, sb * 512:(sb + 1) * 512]),
            16)

        ld.wait(te)
        lq.wait(te)
        for u in range(32):
            sc = u % 2
            g = u // 2
            reg = u % 2
            Ksb = K1 if sc == 0 else K2
            qv = qt[:, P * 1024 + sc * 512: P * 1024 + (sc + 1) * 512]
            wv_ = 32 * i + u - 1
            if wv_ > 0:
                aex.wait(te, v=wv_)
            last = None
            for tj in range(2):
                tt = g * 2 + tj
                last = te.matmul(k.bank(2 * reg + tj),
                                 Ksb[:, tt * 128:(tt + 1) * 128], qv,
                                 start=True, stop=True)
            pmm.inc(last)
        put("pmm", i, pmm)

        hc1 = h * 2
        for u in range(32):
            sc = u % 2
            g = u // 2
            reg = u % 2
            if i >= 1 and not ATT0_NO_POST and u in (10, 16, 22, 28):
                emit_e3q(i - 1, (u - 10) // 6)
            pmm.wait(se, v=at("pmm", i) - 32 + u + 1)
            dstE = E1 if sc == 0 else E2
            if i >= 2 and u < 2:
                vdf.wait(se, v=at("vdf", i - 2))
                psu.wait(se, v=at("psu", i - 2))
                if ("ps3", i - 2) in snap:
                    pav.wait(se, v=at("pav", i - 2))
                    ps3.wait(se, v=at("ps3", i - 2))
            aex.inc(se.activation(
                dstE[:, eP + g * 1024: eP + (g + 1) * 1024],
                k.bank(2 * reg, nb=2), AF.Exp,
                bias=biasB[:, hc1 + sc:hc1 + sc + 1],
                scale=float(1.0 / np.sqrt(128.0))))
        put("aexe", i, aex)

        if i >= 1 and not ATT0_NO_POST:
            emit_post(i - 1)
        if sb == 0 and h > 0:
            if ("pav", i - 1) in snap:
                pav.wait(sy, v=at("pav", i - 1))
            ld.inc(sy.dma_start(out=Vt[:],
                                in_=v0_ap[h * 128:(h + 1) * 128, :]), 16)
            put("ldv", h, ld)

        aex.wait(te)
        if i >= 2:
            vrm.wait(te, v=at("vrm", i - 2))
        first = True
        last = None
        for tt in range(NTL):
            last = te.matmul(k.bank(RB, rows=1), k.ones_col[:],
                             E1[:, eP + tt * 512:eP + (tt + 1) * 512],
                             start=first, stop=False)
            first = False
            last = te.matmul(k.bank(RB, rows=1, row0=32), k.ones_col[:],
                             E2[:, eP + tt * 512:eP + (tt + 1) * 512],
                             start=False, stop=(tt == NTL - 1))
        psu.inc(last)
        put("psu", i, psu)
        psu.wait(se, v=at("psu", i))
        k.chain("se", se.copy(sr, k.ps[0:1, 512 * RB:512 * RB + 512]))
        k.chain("se", se.activation(rr, sr, AF.Ln))
        k.chain("se", se.activation(rb[0:1, P * 1024:P * 1024 + 512], rr,
                                    AF.Exp, scale=-1.0))
        k.chain("se", se.copy(sr, k.ps[32:33, 512 * RB:512 * RB + 512]))
        k.chain("se", se.activation(rr, sr, AF.Ln))
        asr.inc(se.activation(rb2f, rr, AF.Exp, scale=-1.0))
        put("asr", i, asr)
        asr.wait(ve, v=at("asr", i))
        vsr.inc(ve.tensor_scalar(rb[0:1, P * 1024 + 512:P * 1024 + 1024], rb2f,
                                 lam_sb[:, h:h + 1], None, op0=ALU.mult))
        put("vsr", i, vsr)
        vsr.wait(te, v=at("vsr", i))
        pb12.inc(te.matmul(k.bank(RB), k.ones_row[:],
                           rb[0:1, P * 1024:P * 1024 + 512],
                           start=True, stop=True))
        pb12.wait(se)
        aev.inc(se.copy(Rs[:, P * 1536:P * 1536 + 512], k.bank(RB)))
        put("aevR1", i, aev)
        aev.wait(te, v=at("aevR1", i))
        pb12.inc(te.matmul(k.bank(RB), k.ones_row[:],
                           rb[0:1, P * 1024 + 512:P * 1024 + 1024],
                           start=True, stop=True))
        pb12.wait(se)
        aev.inc(se.copy(Rs[:, P * 1536 + 512:P * 1536 + 1024], k.bank(RB)))
        put("aevR2", i, aev)
        put("pb12", i, pb12)

        aev.wait(ve, v=at("aevR2", i))
        R1v = Rs[:, P * 1536:P * 1536 + 512]
        R2v = Rs[:, P * 1536 + 512:P * 1536 + 1024]
        for q_ in range(4):
            for tt in range(q_ * 8, (q_ + 1) * 8):
                sl = slice(eP + tt * 512, eP + (tt + 1) * 512)
                wsl = slice((tt - q_ * 8) * 512, (tt - q_ * 8 + 1) * 512)
                k.chain("ve", ve.tensor_tensor(Wq[:, wsl], E1[:, sl], R1v,
                                               ALU.mult))
            for tt in range(q_ * 8, (q_ + 1) * 8):
                sl = slice(eP + tt * 512, eP + (tt + 1) * 512)
                k.chain("ve", ve.tensor_tensor(E1[:, sl], E2[:, sl], R2v,
                                               ALU.mult))
            for tt in range(q_ * 8, (q_ + 1) * 8):
                sl = slice(eP + tt * 512, eP + (tt + 1) * 512)
                wsl = slice((tt - q_ * 8) * 512, (tt - q_ * 8 + 1) * 512)
                last = ve.tensor_tensor(E2[:, sl], Wq[:, wsl], E1[:, sl],
                                        ALU.subtract)
                if tt != (q_ + 1) * 8 - 1:
                    k.chain("ve", last)
            vdf.inc(last)
            put("vdfq", (i, q_), vdf)
        put("vdf", i, vdf)

    if not ATT0_NO_POST:
        for q_ in range(4):
            emit_e3q(NITL - 1, q_)
        emit_post(NITL - 1)

    rdm.wait(sy)
    lf = k.sem(f"{tag}_lf")
    vvf = k.sem(f"{tag}_vvf")
    afc = k.sem(f"{tag}_afc")
    for cch in range(8):
        csl = slice(cch * 512, (cch + 1) * 512)
        if cch >= 1:
            sst.wait(sy, v=at("fst", cch - 1))
        lf.inc(sy.dma_start(out=rr, in_=rms_ap[0:1, csl]), 16)
        lf.wait(ve)
        k.chain("ve", ve.tensor_scalar(sr, rr, 1.0 / D, None, op0=ALU.mult))
        vvf.inc(ve.tensor_copy(rr, sr))
        vvf.wait(se)
        afc.inc(se.activation(sr, rr, AF.Sqrt, bias=k.constf(EPS_RMS, rows=1)))
        afc.wait(ve)
        k.chain("ve", ve.reciprocal(rr, sr))
        vvf.inc(ve.tensor_scalar(sr, rr, mlap, None, op0=ALU.mult))
        vvf.wait(sy)
        sst.inc(sy.dma_start(out=factor_ap[0:1, csl], in_=sr), 16)
        snap[("fst", cch)] = sst.v
    sst.wait(sy)
    _les.close()


def emit_wo0(k: KB, tag, s0t_ap, factor_ap, wo_ap, ns_ap, s0_ap, n_tok=NT,
             d_out=D, use_ns=True, use_factor=True):
    nc, te, ve, se, sy, ge = k.nc, k.te, k.ve, k.se, k.sy, k.ge
    wl = k.sem(f"{tag}_wl")
    pl = k.sem(f"{tag}_pl")
    fl = k.sem(f"{tag}_fl")
    vsc = k.sem(f"{tag}_vsc")
    pmm = k.sem(f"{tag}_pmm")
    aev = k.sem(f"{tag}_aev")
    pfb = k.sem(f"{tag}_pfb")
    sst = k.sem(f"{tag}_sst")
    NOC = d_out // 128
    NMAC = n_tok // 2048
    snap = {}
    with (
        nc.sbuf_tensor(f"{tag}_wo", [128, NCH * d_out], BF16) as wo_sb,
        nc.sbuf_tensor(f"{tag}_pan", [128, NCH * 2048], BF16) as pan,
        nc.sbuf_tensor(f"{tag}_sc", [128, 2 * NCH * 512], BF16) as scl,
        nc.sbuf_tensor(f"{tag}_fb", [128, 2048], BF16) as fb,
        nc.sbuf_tensor(f"{tag}_fr", [1, 2048], F32) as fr_h,
        nc.sbuf_tensor(f"{tag}_frb", [1, 2048], BF16) as frb_h,
        nc.sbuf_tensor(f"{tag}_ns", [128, 16], F32) as ns_sb,
        nc.sbuf_tensor(f"{tag}_ev", [128, 2 * 512], BF16) as evt,
    ):
        fr, frb = fr_h[:], frb_h[:]
        for fc in range(NCH):
            wl.inc(ge.dma_start(out=wo_sb[:, fc * d_out:(fc + 1) * d_out],
                                in_=wo_ap[fc * 128:(fc + 1) * 128, :]), 16)
        if use_ns:
            wl.inc(ge.dma_start(out=ns_sb[:], in_=ns_ap), 16)
        nit = 0
        noc_it = 0
        for mac in range(NMAC):
            msl = slice(mac * 2048, (mac + 1) * 2048)
            if mac > 0:
                pmm.wait(sy, v=snap[("pmm_mac", mac - 1)])
            for fc in range(NCH):
                pl.inc(sy.dma_start(out=pan[:, fc * 2048:(fc + 1) * 2048],
                                    in_=s0t_ap[fc * 128:(fc + 1) * 128, msl]), 16)
            if use_factor:
                fl.inc(sy.dma_start(out=fr, in_=factor_ap[0:1, msl]), 16)
                fl.wait(ve)
                if mac > 0:
                    vsc.wait(ve, v=snap[("vsc_mac", mac - 1)])
                k.chain("ve", ve.tensor_copy(frb, fr))
                vsc.inc(ve.tensor_copy(frb[:, 0:1], fr[:, 0:1]))
                vsc.wait(te)
                lastb = None
                for j in range(4):
                    lastb = te.matmul(k.bank(4 + j), k.ones_row[:],
                                      frb[:, j * 512:(j + 1) * 512],
                                      start=True, stop=True)
                pfb.inc(lastb)
                pfb.wait(ve)
                for j in range(4):
                    k.chain("ve", ve.tensor_copy(fb[:, j * 512:(j + 1) * 512],
                                                 k.bank(4 + j)))
            for nb in range(4):
                nbs = slice(nb * 512, (nb + 1) * 512)
                sb_ = nit % 2
                ssl0 = sb_ * NCH * 512
                pl.wait(ve)
                if nit >= 2:
                    pmm.wait(ve, v=snap[("pmm_nit", nit - 2)])
                last = None
                for fc in range(NCH):
                    src = pan[:, fc * 2048 + nb * 512: fc * 2048 + (nb + 1) * 512]
                    dst = scl[:, ssl0 + fc * 512:ssl0 + (fc + 1) * 512]
                    if use_ns:
                        k.chain("ve", ve.tensor_scalar(
                            dst, src, ns_sb[:, fc:fc + 1], None, op0=ALU.mult))
                        if use_factor:
                            last = ve.tensor_tensor(dst, dst, fb[:, nbs], ALU.mult)
                    elif use_factor:
                        last = ve.tensor_tensor(dst, src, fb[:, nbs], ALU.mult)
                    else:
                        last = ve.tensor_copy(dst, src)
                    if fc < NCH - 1:
                        k.chain("ve", last)
                vsc.inc(last)
                snap[("vsc_nit", nit)] = vsc.v
                vsc.wait(te)
                wl.wait(te)
                for oc in range(NOC):
                    bank = noc_it % 2
                    if noc_it >= 2:
                        aev.wait(te, v=noc_it - 1)
                    last = None
                    for fc in range(NCH):
                        last = te.matmul(
                            k.bank(bank),
                            wo_sb[:, fc * d_out + oc * 128:
                                  fc * d_out + (oc + 1) * 128],
                            scl[:, ssl0 + fc * 512:ssl0 + (fc + 1) * 512],
                            start=(fc == 0), stop=(fc == NCH - 1))
                    pmm.inc(last)
                    pmm.wait(se)
                    if noc_it >= 2:
                        sst.wait(se, v=16 * (noc_it - 1))
                    aev.inc(se.copy(evt[:, bank * 512:(bank + 1) * 512],
                                    k.bank(bank)))
                    aev.wait(sy)
                    sst.inc(sy.dma_start(
                        out=s0_ap[oc * 128:(oc + 1) * 128,
                                  mac * 2048 + nb * 512:
                                  mac * 2048 + (nb + 1) * 512],
                        in_=evt[:, bank * 512:(bank + 1) * 512]), 16)
                    noc_it += 1
                snap[("pmm_nit", nit)] = pmm.v
                nit += 1
            snap[("pmm_mac", mac)] = pmm.v
            snap[("vsc_mac", mac)] = vsc.v
        sst.wait(sy)


def emit_ffn0(k: KB, tag, xn_ap, wT_ap, a_ap, b_ap, rt_ap, rep_ap, id_ap,
              g0_ap):
    nc, te, ve, se, sy, ge = k.nc, k.te, k.ve, k.se, k.sy, k.ge
    wl = k.sem(f"{tag}_wl")
    pl = k.sem(f"{tag}_pl")
    prt = k.sem(f"{tag}_prt")
    vrt = k.sem(f"{tag}_vrt")
    art = k.sem(f"{tag}_art")
    ptr = k.sem(f"{tag}_ptr")
    aev = k.sem(f"{tag}_aev")
    prp = k.sem(f"{tag}_prp")
    pax = k.sem(f"{tag}_pax")
    vax = k.sem(f"{tag}_vax")
    pmm = k.sem(f"{tag}_pmm")
    asl = k.sem(f"{tag}_asl")
    sst = k.sem(f"{tag}_sst")
    snap = {}
    with (
        nc.sbuf_tensor(f"{tag}_xn", [128, NCH * 2048], BF16) as xp,
        nc.sbuf_tensor(f"{tag}_wt", [128, NCH * D], BF16) as wt,
        nc.sbuf_tensor(f"{tag}_A", [128, NCH * 128], BF16) as A_sb,
        nc.sbuf_tensor(f"{tag}_B", [128, D], BF16) as B_sb,
        nc.sbuf_tensor(f"{tag}_B2", [128, D], BF16) as B2,
        nc.sbuf_tensor(f"{tag}_rt", [128, NCH * 8], BF16) as rt_sb,
        nc.sbuf_tensor(f"{tag}_rep", [8, 128], BF16) as rep_sb,
        nc.sbuf_tensor(f"{tag}_id", [128, 128], BF16) as id_sb,
        nc.sbuf_tensor(f"{tag}_wfm", [8, 2048], BF16) as wfm,
        nc.sbuf_tensor(f"{tag}_wrp", [128, 2048], BF16) as wrp,
        nc.sbuf_tensor(f"{tag}_axw", [128, 2048], BF16) as axw,
        nc.sbuf_tensor(f"{tag}_r1", [128, 64], F32) as r1_h,
        nc.sbuf_tensor(f"{tag}_rb1", [128, 8], BF16) as rb1_h,
        nc.sbuf_tensor(f"{tag}_ev", [128, 2 * 512], BF16) as evt,
    ):
        r1 = r1_h[:]
        rb1 = rb1_h[:]
        for fc in range(NCH):
            wl.inc(ge.dma_start(out=wt[:, fc * D:(fc + 1) * D],
                                in_=wT_ap[fc * 128:(fc + 1) * 128, :]), 16)
        wl.inc(ge.dma_start(out=A_sb[:].rearrange("p (c f) -> p c f", c=NCH),
                            in_=a_ap.rearrange("(c p) f -> p c f", p=128)), 16)
        wl.inc(ge.dma_start(out=B_sb[:], in_=b_ap), 16)
        wl.inc(ge.dma_start(out=rt_sb[:].rearrange("p (c f) -> p c f", c=NCH),
                            in_=rt_ap.rearrange("(c p) f -> p c f", p=128)), 16)
        wl.inc(ge.dma_start(out=rep_sb[:], in_=rep_ap), 16)
        wl.inc(ge.dma_start(out=id_sb[:], in_=id_ap), 16)
        wl.wait(ve)
        vB2 = k.sem(f"{tag}_vB2")
        vB2.inc(ve.tensor_scalar(B2[:], B_sb[:], 2.0, None, op0=ALU.mult))
        vB2.wait(te)

        for mac in range(NT // 2048):
            msl = slice(mac * 2048, (mac + 1) * 2048)
            if mac > 0:
                if ("pmm_mac", mac - 1) in snap:
                    pmm.wait(sy, v=snap[("pmm_mac", mac - 1)])
                if ("pax_mac", mac - 1) in snap:
                    pax.wait(sy, v=snap[("pax_mac", mac - 1)])
                prt.wait(sy, v=snap[("prt_mac", mac - 1)] if
                         ("prt_mac", mac - 1) in snap else prt.v)
            for fc in range(NCH):
                pl.inc(sy.dma_start(out=xp[:, fc * 2048:(fc + 1) * 2048],
                                    in_=xn_ap[fc * 128:(fc + 1) * 128, msl]), 16)
            pl.wait(te)
            wl.wait(te)
            for tt in range(16):
                if mac + tt > 0:
                    prev = (mac, tt - 1) if tt else (mac - 1, 15)
                    vrt.wait(te, v=snap[("vrt", prev)])
                    art.wait(te, v=snap[("art", prev)])
                last = None
                for fc in range(NCH):
                    last = te.matmul(k.bank(0)[:, 0:8],
                                     xp[:, fc * 2048 + tt * 128:
                                        fc * 2048 + (tt + 1) * 128],
                                     rt_sb[:, fc * 8:(fc + 1) * 8],
                                     start=(fc == 0), stop=(fc == NCH - 1))
                prt.inc(last)
                prt.wait(ve)
                e_t = r1[:, 0:8]
                probs = r1[:, 8:16]
                eq1 = r1[:, 16:24]
                pm = r1[:, 24:32]
                eq2 = r1[:, 32:40]
                wsel = r1[:, 40:48]
                eq1s = r1[:, 56:64]
                mx = r1[:, 48:49]
                mxn = r1[:, 49:50]
                sm = r1[:, 50:51]
                rs = r1[:, 51:52]
                m1 = r1[:, 52:53]
                m2 = r1[:, 53:54]
                ws = r1[:, 54:55]
                rw = r1[:, 55:56]
                k.chain("ve", ve.reduce_max(mx, k.bank(0)[:, 0:8], axis=AXX))
                vrt.inc(ve.tensor_scalar(mxn, mx, -1.0, None, op0=ALU.mult))
                vrt.wait(se)
                art.inc(se.activation(e_t, k.bank(0)[:, 0:8], AF.Exp, bias=mxn,
                                      accum_out=sm))
                snap[("art", (mac, tt))] = art.v
                art.wait(ve)
                k.chain("ve", ve.reciprocal(rs, sm))
                k.chain("ve", ve.tensor_scalar(probs, e_t, rs, None, op0=ALU.mult))
                k.chain("ve", ve.reduce_max(m1, probs, axis=AXX))
                k.chain("ve", ve.tensor_scalar(eq1, probs, m1, None,
                                               op0=ALU.is_equal))
                k.chain("ve", ve.tensor_scalar(eq1s, eq1, -2.0, None, op0=ALU.mult))
                k.chain("ve", ve.tensor_tensor(pm, probs, eq1s, ALU.add))
                k.chain("ve", ve.reduce_max(m2, pm, axis=AXX))
                k.chain("ve", ve.tensor_scalar(eq2, probs, m2, None,
                                               op0=ALU.is_equal))
                k.chain("ve", ve.tensor_tensor(pm, eq1, eq2, ALU.add))
                k.chain("ve", ve.tensor_tensor(wsel, probs, pm, ALU.mult))
                k.chain("ve", ve.tensor_tensor(ws, m1, m2, ALU.add))
                k.chain("ve", ve.reciprocal(rw, ws))
                vrt.inc(ve.tensor_scalar(rb1, wsel, rw, None, op0=ALU.mult))
                snap[("vrt", (mac, tt))] = vrt.v
                vrt.wait(te)
                if mac + tt > 0:
                    aev.wait(te, v=snap[("aevt", (mac, tt - 1) if tt
                                         else (mac - 1, 15))])
                ptr.inc(te.matmul(k.bank(1, rows=8)[:, 0:128], rb1, id_sb[:],
                                  start=True, stop=True))
                ptr.wait(se)
                aev.inc(se.copy(wfm[:, tt * 128:(tt + 1) * 128],
                                k.bank(1, rows=8)[:, 0:128]))
                snap[("aevt", (mac, tt))] = aev.v
            snap[("prt_mac", mac)] = prt.v
            if FFN0_STAGE < 2:
                continue
            aev.wait(te)
            if mac > 0:
                vax.wait(se, v=snap[("vax_mac", mac - 1)])
            for nb in range(4):
                if nb > 0:
                    aev.wait(te)
                prp.inc(te.matmul(k.bank(2), rep_sb[:],
                                  wfm[:, nb * 512:(nb + 1) * 512],
                                  start=True, stop=True))
                prp.wait(se)
                aev.inc(se.copy(wrp[:, nb * 512:(nb + 1) * 512], k.bank(2)))
            aev.wait(ve)
            if mac > 0 and ("pmm_mac", mac - 1) in snap:
                pmm.wait(ve, v=snap[("pmm_mac", mac - 1)])
            for nb in range(4):
                vax.wait(te)
                last = None
                for fc in range(NCH):
                    last = te.matmul(k.bank(3), A_sb[:, fc * 128:(fc + 1) * 128],
                                     xp[:, fc * 2048 + nb * 512:
                                        fc * 2048 + (nb + 1) * 512],
                                     start=(fc == 0), stop=(fc == NCH - 1))
                pax.inc(last)
                pax.wait(ve)
                vax.inc(ve.tensor_tensor(axw[:, nb * 512:(nb + 1) * 512],
                                         k.bank(3), wrp[:, nb * 512:(nb + 1) * 512],
                                         ALU.mult))
            snap[("vax_mac", mac)] = vax.v
            snap[("pax_mac", mac)] = pax.v
            if FFN0_STAGE < 3:
                continue
            vax.wait(te)
            vB2.wait(te)
            bit = mac * NCH * 4
            for oc in range(NCH):
                for nb in range(4):
                    bank = 4 + bit % 2
                    if bit >= 2:
                        asl.wait(te, v=snap[("aslG", bit - 2)])
                    last = None
                    for fc in range(NCH):
                        last = te.matmul(
                            k.bank(bank),
                            wt[:, fc * D + oc * 128: fc * D + (oc + 1) * 128],
                            xp[:, fc * 2048 + nb * 512:
                               fc * 2048 + (nb + 1) * 512],
                            start=(fc == 0), stop=False)
                    last = te.matmul(k.bank(bank), B2[:, oc * 128:(oc + 1) * 128],
                                     axw[:, nb * 512:(nb + 1) * 512],
                                     start=False, stop=True)
                    pmm.inc(last)
                    pmm.wait(se)
                    if bit >= 2:
                        sst.wait(se, v=snap[("sstG", bit - 2)])
                    asl.inc(se.activation(evt[:, (bit % 2) * 512:
                                              (bit % 2 + 1) * 512],
                                          k.bank(bank), AF.Silu))
                    snap[("aslG", bit)] = asl.v
                    asl.wait(sy)
                    sst.inc(sy.dma_start(
                        out=g0_ap[oc * 128:(oc + 1) * 128,
                                  mac * 2048 + nb * 512:
                                  mac * 2048 + (nb + 1) * 512],
                        in_=evt[:, (bit % 2) * 512:(bit % 2 + 1) * 512]), 16)
                    snap[("sstG", bit)] = sst.v
                    bit += 1
            snap[("pmm_mac", mac)] = pmm.v
        sst.wait(sy)


def emit_combine(k: KB, tag, x_ap, s_ap, g_ap, xout_ap, n_tok=NT,
                 g_silu=False):
    nc, te, ve, se, sy, ge = k.nc, k.te, k.ve, k.se, k.sy, k.ge
    ld = k.sem(f"{tag}_ld")
    vc = k.sem(f"{tag}_vc")
    agl = k.sem(f"{tag}_agl")
    sst = k.sem(f"{tag}_sst")
    gdt = F32 if g_silu else BF16
    with (
        nc.sbuf_tensor(f"{tag}_x", [128, 2 * n_tok], F32) as xc,
        nc.sbuf_tensor(f"{tag}_s", [128, 2 * n_tok], BF16) as sc,
        nc.sbuf_tensor(f"{tag}_g", [128, 2 * n_tok], gdt) as gc,
        nc.sbuf_tensor(f"{tag}_gs", [128, n_tok], BF16) as gs,
        nc.sbuf_tensor(f"{tag}_sg", [128, n_tok], BF16) as sg,
        nc.sbuf_tensor(f"{tag}_o", [128, 2 * n_tok], F32) as oc_,
    ):
        for c in range(NCH):
            buf = c % 2
            sl = slice(buf * n_tok, (buf + 1) * n_tok)
            if c >= 2:
                vc.wait(sy, v=c - 1)
                if g_silu:
                    agl.wait(sy, v=c - 1)
            ld.inc(sy.dma_start(out=xc[:, sl],
                                in_=x_ap[c * 128:(c + 1) * 128, :]), 16)
            ld.inc(sy.dma_start(out=sc[:, sl],
                                in_=s_ap[c * 128:(c + 1) * 128, :]), 16)
            ld.inc(sy.dma_start(out=gc[:, sl],
                                in_=g_ap[c * 128:(c + 1) * 128, :]), 16)
            if g_silu:
                ld.wait(se)
                if c >= 1:
                    vc.wait(se, v=c)
                agl.inc(se.activation(gs[:], gc[:, sl], AF.Silu))
                agl.wait(ve)
            ld.wait(ve)
            if c >= 2:
                sst.wait(ve, v=16 * (c - 1))
            gsrc = gs[:] if g_silu else gc[:, sl]
            k.chain("ve", ve.tensor_tensor(sg[:], sc[:, sl], gsrc, ALU.mult))
            vc.inc(ve.tensor_tensor(oc_[:, sl], xc[:, sl], sg[:], ALU.add))
            vc.wait(sy)
            sst.inc(sy.dma_start(out=xout_ap[c * 128:(c + 1) * 128, :],
                                 in_=oc_[:, sl]), 16)
        sst.wait(sy)


def emit_qkv1(k: KB, tag, xn_ap, wq_ap, wk_ap, wv_ap, cos_ap, sin_ap, psw_ap,
              q1_ap, k1_ap, v1_ap, bnd_ap):
    nc, te, ve, se, sy, ge = k.nc, k.te, k.ve, k.se, k.sy, k.ge
    xl = k.sem(f"{tag}_xl")
    wl = k.sem(f"{tag}_wl")
    smm = k.sem(f"{tag}_mm")
    sev = k.sem(f"{tag}_ev")
    psw = k.sem(f"{tag}_sw")
    vro = k.sem(f"{tag}_ro")
    pnm = k.sem(f"{tag}_nm")
    vmx = k.sem(f"{tag}_mx")
    vsq = k.sem(f"{tag}_sq")
    sst = k.sem(f"{tag}_st")
    snap = {}
    with (
        nc.sbuf_tensor(f"{tag}_xn", [128, NCH * NT], BF16) as xn_sb,
        nc.sbuf_tensor(f"{tag}_wb", [128, 2 * NCH * 512], BF16) as wb,
        nc.sbuf_tensor(f"{tag}_cos", [128, NT], F32) as cos_sb,
        nc.sbuf_tensor(f"{tag}_sin", [128, NT], F32) as sin_sb,
        nc.sbuf_tensor(f"{tag}_psw", [128, 128], BF16) as psw_sb,
        nc.sbuf_tensor(f"{tag}_ho", [128, 2], BF16) as ho_sb,
        nc.sbuf_tensor(f"{tag}_qe", [128, 2 * 512], BF16) as qe,
        nc.sbuf_tensor(f"{tag}_u1", [128, 512], F32) as u1_h,
        nc.sbuf_tensor(f"{tag}_u2", [128, 512], F32) as u2_h,
        nc.sbuf_tensor(f"{tag}_rq", [128, 2 * 512], BF16) as rq,
        nc.sbuf_tensor(f"{tag}_sq2", [128, 2 * 512], BF16) as sq2,
        nc.sbuf_tensor(f"{tag}_mxr", [2, 512], F32) as mxr_h,
    ):
        u1, u2, mxr = u1_h[:], u2_h[:], mxr_h[:]
        init2 = k.sem(f"{tag}_in2")
        k.ve.memset(ho_sb[:], 0.0)
        k.ve.memset(ho_sb[0:64, 0:1], 1.0)
        init2.inc(k.ve.memset(ho_sb[64:128, 1:2], 1.0))
        init2.wait(te)
        for c in range(NCH):
            xl.inc(sy.dma_start(out=xn_sb[:, c * NT:(c + 1) * NT],
                                in_=xn_ap[c * 128:(c + 1) * 128, :]), 16)
        xl.inc(sy.dma_start(out=cos_sb[:], in_=cos_ap), 16)
        xl.inc(sy.dma_start(out=sin_sb[:], in_=sin_ap), 16)
        xl.inc(ge.dma_start(out=psw_sb[:], in_=psw_ap), 16)
        xl.wait(te)

        blocks = [("q", wq_ap, q1_ap, 4, 4, 0), ("k", wk_ap, k1_ap, 1, 8, 64)]
        it = 0
        for name, w_ap, dst_ap, nfb, ntb, nslot0 in blocks:
            for fb in range(nfb):
                bi = it
                wbuf = (len(snap) * 0 + fb + (0 if name == "q" else 4)) % 2
                if ("wld", wbuf) in snap:
                    smm.wait(ge, v=snap[("wld", wbuf)])
                src = w_ap[:, fb * 512:(fb + 1) * 512]
                src = src.rearrange("(dc p) f -> p dc f", p=128)
                dst3 = wb[:, wbuf * NCH * 512:(wbuf + 1) * NCH * 512].rearrange(
                    "p (dc f) -> p dc f", dc=NCH)
                wl.inc(ge.dma_start(out=dst3, in_=src), 16)
                wseen = wl.v
                for fc in range(4):
                    for tb in range(ntb):
                        buf = it % 2
                        bank = it % 2
                        wl.wait(te, v=wseen)
                        if it >= 2:
                            sev.wait(te, v=snap[("sev", it - 2)])
                        last = None
                        for dc in range(NCH):
                            lhsT = wb[:, wbuf * NCH * 512 + dc * 512 + fc * 128:
                                      wbuf * NCH * 512 + dc * 512 + (fc + 1) * 128]
                            rhs = xn_sb[:, dc * NT + tb * 512:
                                        dc * NT + (tb + 1) * 512]
                            last = te.matmul(k.bank(bank), lhsT, rhs,
                                             start=(dc == 0), stop=(dc == NCH - 1))
                        smm.inc(last)
                        snap[("wld", wbuf)] = smm.v
                        smm.wait(se, v=smm.v)
                        if it >= 2:
                            psw.wait(se, v=snap[("psw", it - 2)])
                            k.sem("chain_ve").wait(se, v=snap[("vroU2", it - 2)])
                        sev.inc(se.copy(qe[:, buf * 512:(buf + 1) * 512],
                                        k.bank(bank)))
                        snap[("sev", it)] = sev.v
                        sev.wait(te, v=snap[("sev", it)])
                        if it >= 2:
                            k.sem("chain_ve").wait(te, v=snap[("vroU2", it - 2)])
                        psw.inc(te.matmul(k.bank(2 + bank), psw_sb[:],
                                          qe[:, buf * 512:(buf + 1) * 512],
                                          start=True, stop=True))
                        snap[("psw", it)] = psw.v
                        csl = slice(tb * 512, (tb + 1) * 512)
                        psw.wait(ve, v=snap[("psw", it)])
                        k.chain("ve", ve.tensor_tensor(
                            u1, qe[:, buf * 512:(buf + 1) * 512],
                            cos_sb[:, csl], ALU.mult))
                        k.chain("ve", ve.tensor_tensor(u2, k.bank(2 + bank),
                                                       sin_sb[:, csl], ALU.mult))
                        chv = k.sem("chain_ve")
                        snap[("vroU2", it)] = chv.v
                        if it >= 2:
                            sst.wait(ve, v=snap[("sstq", it - 2)])
                        k.chain("ve", ve.tensor_tensor(
                            rq[:, buf * 512:(buf + 1) * 512], u1, u2, ALU.add))
                        snap[("vro", it)] = chv.v
                        if it >= 1:
                            pnm.wait(ve, v=snap[("pnm", it - 1)])
                        vsq.inc(ve.tensor_tensor(
                            sq2[:, buf * 512:(buf + 1) * 512],
                            rq[:, buf * 512:(buf + 1) * 512],
                            rq[:, buf * 512:(buf + 1) * 512], ALU.mult))
                        vsq.wait(te, v=vsq.v)
                        if it >= 2:
                            vmx.wait(te, v=snap[("vmx", it - 2)])
                        pnm.inc(te.matmul(k.bank(4 + bank, rows=2), ho_sb[:],
                                          sq2[:, buf * 512:(buf + 1) * 512],
                                          start=True, stop=True))
                        snap[("pnm", it)] = pnm.v
                        pnm.wait(ve, v=snap[("pnm", it)])
                        slot = nslot0 + (fb * 4 + fc) * ntb + tb
                        vmx.inc(ve.reduce_max(mxr[:, slot:slot + 1],
                                              k.bank(4 + bank, rows=2), axis=AXX))
                        snap[("vmx", it)] = vmx.v
                        k.sem("chain_ve").wait(sy, v=snap[("vro", it)])
                        sst.inc(sy.dma_start(
                            out=dst_ap[fb * 512 + fc * 128:
                                       fb * 512 + (fc + 1) * 128,
                                       tb * 512:(tb + 1) * 512],
                            in_=rq[:, buf * 512:(buf + 1) * 512]), 16)
                        snap[("sstq", it)] = sst.v
                        it += 1
        vmx.wait(sy)
        sst.inc(sy.dma_start(out=bnd_ap, in_=mxr), 16)

        wbuf = 0
        smm.wait(ge, v=smm.v)
        src = wv_ap.rearrange("(dc p) f -> p dc f", p=128)
        dst3 = wb[:, 0:NCH * 512].rearrange("p (dc f) -> p dc f", dc=NCH)
        wl.inc(ge.dma_start(out=dst3, in_=src), 16)
        st_v = {}
        for tt in range(NTL):
            bank = tt % 2
            buf = tt % 2
            wl.wait(te)
            if tt >= 2:
                sev.wait(te, v=snap[("sevv", tt - 2)])
            last = None
            for dc in range(NCH):
                lhsT = xn_sb[:, dc * NT + tt * 128: dc * NT + (tt + 1) * 128]
                rhs = wb[:, dc * 512:(dc + 1) * 512]
                last = te.matmul(k.bank(bank), lhsT, rhs,
                                 start=(dc == 0), stop=(dc == NCH - 1))
            smm.inc(last)
            smm.wait(se, v=smm.v)
            if tt >= 2:
                sst.wait(se, v=st_v[tt - 2])
            sev.inc(se.copy(qe[:, buf * 512:(buf + 1) * 512], k.bank(bank)))
            snap[("sevv", tt)] = sev.v
            sev.wait(sy, v=snap[("sevv", tt)])
            for g in range(8):
                sst.inc(sy.dma_start(
                    out=v1_ap[g * 128:(g + 1) * 128, tt * 64:(tt + 1) * 64],
                    in_=qe[:, buf * 512 + g * 64: buf * 512 + (g + 1) * 64]), 16)
            st_v[tt] = sst.v
        sst.wait(sy)


def emit_att1(k: KB, tag, q1_ap, k1_ap, v1_ap, bnd_ap, cm_ap, a1_ap):
    nc, te, ve, se, sy, ge = k.nc, k.te, k.ve, k.se, k.sy, k.ge
    ld = k.sem(f"{tag}_ld")
    lq = k.sem(f"{tag}_lq")
    pmm = k.sem(f"{tag}_pmm")
    aex = k.sem(f"{tag}_aex")
    vmk = k.sem(f"{tag}_vmk")
    pav = k.sem(f"{tag}_pav")
    vs3 = k.sem(f"{tag}_vs3")
    pb3 = k.sem(f"{tag}_pb3")
    aev = k.sem(f"{tag}_aev")
    vout = k.sem(f"{tag}_vout")
    vms = k.sem(f"{tag}_vms")
    sst = k.sem(f"{tag}_sst")
    vpr = k.sem(f"{tag}_vpr")
    snap = {}

    def put(n, i, sem):
        snap[(n, i)] = sem.v

    def at(n, i):
        return snap[(n, i)]

    from contextlib import ExitStack as _ES
    _les = _ES()

    def _sb(name, shape, dt):
        return _les.enter_context(nc.sbuf_tensor(name, shape, dt))

    E1 = _sb(f"{tag}_E1", [128, 2 * 16384], BF16)
    MB = _sb(f"{tag}_MB", [128, 2 * 4096], BF16)
    Kp = _sb(f"{tag}_K", [128, NT], BF16)
    V65 = _sb(f"{tag}_V", [128, 2 * 32 * 65], BF16)
    qt = _sb(f"{tag}_q", [128, 2 * 512], BF16)
    CM = _sb(f"{tag}_cm", [128, 4 * 8 * 512], BF16)
    Rs = _sb(f"{tag}_Rs", [128, 512], BF16)
    ob = _sb(f"{tag}_ob", [128, 512], BF16)
    s3r = _sb(f"{tag}_s3", [1, 512], F32)[:]
    r3r = _sb(f"{tag}_r3", [1, 512], F32)[:]
    b3r = _sb(f"{tag}_b3", [1, 512], BF16)[:]
    pr = _sb(f"{tag}_pr", [1, 1280], F32)[:]
    prb = _sb(f"{tag}_prb", [1, 64], BF16)[:]
    biasB = _sb(f"{tag}_bias", [128, 32], F32)[:]

    bnd = pr[:, 0:1024]
    lq.inc(sy.dma_start(out=pr[:, 0:512], in_=bnd_ap[0:1, :]), 16)
    lq.inc(sy.dma_start(out=pr[:, 512:1024], in_=bnd_ap[1:2, :]), 16)
    lq.inc(ge.dma_start(out=CM[:], in_=cm_ap), 16)
    lq.wait(ve)
    bq = pr[:, 1024 + 0:1024 + 32]
    bk = pr[:, 1024 + 32:1024 + 40]
    scr40 = pr[:, 1064:1104]
    bk4 = pr[:, 1104:1136]
    prod = pr[:, 1136:1168]
    for h in range(32):
        col = (h % 2) * 512 + (h // 2) * 4
        k.chain("ve", ve.reduce_max(bq[:, h:h + 1], pr[:, col:col + 4], axis=AXX))
    for g in range(8):
        col = (g % 2) * 512 + 64 + (g // 2) * 8
        k.chain("ve", ve.reduce_max(bk[:, g:g + 1], pr[:, col:col + 8], axis=AXX))
    vpr.inc(ve.tensor_copy(scr40, pr[:, 1024:1064]))
    vpr.wait(se)
    apr = k.sem(f"{tag}_apr")
    apr.inc(se.activation(pr[:, 1024:1064], scr40, AF.Sqrt))
    apr.wait(ve)
    bk4v = bk4.rearrange("p (a b) -> p a b", b=4)
    for j in range(4):
        k.chain("ve", ve.tensor_copy(bk4v[:, :, j:j + 1],
                                     bk.rearrange("p (a b) -> p a b", b=1)))
    k.chain("ve", ve.tensor_tensor(prod, bq, bk4, ALU.mult))
    vpr.inc(ve.tensor_scalar(prb[:, 0:32], prod, -1.0 / 8.0, None, op0=ALU.mult))
    vpr.wait(te)
    ppr = k.sem(f"{tag}_ppr")
    ppr.inc(te.matmul(k.bank(0)[:, 0:32], k.ones_row[:], prb[:, 0:32],
                      start=True, stop=True))
    ppr.wait(ve)
    vpr.inc(ve.tensor_copy(biasB, k.bank(0)[:, 0:32]))
    vpr.wait(te)
    vpr.wait(se)
    vpr.wait(sy)
    vpr.wait(ge)
    if ATT1_STAGE == 0:
        _les.close()
        return

    def emit_post(j):
        jp, sb, hq8 = j // 32, (j // 8) % 4, j % 8
        gi = hq8 // 4
        hq = jp * 8 + hq8
        P = j % 2
        eP = 16384 * P
        mP = 4096 * P
        nt = 8 * sb + 8
        aex.wait(te, v=at("aexe", j))
        vmk.wait(te, v=at("vmk", j))
        if j >= 1:
            vout.wait(te, v=at("vout", j - 1))
        ld.wait(te, v=at("ldv", j // 32))
        last = None
        for jj in range(nt):
            own = jj < nt // 2
            c65 = jj if own else 16 + (jj - nt // 2)
            near = (jj >= nt // 2 - 4 and own) or (jj >= nt - 4)
            if near:
                m = (jj - (nt // 2 - 4)) if own else (4 + jj - (nt - 4))
                rhs = MB[:, mP + m * 512: mP + (m + 1) * 512]
            else:
                rhs = E1[:, eP + jj * 512: eP + (jj + 1) * 512]
            last = te.matmul(k.bank(6)[0:65, :],
                             V65[:, gi * 2080 + c65 * 65: gi * 2080 + (c65 + 1) * 65],
                             rhs, start=(jj == 0), stop=(jj == nt - 1))
        pav.inc(last)
        put("pav", j, pav)
        pav.wait(ve, v=at("pav", j))
        if j >= 1:
            pb3.wait(ve, v=at("pb3", j - 1))
        k.chain("ve", ve.tensor_copy(s3r, k.ps[64:65, 512 * 6:512 * 7]))
        k.chain("ve", ve.reciprocal(r3r, s3r))
        vs3.inc(ve.tensor_copy(b3r, r3r))
        put("vs3", j, vs3)
        vs3.wait(te, v=at("vs3", j))
        if j >= 1:
            aev.wait(te, v=at("aevR3", j - 1))
        pb3.inc(te.matmul(k.bank(7, rows=64), k.ones_row[0:1, 0:64], b3r,
                          start=True, stop=True))
        put("pb3", j, pb3)
        pb3.wait(se, v=at("pb3", j))
        aev.inc(se.copy(Rs[0:64, :], k.bank(7, rows=64)))
        put("aevR3", j, aev)
        aev.wait(ve, v=at("aevR3", j))
        if j >= 1:
            sst.wait(ve, v=at("sst", j - 1))
        vout.inc(ve.tensor_tensor(ob[0:64, :], k.bank(6)[0:64, :], Rs[0:64, :],
                                  ALU.mult))
        put("vout", j, vout)
        vout.wait(sy, v=at("vout", j))
        sst.inc(sy.dma_start(
            out=a1_ap[hq * 64:(hq + 1) * 64, sb * 512:(sb + 1) * 512],
            in_=ob[0:64, :]), 16)
        put("sst", j, sst)

    NIT = min(128, ATT1_IT_LIMIT) if ATT1_IT_LIMIT else 128
    for i in range(NIT):
        jp, sb, hq8 = i // 32, (i // 8) % 4, i % 8
        gi = hq8 // 4
        hq = jp * 8 + hq8
        P = i % 2
        eP = 16384 * P
        mP = 4096 * P
        nt = 8 * sb + 8
        if sb == 0 and hq8 == 0:
            if jp > 0:
                pmm.wait(sy, v=at("pmm", i - 1))
            ld.inc(sy.dma_start(out=Kp[:],
                                in_=k1_ap[jp * 128:(jp + 1) * 128, :]), 16)
        if i >= 2:
            pmm.wait(sy, v=at("pmm", i - 1))
        lq.inc(sy.dma_start(
            out=qt[gi * 64:(gi + 1) * 64, P * 512:(P + 1) * 512],
            in_=q1_ap[hq * 64:(hq + 1) * 64, sb * 512:(sb + 1) * 512]), 16)

        ld.wait(te)
        lq.wait(te)
        nun = nt // 2
        gu0 = snap.get(("gun", i - 1), 0)
        for u in range(nun):
            G = gu0 + u
            reg = G % 3
            wv_ = G - 2
            if wv_ > 0:
                aex.wait(te, v=wv_)
            last = None
            for tj in range(2):
                jj = u * 2 + tj
                own = jj < nt // 2
                col = jj * 128 if own else 2048 + (jj - nt // 2) * 128
                last = te.matmul(
                    k.bank(2 * reg + tj),
                    Kp[gi * 64:(gi + 1) * 64, col:col + 128],
                    qt[gi * 64:(gi + 1) * 64, P * 512:(P + 1) * 512],
                    start=True, stop=True)
            pmm.inc(last)
        put("pmm", i, pmm)
        snap[("gun", i)] = gu0 + nun

        for u in range(nun):
            G = gu0 + u
            reg = G % 3
            pmm.wait(se, v=at("pmm", i) - nun + u + 1)
            if i >= 2 and u == 0:
                if ("pav", i - 2) in snap:
                    pav.wait(se, v=at("pav", i - 2))
                vmk.wait(se, v=at("vmk", i - 2))
            aex.inc(se.activation(
                E1[:, eP + u * 1024: eP + (u + 1) * 1024],
                k.bank(2 * reg, nb=2), AF.Exp,
                bias=biasB[:, hq:hq + 1], scale=float(1.0 / 8.0)))
        put("aexe", i, aex)

        if i >= 1 and ATT1_STAGE >= 2:
            emit_post(i - 1)
        if sb == 0 and hq8 == 0:
            if jp > 0 and ("pav", i - 1) in snap:
                pav.wait(ve, v=at("pav", i - 1))
            vms.inc(ve.memset(V65[:], 1.0))
            vms.wait(sy)
            for gi2 in range(2):
                g = jp * 2 + gi2
                dst = V65[:, gi2 * 2080:(gi2 + 1) * 2080].rearrange(
                    "p (c d) -> p c d", d=65)[:, :, 0:64]
                ld.inc(sy.dma_start(out=dst, in_=v1_ap[g * 128:(g + 1) * 128, :]
                                    .rearrange("p (c d) -> p c d", d=64)), 16)
            put("ldv", jp, ld)

        aex.wait(ve, v=at("aexe", i))
        if i >= 2 and ("pav", i - 2) in snap:
            pav.wait(ve, v=at("pav", i - 2))
        for m in range(8):
            own = m < 4
            jj = (nt // 2 - 4 + m) if own else (nt - 8 + m)
            last = ve.tensor_tensor(
                MB[:, mP + m * 512: mP + (m + 1) * 512],
                E1[:, eP + jj * 512: eP + (jj + 1) * 512],
                CM[:, (sb * 8 + m) * 512:(sb * 8 + m + 1) * 512], ALU.mult)
            if m < 7:
                k.chain("ve", last)
        vmk.inc(last)
        put("vmk", i, vmk)

    if ATT1_STAGE >= 2:
        emit_post(NIT - 1)
    sst.wait(sy)
    _les.close()


def emit_ffn1(k: KB, tag, xn_ap, rt_ap, wsg_ap, wgu_ap, wsd_ap, wegu_ap,
              wed_ap, id_ap, g1p_ap, w1fm_ap):
    nc, te, ve, se, sy, ge = k.nc, k.te, k.ve, k.se, k.sy, k.ge
    wl = k.sem(f"{tag}_wl")
    pl = k.sem(f"{tag}_pl")
    prt = k.sem(f"{tag}_prt")
    vrt = k.sem(f"{tag}_vrt")
    art = k.sem(f"{tag}_art")
    ptr = k.sem(f"{tag}_ptr")
    aev = k.sem(f"{tag}_aev")
    psg = k.sem(f"{tag}_psg")
    pgu = k.sem(f"{tag}_pgu")
    ahg = k.sem(f"{tag}_ahg")
    vh = k.sem(f"{tag}_vh")
    pdn = k.sem(f"{tag}_pdn")
    adn = k.sem(f"{tag}_adn")
    sst = k.sem(f"{tag}_sst")
    rdm = k.sem(f"{tag}_rdm")
    lfm = k.sem(f"{tag}_lfm")
    vsg = k.sem(f"{tag}_vsg")
    asg = k.sem(f"{tag}_asg")
    snap = {}
    from contextlib import ExitStack as _ES
    _les = _ES()

    def _sb(name, shape, dt):
        return _les.enter_context(nc.sbuf_tensor(name, shape, dt))

    xp = _sb(f"{tag}_xn", [128, NCH * 2048], BF16)
    hbuf = _sb(f"{tag}_h", [128, 45056], BF16)
    wg = _sb(f"{tag}_wg", [128, 2 * NCH * 128], BF16)
    wu = _sb(f"{tag}_wu", [128, 2 * NCH * 128], BF16)
    wd = _sb(f"{tag}_wd", [128, 2 * 5632], BF16)
    rt_sb = _sb(f"{tag}_rt", [128, NCH * 8], BF16)
    wsg_sb = _sb(f"{tag}_ws", [128, NCH], BF16)
    id_sb = _sb(f"{tag}_id", [128, 128], BF16)
    wfm = _sb(f"{tag}_wfm", [8, 128], BF16)
    sgB = _sb(f"{tag}_sgB", [128, 2048], BF16)
    ceB = sgB
    scr = _sb(f"{tag}_scr", [128, 2048], BF16)
    r1 = _sb(f"{tag}_r1", [128, 64], F32)[:]
    rb1 = _sb(f"{tag}_rb1", [128, 8], BF16)[:]
    srow = _sb(f"{tag}_srow", [1, 512], F32)[:]
    srb = _sb(f"{tag}_srb", [1, 512], BF16)[:]
    evf = _sb(f"{tag}_evf", [128, 2 * 512], F32)

    for fc in range(NCH):
        pl.inc(sy.dma_start(out=xp[:, fc * 2048:(fc + 1) * 2048],
                            in_=xn_ap[fc * 128:(fc + 1) * 128, 0:2048]), 16)
    wl.inc(ge.dma_start(out=rt_sb[:].rearrange("p (c f) -> p c f", c=NCH),
                        in_=rt_ap.rearrange("(c p) f -> p c f", p=128)), 16)
    wl.inc(ge.dma_start(out=wsg_sb[:], in_=wsg_ap), 16)
    wl.inc(ge.dma_start(out=id_sb[:], in_=id_ap), 16)
    wl.wait(te)
    pl.wait(te)

    for tt in range(16):
        if tt > 0:
            vrt.wait(te, v=snap[("vrt", tt - 1)])
            art.wait(te, v=snap[("art", tt - 1)])
        last = None
        for fc in range(NCH):
            last = te.matmul(k.bank(0)[:, 0:8],
                             xp[:, fc * 2048 + tt * 128:
                                fc * 2048 + (tt + 1) * 128],
                             rt_sb[:, fc * 8:(fc + 1) * 8],
                             start=(fc == 0), stop=(fc == NCH - 1))
        prt.inc(last)
        prt.wait(ve)
        e_t = r1[:, 0:8]
        probs = r1[:, 8:16]
        eq1 = r1[:, 16:24]
        pm = r1[:, 24:32]
        eq2 = r1[:, 32:40]
        eq1s = r1[:, 56:64]
        mx = r1[:, 48:49]
        mxn = r1[:, 49:50]
        sm = r1[:, 50:51]
        rs = r1[:, 51:52]
        m1 = r1[:, 52:53]
        m2 = r1[:, 53:54]
        k.chain("ve", ve.reduce_max(mx, k.bank(0)[:, 0:8], axis=AXX))
        vrt.inc(ve.tensor_scalar(mxn, mx, -1.0, None, op0=ALU.mult))
        vrt.wait(se)
        art.inc(se.activation(e_t, k.bank(0)[:, 0:8], AF.Exp, bias=mxn,
                              accum_out=sm))
        snap[("art", tt)] = art.v
        art.wait(ve)
        k.chain("ve", ve.reciprocal(rs, sm))
        k.chain("ve", ve.tensor_scalar(probs, e_t, rs, None, op0=ALU.mult))
        k.chain("ve", ve.reduce_max(m1, probs, axis=AXX))
        k.chain("ve", ve.tensor_scalar(eq1, probs, m1, None, op0=ALU.is_equal))
        k.chain("ve", ve.tensor_scalar(eq1s, eq1, -2.0, None, op0=ALU.mult))
        k.chain("ve", ve.tensor_tensor(pm, probs, eq1s, ALU.add))
        k.chain("ve", ve.reduce_max(m2, pm, axis=AXX))
        k.chain("ve", ve.tensor_scalar(eq2, probs, m2, None, op0=ALU.is_equal))
        k.chain("ve", ve.tensor_tensor(pm, eq1, eq2, ALU.add))
        vrt.inc(ve.tensor_tensor(rb1, probs, pm, ALU.mult))
        snap[("vrt", tt)] = vrt.v
        vrt.wait(te)
        if tt > 0:
            aev.wait(te, v=snap[("aevt", tt - 1)])
        ptr.inc(te.matmul(k.bank(1, rows=8)[:, 0:128], rb1, id_sb[:],
                          start=True, stop=True))
        ptr.wait(se)
        if tt > 0:
            sst.wait(se, v=snap[("wfmst", tt - 1)])
        aev.inc(se.copy(wfm[:], k.bank(1, rows=8)[:, 0:128]))
        snap[("aevt", tt)] = aev.v
        aev.wait(sy)
        sst.inc(sy.dma_start(out=w1fm_ap[:, tt * 128:(tt + 1) * 128],
                             in_=wfm[:]), 16)
        snap[("wfmst", tt)] = sst.v

    pbj = k.sem(f"{tag}_pbj")
    for j in range(4):
        if j > 0:
            vsg.wait(te, v=snap[("vsgj", j - 1)])
        last = None
        for fc in range(NCH):
            last = te.matmul(k.bank(2, rows=1), wsg_sb[:, fc:fc + 1],
                             xp[:, fc * 2048 + j * 512:fc * 2048 + j * 512 + 512],
                             start=(fc == 0), stop=(fc == NCH - 1))
        psg.inc(last)
        psg.wait(ve)
        if j > 0:
            asg.wait(ve, v=snap[("asgj", j - 1)])
        vsg.inc(ve.tensor_copy(srow, k.bank(2, rows=1)))
        snap[("vsgj", j)] = vsg.v
        vsg.wait(se)
        if j > 0:
            pbj.wait(se, v=snap[("pbjj", j - 1)])
        asg.inc(se.activation(srb, srow, AF.Sigmoid))
        snap[("asgj", j)] = asg.v
        asg.wait(te)
        if j > 0:
            aev.wait(te, v=snap[("aevj", j - 1)])
        pbj.inc(te.matmul(k.bank(3), k.ones_row[:], srb,
                          start=True, stop=True))
        snap[("pbjj", j)] = pbj.v
        pbj.wait(se)
        aev.inc(se.copy(sgB[:, j * 512:(j + 1) * 512], k.bank(3)))
        snap[("aevj", j)] = aev.v
    aev.wait(ve)
    aev.wait(te)

    state = {"git": 0, "dit": 0}

    def mlp_half(gu_ap, up_off, nip, dn_ap, scale_sb, r, accum, ekey):
        git0 = state["git"]
        tsl = slice(r * 1024, (r + 1) * 1024)
        for ip in range(nip):
            git = state["git"]
            wbuf = git % 2
            if git >= 2:
                pgu.wait(ge, v=snap[("pgu", git - 2)])
            for wdst, col0 in ((wg, ip * 128), (wu, up_off + ip * 128)):
                src = gu_ap[:, col0:col0 + 128]
                wl.inc(ge.dma_start(
                    out=wdst[:, wbuf * NCH * 128:(wbuf + 1) * NCH * 128]
                    .rearrange("p (c f) -> p c f", c=NCH),
                    in_=src.rearrange("(c p) f -> p c f", p=128)), 16)
            wseen = wl.v
            wl.wait(te, v=wseen)
            if git >= 1:
                vh.wait(te, v=snap[("vh", git - 1)])
                ahg.wait(te, v=snap[("ahg", git - 1)])
            if ip == 0 and ("hwar", ekey) in snap:
                pdn.wait(ve, v=snap[("hwar", ekey)])
            for bank0, wsrc in ((0, wg), (2, wu)):
                last = None
                for th in range(2):
                    for fc in range(NCH):
                        last = te.matmul(
                            k.bank(bank0 + th),
                            wsrc[:, wbuf * NCH * 128 + fc * 128:
                                 wbuf * NCH * 128 + (fc + 1) * 128],
                            xp[:, fc * 2048 + r * 1024 + th * 512:
                               fc * 2048 + r * 1024 + (th + 1) * 512],
                            start=(fc == 0), stop=(fc == NCH - 1))
                pgu.inc(last)
            snap[("pgu", git)] = pgu.v
            pgu.wait(se, v=snap[("pgu", git)])
            hsl = slice(ip * 1024, (ip + 1) * 1024)
            ahg.inc(se.activation(scr[:, 0:1024], k.bank(0, nb=2), AF.Silu))
            snap[("ahg", git)] = ahg.v
            ahg.wait(ve)
            k.chain("ve", ve.tensor_tensor(scr[:, 1024:2048], scr[:, 0:1024],
                                           k.bank(2, nb=2), ALU.mult))
            vh.inc(ve.tensor_tensor(hbuf[:, hsl], scr[:, 1024:2048],
                                    scale_sb[:, tsl], ALU.mult))
            snap[("vh", git)] = vh.v
            state["git"] = git + 1
        vh.wait(te)

        def wd_load(oc):
            dwb = oc % 2
            if ("wdw", dwb) in snap:
                pdn.wait(ge, v=snap[("wdw", dwb)])
            wl.inc(ge.dma_start(
                out=wd[:, dwb * 5632: dwb * 5632 + nip * 128]
                .rearrange("p (c f) -> p c f", c=nip),
                in_=dn_ap[:, oc * 128:(oc + 1) * 128]
                .rearrange("(c p) f -> p c f", p=128)), 16)
            snap[("wds", oc)] = wl.v

        wd_load(0)
        for oc in range(NCH):
            dwb = oc % 2
            if oc + 1 < NCH:
                wd_load(oc + 1)
            wl.wait(te, v=snap[("wds", oc)])
            for th in range(2):
                dit = state["dit"]
                bank = 4 + dit % 2
                if dit >= 2:
                    adn.wait(te, v=snap[("adn", dit - 2)])
                last = None
                for ip in range(nip):
                    last = te.matmul(
                        k.bank(bank),
                        wd[:, dwb * 5632 + ip * 128:dwb * 5632 + (ip + 1) * 128],
                        hbuf[:, ip * 1024 + th * 512: ip * 1024 + th * 512 + 512],
                        start=(ip == 0), stop=(ip == nip - 1))
                pdn.inc(last)
                snap[("wdw", dwb)] = pdn.v
                pdn.wait(se, v=pdn.v)
                if dit >= 2:
                    sst.wait(se, v=snap[("dst", dit - 2)])
                    rdm.wait(se, v=snap[("drm", dit - 2)])
                adn.inc(se.copy(evf[:, (dit % 2) * 512:(dit % 2 + 1) * 512],
                                k.bank(bank)))
                snap[("adn", dit)] = adn.v
                reg = (oc, r * 2 + th)
                dst = g1p_ap[oc * 128:(oc + 1) * 128,
                             r * 1024 + th * 512: r * 1024 + (th + 1) * 512]
                srcv = evf[:, (dit % 2) * 512:(dit % 2 + 1) * 512]
                if accum:
                    adn.wait(ge, v=snap[("adn", dit)])
                    if ("g1w", reg) in snap:
                        rdm.wait(ge, v=snap[("g1w", reg)])
                    rdm.inc(ge.dma_start(out=dst, in_=srcv, accum_op=ALU.add), 16)
                    snap[("g1w", reg)] = rdm.v
                    snap[("drm", dit)] = rdm.v
                    snap[("dst", dit)] = sst.v
                else:
                    adn.wait(sy, v=snap[("adn", dit)])
                    sst.inc(sy.dma_start(out=dst, in_=srcv), 16)
                    snap[("dst", dit)] = sst.v
                    snap[("drm", dit)] = rdm.v
                state["dit"] = dit + 1
        snap[("hwar", ekey + 1)] = pdn.v

    snap[("hwar", 0)] = 0
    ek = 0
    for r in range(2):
        mlp_half(wgu_ap, 5632, 44, wsd_ap, sgB, r, False, ek)
        ek += 1
    sst.wait(ge)
    for e in range(8):
        vh.wait(te)
        aev.wait(te)
        for j in range(4):
            pbj.wait(sy)
            lfm.inc(sy.dma_start(out=srb, in_=w1fm_ap[e:e + 1,
                                                      j * 512:(j + 1) * 512]), 16)
            lfm.wait(te)
            pbj.inc(te.matmul(k.bank(3), k.ones_row[:], srb,
                              start=True, stop=True))
            pbj.wait(se)
            aev.inc(se.copy(ceB[:, j * 512:(j + 1) * 512], k.bank(3)))
            aev.wait(te)
        aev.wait(ve)
        for r in range(2):
            mlp_half(wegu_ap[e * 2048:(e + 1) * 2048, :], 1408, 11,
                     wed_ap[e * 1408:(e + 1) * 1408, :], ceB, r, True, ek)
            ek += 1
    rdm.wait(ge)
    sst.wait(sy)
    _les.close()


def _host_perm(parity):
    own = np.arange(parity, NTL, 2)
    other = np.arange(1 - parity, NTL, 2)
    tiles = np.concatenate([own, other])
    return (tiles[:, None] * 128 + np.arange(128)[None, :]).reshape(-1)


def build_program(dbg=(), phases=None):
    k = KB(dbg=dbg, phases=phases)
    k.constf(EPS_LN, rows=1)
    k.constf(EPS_RMS, rows=1)
    k.wait_init()
    x0 = k.din("x0", [D, NT], F32)
    if k.want("ln0"):
        xn = k.dram("xn0", [D, NT], BF16)
        emit_ln(k, "ln0", x0, xn)
        k.barrier()
    if k.want("qkv0"):
        xn = k.dram("xn0", [D, NT], BF16)
        wq = k.din("wq0", [D, 4096], F32)
        wk = k.din("wk0", [D, 4096], F32)
        wv = k.din("wv0", [D, D], F32)
        q0 = k.dram("q0", [4096, NT], BF16)
        k0 = k.dram("k0", [4096, NT], BF16)
        v0 = k.dram("v0", [D, NT], BF16)
        bnd = k.dram("bnd0", [1, 512], F32)
        emit_qkv0(k, "qkv0", xn, wq, wk, wv, q0, k0, v0, bnd)
        k.barrier()
    if k.want("att0"):
        q0 = k.dram("q0", [4096, NT], BF16)
        k0 = k.dram("k0", [4096, NT], BF16)
        v0 = k.dram("v0", [D, NT], BF16)
        bnd = k.dram("bnd0", [1, 512], F32)
        lam = k.din("lam0", [1, 16], F32)
        s0t = k.dram("s0t", [D, NT], BF16)
        factor = k.dram("factor0", [1, NT], F32)
        rms0 = k.dram("rms0", [1, NT], F32)
        emit_att0(k, "att0", q0, k0, v0, bnd, lam, s0t, factor, rms0)
        k.barrier()
    if k.want("wo0"):
        s0t = k.dram("s0t", [D, NT], BF16)
        factor = k.dram("factor0", [1, NT], F32)
        wo = k.din("wo0w", [D, D], F32)
        ns = k.din("ns0", [128, 16], F32)
        s0 = k.dram("s0", [D, NT], BF16)
        emit_wo0(k, "wo0", s0t, factor, wo, ns, s0)
        k.barrier()
    if k.want("ffn0"):
        xn = k.dram("xn0", [D, NT], BF16)
        wT = k.din("w0ffnT", [D, D], F32)
        aL = k.din("a0l", [D, 128], F32)
        bL = k.din("b0l", [128, D], F32)
        rt = k.din("rt0", [D, 8], F32)
        rep = k.din("rep", [8, 128], F32)
        ident = k.din("ident", [128, 128], F32)
        g0 = k.dram("g0", [D, NT], BF16)
        emit_ffn0(k, "ffn0", xn, wT, aL, bL, rt, rep, ident, g0)
        k.barrier()
    if k.want("cmb0"):
        x0_ = k.din("x0", [D, NT], F32)
        s0 = k.dram("s0", [D, NT], BF16)
        g0 = k.dram("g0", [D, NT], BF16)
        x1 = k.dram("x1", [D, NT], F32)
        emit_combine(k, "cmb0", x0_, s0, g0, x1)
        k.barrier()
    if k.want("ln1"):
        x1 = k.dram("x1", [D, NT], F32)
        xn1 = k.dram("xn1", [D, NT], BF16)
        emit_ln(k, "ln1", x1, xn1)
        k.barrier()
    if k.want("qkv1"):
        xn1 = k.dram("xn1", [D, NT], BF16)
        wq1 = k.din("wq1T", [D, D], F32)
        wk1 = k.din("wk1T", [D, 512], F32)
        wv1 = k.din("wv1T", [D, 512], F32)
        cosT = k.din("cos1", [128, NT], F32)
        sinT = k.din("sin1", [128, NT], F32)
        pswi = k.din("psw", [128, 128], F32)
        q1 = k.dram("q1", [D, NO], BF16)
        k1 = k.dram("k1", [512, NT], BF16)
        v1 = k.dram("v1", [1024, 2048], BF16)
        bnd1 = k.dram("bnd1", [2, 512], F32)
        emit_qkv1(k, "qkv1", xn1, wq1, wk1, wv1, cosT, sinT, pswi,
                  q1, k1, v1, bnd1)
        k.barrier()
    if k.want("att1"):
        q1 = k.dram("q1", [D, NO], BF16)
        k1 = k.dram("k1", [512, NT], BF16)
        v1 = k.dram("v1", [1024, 2048], BF16)
        bnd1 = k.dram("bnd1", [2, 512], F32)
        cm = k.din("cmask", [128, 4 * 8 * 512], F32)
        a1 = k.dram("a1", [D, NO], BF16)
        emit_att1(k, "att1", q1, k1, v1, bnd1, cm, a1)
        k.barrier()
    if k.want("wo1"):
        a1 = k.dram("a1", [D, NO], BF16)
        wo1 = k.din("wo1T", [D, D], F32)
        s1 = k.dram("s1", [D, NO], BF16)
        emit_wo0(k, "wo1", a1, None, wo1, None, s1, n_tok=NO,
                 use_ns=False, use_factor=False)
        k.barrier()
    if k.want("ffn1"):
        xn1 = k.dram("xn1", [D, NT], BF16)
        rt1 = k.din("rt1T", [D, 8], F32)
        wsg1 = k.din("wsg1T", [128, NCH], F32)
        wgu1 = k.din("wgu1", [D, 11264], F32)
        wsd1 = k.din("wsd1", [5632, D], F32)
        wegu1 = k.din("wegu1", [8 * D, 2816], F32)
        wed1 = k.din("wed1", [8 * 1408, D], F32)
        ident = k.din("ident", [128, 128], F32)
        g1p = k.dram("g1p", [D, NO], F32)
        w1fm = k.dram("w1fm", [8, 2048], BF16)
        emit_ffn1(k, "ffn1", xn1, rt1, wsg1, wgu1, wsd1, wegu1, wed1, ident,
                  g1p, w1fm)
        k.barrier()
    if k.want("cmb1"):
        x1 = k.dram("x1", [D, NT], F32)
        s1 = k.dram("s1", [D, NO], BF16)
        g1p = k.dram("g1p", [D, NO], F32)
        x2 = k.dout("out", [D, NO], F32)
        emit_combine(k, "cmb1", x1[:, 0:NO], s1, g1p, x2, n_tok=NO,
                     g_silu=True)
        k.barrier()
    return k


def _stage_weights(inputs):
    f32 = np.float32
    A = np.asarray(inputs["L0_ffn_lora_A"], f32)
    B = np.asarray(inputs["L0_ffn_lora_B"], f32)
    rep = np.zeros((8, 128), f32)
    for e in range(8):
        rep[e, e * 16:(e + 1) * 16] = 1.0
    psw = np.zeros((128, 128), f32)
    psw[np.arange(128), np.arange(128) ^ 1] = 1.0
    return {
        "wq0": np.asarray(inputs["L0_attn_Wq"], f32),
        "wk0": np.asarray(inputs["L0_attn_Wk"], f32),
        "wv0": np.asarray(inputs["L0_attn_Wv"], f32),
        "wo0w": np.asarray(inputs["L0_attn_Wo"], f32),
        "ns0": np.ascontiguousarray(
            np.asarray(inputs["L0_attn_norm_scale"], f32).reshape(16, 128).T),
        "lam0": np.asarray(inputs["L0_attn_lam"], f32).reshape(1, 16),
        "w0ffnT": np.ascontiguousarray(np.asarray(inputs["L0_ffn_weight"], f32).T),
        "a0l": np.ascontiguousarray(np.transpose(A, (2, 0, 1)).reshape(2048, 128)),
        "b0l": np.ascontiguousarray(np.transpose(B, (0, 2, 1)).reshape(128, 2048)),
        "rt0": np.ascontiguousarray(np.asarray(inputs["L0_ffn_router_w"], f32).T),
        "rep": rep,
        "ident": np.eye(128, dtype=f32),
        "psw": psw,
        "wq1T": np.ascontiguousarray(np.asarray(inputs["L1_attn_wq"], f32).T),
        "wk1T": np.ascontiguousarray(np.asarray(inputs["L1_attn_wk"], f32).T),
        "wv1T": np.ascontiguousarray(np.asarray(inputs["L1_attn_wv"], f32).T),
        "wo1T": np.ascontiguousarray(np.asarray(inputs["L1_attn_wo"], f32).T),
        "rt1T": np.ascontiguousarray(np.asarray(inputs["L1_ffn_W_router"], f32).T),
        "wsg1T": np.ascontiguousarray(
            np.asarray(inputs["L1_ffn_W_sgate"], f32).reshape(16, 128).T),
        "wgu1": np.asarray(inputs["L1_ffn_Ws_gate_up"], f32),
        "wsd1": np.asarray(inputs["L1_ffn_Ws_down"], f32),
        "wegu1": np.asarray(inputs["L1_ffn_We_gate_up"], f32).reshape(8 * 2048, 2816),
        "wed1": np.asarray(inputs["L1_ffn_We_down"], f32).reshape(8 * 1408, 2048),
    }


def _stage_percore(inputs, core):
    f32 = np.float32
    p, parity = core // 2, core % 2
    perm = _host_perm(parity)
    x = np.asarray(inputs["L0_attn_x"][p], f32)
    st = {"x0": np.ascontiguousarray(x.T[:, perm])}
    pos = perm.astype(np.float64)
    r = np.arange(128)
    theta = 10000.0 ** (-(2.0 * ((r % 64) // 2)) / 64.0)
    ang = pos[None, :] * theta[:, None]
    st["cos1"] = np.cos(ang).astype(f32)
    sgn = np.where(r % 2 == 0, -1.0, 1.0)
    st["sin1"] = (sgn[:, None] * np.sin(ang)).astype(f32)
    own_t = np.arange(parity, 32, 2)
    oth_t = np.arange(1 - parity, 32, 2)
    cm = np.zeros((4, 8, 128, 512), f32)
    for sb in range(4):
        for m in range(8):
            own = m < 4
            j = 4 * sb + (m if own else m - 4)
            keyg = (own_t if own else oth_t)[j]
            kpos = keyg * 128 + np.arange(128)
            for c in range(4):
                qg = own_t[4 * sb + c]
                qpos = qg * 128 + np.arange(128)
                cm[sb, m, :, c * 128:(c + 1) * 128] = (
                    kpos[:, None] <= qpos[None, :]).astype(f32)
    st["cmask"] = np.ascontiguousarray(
        cm.reshape(32, 128, 512).transpose(1, 0, 2).reshape(128, 32 * 512))
    return st, perm


def kernel(**inputs):
    k = build_program()
    names = set(k.inputs)
    w = _stage_weights(inputs)
    in_maps = []
    perms = []
    for core in range(8):
        st, perm = _stage_percore(inputs, core)
        st.update(w)
        in_maps.append({n: st[n] for n in names})
        perms.append(perm)
    res = run_bass_kernel_spmd(k.nc, in_maps, core_ids=list(range(8)))
    k.close()
    out = np.zeros((4, NT, D), np.float32)
    for core in range(8):
        p = core // 2
        o = np.asarray(res.results[core]["out"], np.float32)
        out[p, perms[core][:NO], :] = o.T
    return out


# version 09
# speedup vs baseline: 1.1881x; 1.0417x over previous
import numpy as np
import ml_dtypes
from contextlib import ExitStack

import concourse.bass as bass
import concourse.mybir as mybir
from concourse.bass_utils import run_bass_kernel_spmd

F32 = mybir.dt.float32
BF16 = mybir.dt.bfloat16
AF = mybir.ActivationFunctionType
ALU = mybir.AluOpType
AXX = mybir.AxisListType.X

D = 2048
NT = 4096
NO = 2048
NCH = D // 128
NTL = NT // 128
EPS_LN = 1e-5
EPS_RMS = 1e-6

SAFE_CHAIN = True
ATT0_IT_LIMIT = None
ATT0_NO_POST = False
FFN0_STAGE = 3
ATT1_IT_LIMIT = None
ATT1_STAGE = 2


class PhysSem:
    def __init__(self, nc, es, name):
        self.h = es.enter_context(nc.semaphore(name))
        self.total = 0


class Sem:

    def __init__(self, phys):
        self.p = phys
        self.base = phys.total
        self.v = 0

    def inc(self, instr, n=1):
        instr.then_inc(self.p.h, n)
        self.p.total += n
        self.v += n
        return instr

    def wait(self, eng, v=None):
        eng.wait_ge(self.p.h, self.base + (self.v if v is None else v))


class KB:

    def __init__(self, dbg=(), phases=None):
        self.es = ExitStack()
        self.nc = bass.Bass("TRN2", target_bir_lowering=False)
        self.dbg = set(dbg)
        self.phases = phases
        nc = self.nc
        self.te, self.ve, self.se = nc.tensor, nc.vector, nc.scalar
        self.ge, self.sy = nc.gpsimd, nc.sync
        self.sems = {}
        self._phys = {}
        self.drams = {}
        self.inputs = set()
        self._chain_n = 0
        self.ps = self.es.enter_context(nc.psum_tensor("ps_arena", [128, 4096], F32))
        self._consts = {}
        self._const_pool = self.es.enter_context(
            nc.sbuf_tensor("const_pool", [128, 64], F32))
        self._const_n = 0
        init = self.sem("init")
        self.ones_col = self.es.enter_context(nc.sbuf_tensor("ones_col", [128, 1], BF16))
        self.ones_row = self.es.enter_context(nc.sbuf_tensor("ones_row", [1, 128], BF16))
        self.ve.memset(self.ones_col[:], 1.0)
        init.inc(self.ve.memset(self.ones_row[:], 1.0))

    def constf(self, val, rows=128):
        key = (float(val), rows)
        if key not in self._consts:
            assert not getattr(self, "_init_done", False), \
                "register consts before wait_init()"
            col = self._const_n
            self._const_n += 1
            ap = self._const_pool[0:rows, col:col + 1]
            init = self.sem("init")
            init.inc(self.ve.memset(self._const_pool[:, col:col + 1], float(val)))
            self._consts[key] = ap
        return self._consts[key]

    def wait_init(self):
        self._init_done = True
        init = self.sem("init")
        for eng in (self.te, self.ve, self.se, self.ge, self.sy):
            init.wait(eng)

    def sem(self, name):
        if name in self.sems:
            return self.sems[name]
        if name.startswith("chain") or name == "init":
            role = name
        else:
            role = name.split("_", 1)[-1] if "_" in name else name
        if role not in self._phys:
            self._phys[role] = PhysSem(self.nc, self.es, role)
        self.sems[name] = Sem(self._phys[role])
        return self.sems[name]

    def bank(self, b, nb=1, rows=128, row0=0):
        return self.ps[row0:row0 + rows, 512 * b:512 * b + 512 * nb]

    def chain(self, eng_name, instr):
        if SAFE_CHAIN:
            s = self.sem(f"chain_{eng_name}")
            s.inc(instr)
            eng = getattr(self, eng_name)
            s.wait(eng)
        return instr

    def din(self, name, shape, dtype=F32):
        if name not in self.drams:
            self.inputs.add(name)
            self.drams[name] = self.nc.dram_tensor(
                name, list(shape), dtype, kind="ExternalInput").ap()
        return self.drams[name]

    def dout(self, name, shape, dtype=F32):
        if name not in self.drams:
            self.drams[name] = self.nc.dram_tensor(
                name, list(shape), dtype, kind="ExternalOutput").ap()
        return self.drams[name]

    def dram(self, name, shape, dtype=BF16):
        if name not in self.drams:
            if name in self.dbg:
                return self.dout(name, shape, dtype)
            self.drams[name] = self.nc.dram_tensor(
                name, list(shape), dtype, kind="Internal").ap()
        return self.drams[name]

    def want(self, phase):
        return self.phases is None or phase in self.phases

    def barrier(self):
        self.nc.all_engine_barrier()

    def close(self):
        self.es.close()


def emit_ln(k: KB, tag, x_ap, xn_ap):
    nc, te, ve, se, sy, ge = k.nc, k.te, k.ve, k.se, k.sy, k.ge
    ld = k.sem(f"{tag}_ld")
    va = k.sem(f"{tag}_va")
    sa = k.sem(f"{tag}_sa")
    pa = k.sem(f"{tag}_pa")
    ev = k.sem(f"{tag}_ev")
    bc = k.sem(f"{tag}_bc")
    ld2 = k.sem(f"{tag}_ld2")
    vb = k.sem(f"{tag}_vb")
    st = k.sem(f"{tag}_st")
    with (
        nc.sbuf_tensor(f"{tag}_xc", [128, 2 * NT], F32) as xc,
        nc.sbuf_tensor(f"{tag}_xb", [128, 2 * NT], BF16) as xb,
        nc.sbuf_tensor(f"{tag}_xq", [128, 2 * NT], BF16) as xq,
        nc.sbuf_tensor(f"{tag}_s0", [1, NT], F32) as sums_h,
        nc.sbuf_tensor(f"{tag}_s1", [1, NT], F32) as sumsq_h,
        nc.sbuf_tensor(f"{tag}_s2", [1, NT], F32) as mrow_h,
        nc.sbuf_tensor(f"{tag}_s3", [1, NT], F32) as v4_h,
        nc.sbuf_tensor(f"{tag}_mB", [128, NT], F32) as mB,
        nc.sbuf_tensor(f"{tag}_rB", [128, NT], F32) as rB,
        nc.sbuf_tensor(f"{tag}_t1", [128, NT], F32) as t1,
        nc.sbuf_tensor(f"{tag}_xnb", [128, 2 * NT], BF16) as xnb,
    ):
        sums, sumsq, mrow, v4 = sums_h[:], sumsq_h[:], mrow_h[:], v4_h[:]

        for c in range(NCH):
            buf = c % 2
            sl = slice(buf * NT, (buf + 1) * NT)
            if c >= 2:
                va.wait(sy, v=c - 1)
                sa.wait(sy, v=c - 1)
            ld.inc(sy.dma_start(out=xc[:, sl], in_=x_ap[c * 128:(c + 1) * 128, :]), 16)
            ld.wait(ve)
            if c >= 2:
                pa.wait(ve, v=c - 1)
            va.inc(ve.tensor_copy(xb[:, sl], xc[:, sl]))
            ld.wait(se)
            if c >= 2:
                pa.wait(se, v=c - 1)
            sa.inc(se.activation(xq[:, sl], xc[:, sl], AF.Square))
            va.wait(te)
            sa.wait(te)
            last = None
            for j in range(8):
                last = te.matmul(k.bank(j, rows=1, row0=0), k.ones_col[:],
                                 xb[:, buf * NT + j * 512: buf * NT + (j + 1) * 512],
                                 start=(c == 0), stop=(c == NCH - 1))
            for j in range(8):
                last = te.matmul(k.bank(j, rows=1, row0=32), k.ones_col[:],
                                 xq[:, buf * NT + j * 512: buf * NT + (j + 1) * 512],
                                 start=(c == 0), stop=(c == NCH - 1))
            pa.inc(last)

        pa.wait(ve)
        k.chain("ve", ve.tensor_copy(sums, k.ps[0:1, 0:4096]))
        k.chain("ve", ve.tensor_copy(sumsq, k.ps[32:33, 0:4096]))
        k.chain("ve", ve.tensor_scalar(mrow, sums, 1.0 / D, None, op0=ALU.mult))
        ev.inc(ve.tensor_scalar(v4, sumsq, 1.0 / D, None, op0=ALU.mult))
        ev.wait(se)
        sq = k.sem(f"{tag}_sq")
        sq.inc(se.activation(sums, mrow, AF.Square))
        sq.wait(ve)
        ev.inc(ve.tensor_tensor(sumsq, v4, sums, ALU.subtract))
        ev.wait(se)
        sq.inc(se.activation(v4, sumsq, AF.Sqrt, bias=k.constf(EPS_LN, rows=1)))
        sq.wait(ve)
        k.chain("ve", ve.reciprocal(sums, v4))
        mrow_bf = xb[0:1, 0:NT]
        rrow_bf = xb[0:1, NT:2 * NT]
        k.chain("ve", ve.tensor_copy(mrow_bf, mrow))
        ev.inc(ve.tensor_copy(rrow_bf, sums))
        evb = k.sem(f"{tag}_evb")
        ev.wait(te)
        for r in range(2):
            src = mrow_bf if r == 0 else rrow_bf
            dst = mB if r == 0 else rB
            if r == 1:
                evb.wait(te, v=8)
            mmw = k.sem(f"{tag}_bb")
            for j in range(8):
                mmw.inc(te.matmul(k.bank(j), k.ones_row[:],
                                  src[:, j * 512:(j + 1) * 512],
                                  start=True, stop=True))
            mmw.wait(ve)
            for j in range(8):
                evb.inc(ve.tensor_copy(dst[:, j * 512:(j + 1) * 512], k.bank(j)))
        evb.wait(sy)
        evb.wait(ve)
        for c in range(NCH):
            buf = c % 2
            sl = slice(buf * NT, (buf + 1) * NT)
            if c >= 2:
                vb.wait(sy, v=c - 1)
            ld2.inc(sy.dma_start(out=xc[:, sl], in_=x_ap[c * 128:(c + 1) * 128, :]), 16)
            ld2.wait(ve)
            if c >= 2:
                st.wait(ve, v=16 * (c - 1))
            k.chain("ve", ve.tensor_tensor(t1[:], xc[:, sl], mB[:], ALU.subtract))
            vb.inc(ve.tensor_tensor(xnb[:, sl], t1[:], rB[:], ALU.mult))
            vb.wait(sy)
            st.inc(sy.dma_start(out=xn_ap[c * 128:(c + 1) * 128, :], in_=xnb[:, sl]), 16)
        st.wait(sy)


def emit_qkv0(k: KB, tag, xn_ap, wq_ap, wk_ap, wv_ap, q0_ap, k0_ap, v0_ap,
              bnd_ap):
    nc, te, ve, se, sy, ge = k.nc, k.te, k.ve, k.se, k.sy, k.ge
    xl = k.sem(f"{tag}_xl")
    wl = k.sem(f"{tag}_wl")
    smm = k.sem(f"{tag}_mm")
    sev = k.sem(f"{tag}_ev")
    ssq = k.sem(f"{tag}_sq")
    snm = k.sem(f"{tag}_nm")
    smx = k.sem(f"{tag}_mx")
    sst = k.sem(f"{tag}_st")
    with (
        nc.sbuf_tensor(f"{tag}_xn", [128, NCH * NT], BF16) as xn_sb,
        nc.sbuf_tensor(f"{tag}_wb", [128, 2 * NCH * 512], BF16) as wb,
        nc.sbuf_tensor(f"{tag}_evt", [128, 2 * 512], BF16) as evt,
        nc.sbuf_tensor(f"{tag}_sqt", [128, 2 * 512], BF16) as sqt,
        nc.sbuf_tensor(f"{tag}_mxr", [1, 512], F32) as mxr,
    ):
        for c in range(NCH):
            xl.inc(sy.dma_start(out=xn_sb[:, c * NT:(c + 1) * NT],
                                in_=xn_ap[c * 128:(c + 1) * 128, :]), 16)
        xl.wait(te)

        it = 0
        nqk_blocks = [(wq_ap, q0_ap, fb) for fb in range(8)] + \
                     [(wk_ap, k0_ap, fb) for fb in range(8)]
        for bi, (w_ap, dst_ap, fb) in enumerate(nqk_blocks):
            wbuf = bi % 2
            if bi >= 2:
                smm.wait(ge, v=(bi - 1) * 32)
            src = w_ap[:, fb * 512:(fb + 1) * 512]
            src = src.rearrange("(dc p) f -> p dc f", p=128)
            dst3 = wb[:, wbuf * NCH * 512:(wbuf + 1) * NCH * 512].rearrange(
                "p (dc f) -> p dc f", dc=NCH)
            wl.inc(ge.dma_start(out=dst3, in_=src), 16)
            for fc in range(4):
                for tb in range(8):
                    bank = it % 2
                    ebuf = it % 2
                    wl.wait(te, v=(bi + 1) * 16)
                    if it >= 2:
                        sev.wait(te, v=it - 1)
                    last = None
                    for dc in range(NCH):
                        lhsT = wb[:, wbuf * NCH * 512 + dc * 512 + fc * 128:
                                  wbuf * NCH * 512 + dc * 512 + fc * 128 + 128]
                        rhs = xn_sb[:, dc * NT + tb * 512: dc * NT + (tb + 1) * 512]
                        last = te.matmul(k.bank(bank), lhsT, rhs,
                                         start=(dc == 0), stop=(dc == NCH - 1))
                    smm.inc(last)
                    smm.wait(se, v=it + 1)
                    if it >= 2:
                        sst.wait(se, v=16 * (it - 1))
                        ssq.wait(se, v=it - 1)
                    sev.inc(se.copy(evt[:, ebuf * 512:(ebuf + 1) * 512], k.bank(bank)))
                    sev.wait(ve, v=it + 1)
                    if it >= 2:
                        snm.wait(ve, v=it - 1)
                    ssq.inc(ve.tensor_tensor(sqt[:, ebuf * 512:(ebuf + 1) * 512],
                                             evt[:, ebuf * 512:(ebuf + 1) * 512],
                                             evt[:, ebuf * 512:(ebuf + 1) * 512],
                                             ALU.mult))
                    ssq.wait(te, v=it + 1)
                    if it >= 2:
                        smx.wait(te, v=it - 1)
                    snm.inc(te.matmul(k.bank(2 + bank, rows=1), k.ones_col[:],
                                      sqt[:, ebuf * 512:(ebuf + 1) * 512],
                                      start=True, stop=True))
                    snm.wait(ve, v=it + 1)
                    chunk_id = bi * 4 + fc
                    smx.inc(ve.reduce_max(mxr[:, chunk_id * 8 + tb:
                                              chunk_id * 8 + tb + 1],
                                          k.bank(2 + bank, rows=1), axis=AXX))
                    sev.wait(sy, v=it + 1)
                    sst.inc(sy.dma_start(
                        out=dst_ap[fb * 512 + fc * 128: fb * 512 + (fc + 1) * 128,
                                   tb * 512:(tb + 1) * 512],
                        in_=evt[:, ebuf * 512:(ebuf + 1) * 512]), 16)
                    it += 1
        smx.wait(sy)
        sst.inc(sy.dma_start(out=bnd_ap, in_=mxr[:]), 16)

        st_snap = {}
        for vb in range(4):
            wbuf = vb % 2
            if vb >= 2:
                smm.wait(ge, v=len(nqk_blocks) * 32 + (vb - 1) * 32)
            else:
                smm.wait(ge, v=len(nqk_blocks) * 32)
            src = wv_ap[:, vb * 512:(vb + 1) * 512]
            src = src.rearrange("(dc p) f -> p dc f", p=128)
            dst3 = wb[:, wbuf * NCH * 512:(wbuf + 1) * NCH * 512].rearrange(
                "p (dc f) -> p dc f", dc=NCH)
            wl.inc(ge.dma_start(out=dst3, in_=src), 16)
            for tt in range(NTL):
                vit = vb * NTL + tt
                bank = it % 2
                ebuf = it % 2
                wl.wait(te, v=(len(nqk_blocks) + vb + 1) * 16)
                if it >= 2:
                    sev.wait(te, v=it - 1)
                last = None
                for dc in range(NCH):
                    lhsT = xn_sb[:, dc * NT + tt * 128: dc * NT + (tt + 1) * 128]
                    rhs = wb[:, wbuf * NCH * 512 + dc * 512:
                             wbuf * NCH * 512 + (dc + 1) * 512]
                    last = te.matmul(k.bank(bank), lhsT, rhs,
                                     start=(dc == 0), stop=(dc == NCH - 1))
                smm.inc(last)
                smm.wait(se, v=it + 1)
                if vit >= 2:
                    sst.wait(se, v=st_snap[vit - 2])
                else:
                    sst.wait(se, v=512 * 16 + 16)
                sev.inc(se.copy(evt[:, ebuf * 512:(ebuf + 1) * 512], k.bank(bank)))
                sev.wait(sy, v=it + 1)
                for hh in range(4):
                    h = vb * 4 + hh
                    sst.inc(sy.dma_start(
                        out=v0_ap[h * 128:(h + 1) * 128, tt * 128:(tt + 1) * 128],
                        in_=evt[:, ebuf * 512 + hh * 128:
                                ebuf * 512 + (hh + 1) * 128]), 16)
                st_snap[vit] = sst.v
                it += 1
        sst.wait(sy)


def emit_att0(k: KB, tag, q0_ap, k0_ap, v0_ap, bnd_ap, lam_ap, s0t_ap,
              factor_ap, rms_ap):
    nc, te, ve, se, sy, ge = k.nc, k.te, k.ve, k.se, k.sy, k.ge
    NH, SB = 16, 8
    NIT = NH * SB
    ld = k.sem(f"{tag}_ld")
    lq = k.sem(f"{tag}_lq")
    pmm = k.sem(f"{tag}_pmm")
    aex = k.sem(f"{tag}_aex")
    psu = k.sem(f"{tag}_psu")
    vsr = k.sem(f"{tag}_vsr")
    pb12 = k.sem(f"{tag}_pb12")
    aev = k.sem(f"{tag}_aev")
    vdf = k.sem(f"{tag}_vdf")
    ae3 = k.sem(f"{tag}_ae3")
    pav = k.sem(f"{tag}_pav")
    ps3 = k.sem(f"{tag}_ps3")
    vs3 = k.sem(f"{tag}_vs3")
    pb3 = k.sem(f"{tag}_pb3")
    rdm = k.sem(f"{tag}_rdm")
    vout = k.sem(f"{tag}_vout")
    vsq = k.sem(f"{tag}_vsq")
    prm = k.sem(f"{tag}_prm")
    vrm = k.sem(f"{tag}_vrm")
    sst = k.sem(f"{tag}_sst")
    vpr = k.sem(f"{tag}_vpr")
    asr = k.sem(f"{tag}_asr")
    snap = {}

    def put(name, i, sem):
        snap[(name, i)] = sem.v

    def at(name, i):
        return snap[(name, i)]

    from contextlib import ExitStack as _ES
    _les = _ES()

    def _sb(name, shape, dt):
        return _les.enter_context(nc.sbuf_tensor(name, shape, dt))

    E1 = _sb(f"{tag}_E1", [128, 2 * 16384], BF16)
    E2 = _sb(f"{tag}_E2", [128, 2 * 16384], BF16)
    Wq = _sb(f"{tag}_Wq", [128, 4096], BF16)
    K1 = _sb(f"{tag}_K1", [128, NT], BF16)
    K2 = _sb(f"{tag}_K2", [128, NT], BF16)
    Vt = _sb(f"{tag}_V", [128, NT], BF16)
    qt = _sb(f"{tag}_q", [128, 4 * 512], BF16)
    Rs = _sb(f"{tag}_Rs", [128, 2 * 1536], BF16)
    ob = _sb(f"{tag}_ob", [128, 2 * 512], BF16)
    sq2b = _sb(f"{tag}_sq2", [128, 2 * 512], BF16)
    sr = _sb(f"{tag}_sr", [1, 512], F32)[:]
    rr = _sb(f"{tag}_rr", [1, 512], F32)[:]
    rb = _sb(f"{tag}_rb", [1, 2 * 1024], BF16)
    rb2f = _sb(f"{tag}_r2f", [1, 512], F32)[:]
    s3r = _sb(f"{tag}_s3", [1, 2 * 512], F32)
    b3r = _sb(f"{tag}_b3", [1, 2 * 512], BF16)
    st1 = _sb(f"{tag}_st1", [1, 512], F32)[:]
    pr = _sb(f"{tag}_pr", [1, 640], F32)[:]
    prb = _sb(f"{tag}_prb", [1, 64], BF16)[:]
    biasB = _sb(f"{tag}_bias", [128, 64], F32)[:]

    bnd = pr[:, 0:512]
    n2 = pr[:, 512:576]
    nrm = pr[:, 576:640]
    lq.inc(sy.dma_start(out=bnd, in_=bnd_ap), 16)
    lam_sb = biasB[0:1, 48:64]
    lq.inc(sy.dma_start(out=lam_sb, in_=lam_ap), 16)
    lq.wait(ve)
    for cid in range(64):
        k.chain("ve", ve.reduce_max(n2[:, cid:cid + 1],
                                    bnd[:, cid * 8:(cid + 1) * 8], axis=AXX))
    vpr.inc(ve.tensor_copy(nrm, n2))
    vpr.wait(se)
    aprep = k.sem(f"{tag}_aprep")
    aprep.inc(se.activation(nrm, n2, AF.Sqrt))
    aprep.wait(ve)
    k.chain("ve", ve.tensor_tensor(pr[:, 0:32], nrm[:, 0:32], nrm[:, 32:64],
                                   ALU.mult))
    k.chain("ve", ve.tensor_scalar(pr[:, 0:32], pr[:, 0:32],
                                   -1.0 / float(np.sqrt(128.0)), None,
                                   op0=ALU.mult))
    k.chain("ve", ve.tensor_scalar(pr[:, 64:80], lam_sb, -1.0, None,
                                   op0=ALU.mult))
    vpr.inc(ve.tensor_copy(pr[:, 80:96], pr[:, 64:80]))
    vpr.wait(se)
    aprep.inc(se.activation(pr[:, 64:80], pr[:, 80:96], AF.Relu))
    aprep.wait(ve)
    k.chain("ve", ve.tensor_scalar(pr[:, 32:48], pr[:, 64:80], -1.0, -1.0,
                                   op0=ALU.mult, op1=ALU.add))
    mlap = biasB[0:1, 62:63]
    k.chain("ve", ve.reduce_sum(pr[:, 96:97], lam_sb, axis=AXX))
    k.chain("ve", ve.tensor_scalar(mlap, pr[:, 96:97], -1.0 / 16.0,
                                   1.0, op0=ALU.mult, op1=ALU.add))
    vpr.inc(ve.tensor_copy(prb[:, 0:48], pr[:, 0:48]))
    vpr.wait(te)
    pprep = k.sem(f"{tag}_pprep")
    pprep.inc(te.matmul(k.bank(0)[:, 0:48], k.ones_row[:], prb[:, 0:48],
                        start=True, stop=True))
    pprep.wait(ve)
    vpr.inc(ve.tensor_copy(biasB[:, 0:48], k.bank(0)[:, 0:48]))
    vpr.inc(ve.memset(st1, 0.0))
    vpr.wait(ge)
    for sb_ in range(8):
        rdm.inc(ge.dma_start(out=rms_ap[0:1, sb_ * 512:(sb_ + 1) * 512],
                             in_=st1), 16)
    put("rdm", -1, rdm)
    rdm.wait(ve)
    vpr.wait(te)
    vpr.wait(se)
    vpr.wait(sy)

    def emit_e3q(j, q_):
        h = j // SB
        P = j % 2
        eP = 16384 * P
        vdf.wait(se, v=at("vdfq", (j, q_)))
        qs = slice(eP + q_ * 4096, eP + (q_ + 1) * 4096)
        ae3.inc(se.activation(E1[:, qs], E2[:, qs], AF.Exp,
                              bias=biasB[:, 32 + h:33 + h]))
        put(("ae3q", q_), j, ae3)
        if q_ == 3:
            put("ae3", j, ae3)

    def emit_post(j):
        h, sb = j // SB, j % SB
        P = j % 2
        eP = 16384 * P
        AVB = 4 + P
        RB = 6 + P
        if j >= 2:
            vout.wait(te, v=at("vout", j - 2))
        ld.wait(te, v=at("ldv", j // SB))
        last = None
        for tt in range(NTL):
            if tt % 8 == 0:
                ae3.wait(te, v=at(("ae3q", tt // 8), j))
            last = te.matmul(k.bank(AVB), Vt[:, tt * 128:(tt + 1) * 128],
                             E1[:, eP + tt * 512:eP + (tt + 1) * 512],
                             start=(tt == 0), stop=(tt == NTL - 1))
        pav.inc(last)
        put("pav", j, pav)
        aev.wait(te, v=at("aevR2", j))
        last = None
        for tt in range(NTL):
            last = te.matmul(k.bank(RB, rows=1, row0=64), k.ones_col[:],
                             E1[:, eP + tt * 512:eP + (tt + 1) * 512],
                             start=(tt == 0), stop=(tt == NTL - 1))
        ps3.inc(last)
        put("ps3", j, ps3)
        ps3.wait(se, v=at("ps3", j))
        k.chain("se", se.copy(s3r[:, P * 512:(P + 1) * 512],
                              k.ps[64:65, 512 * RB:512 * RB + 512]))
        k.chain("se", se.activation(rr, s3r[:, P * 512:(P + 1) * 512], AF.Ln))
        vs3.inc(se.activation(b3r[:, P * 512:(P + 1) * 512], rr, AF.Exp,
                              scale=-1.0))
        put("vs3", j, vs3)
        vs3.wait(te, v=at("vs3", j))
        pb3.inc(te.matmul(k.bank(RB), k.ones_row[:],
                          b3r[:, P * 512:(P + 1) * 512],
                          start=True, stop=True))
        put("pb3", j, pb3)
        pb3.wait(se, v=at("pb3", j))
        aev.inc(se.copy(Rs[:, P * 1536 + 1024:P * 1536 + 1536], k.bank(RB)))
        put("aevR3", j, aev)
        pav.wait(ve, v=at("pav", j))
        aev.wait(ve, v=at("aevR3", j))
        if j >= 2:
            sst.wait(ve, v=at("sst", j - 2))
        vout.inc(ve.tensor_tensor(ob[:, 512 * P:512 * (P + 1)], k.bank(AVB),
                                  Rs[:, P * 1536 + 1024:P * 1536 + 1536],
                                  ALU.mult))
        put("vout", j, vout)
        if j >= 2:
            prm.wait(ve, v=at("prm", j - 2))
        vsq.inc(ve.tensor_tensor(sq2b[:, 512 * P:512 * (P + 1)],
                                 ob[:, 512 * P:512 * (P + 1)],
                                 ob[:, 512 * P:512 * (P + 1)], ALU.mult))
        put("vsq", j, vsq)
        vsq.wait(te, v=at("vsq", j))
        aev.wait(te, v=at("aevR3", j))
        prm.inc(te.matmul(k.bank(RB, rows=1, row0=32), k.ones_col[:],
                          sq2b[:, 512 * P:512 * (P + 1)],
                          start=True, stop=True))
        put("prm", j, prm)
        prm.wait(ve, v=at("prm", j))
        rdm.wait(ve, v=at("rdm", j - 1))
        vrm.inc(ve.tensor_copy(st1, k.ps[32:33, 512 * RB:512 * RB + 512]))
        put("vrm", j, vrm)
        vrm.wait(ge, v=at("vrm", j))
        rdm.inc(ge.dma_start(out=rms_ap[0:1, sb * 512:(sb + 1) * 512],
                             in_=st1, accum_op=ALU.add), 16)
        put("rdm", j, rdm)
        vout.wait(sy, v=at("vout", j))
        sst.inc(sy.dma_start(
            out=s0t_ap[h * 128:(h + 1) * 128, sb * 512:(sb + 1) * 512],
            in_=ob[:, 512 * P:512 * (P + 1)]), 16)
        put("sst", j, sst)

    NITL = min(NIT, ATT0_IT_LIMIT) if ATT0_IT_LIMIT else NIT
    for i in range(NITL):
        h, sb = i // SB, i % SB
        P = i % 2
        eP = 16384 * P
        RB = 6 + P
        if sb == 0:
            if h > 0:
                pmm.wait(sy, v=at("pmm", i - 1))
            ld.inc(sy.dma_start(out=K1[:],
                                in_=k0_ap[h * 256:h * 256 + 128, :]), 16)
            ld.inc(sy.dma_start(out=K2[:],
                                in_=k0_ap[h * 256 + 128:h * 256 + 256, :]), 16)
            if h == 0:
                ld.inc(sy.dma_start(out=Vt[:], in_=v0_ap[0:128, :]), 16)
                put("ldv", 0, ld)
        if i >= 2:
            pmm.wait(sy, v=at("pmm", i - 1))
        lq.inc(sy.dma_start(
            out=qt[:, P * 1024:P * 1024 + 512],
            in_=q0_ap[h * 256:h * 256 + 128, sb * 512:(sb + 1) * 512]), 16)
        lq.inc(sy.dma_start(
            out=qt[:, P * 1024 + 512:P * 1024 + 1024],
            in_=q0_ap[h * 256 + 128:h * 256 + 256, sb * 512:(sb + 1) * 512]),
            16)

        ld.wait(te)
        lq.wait(te)
        for u in range(32):
            sc = u % 2
            g = u // 2
            reg = u % 2
            Ksb = K1 if sc == 0 else K2
            qv = qt[:, P * 1024 + sc * 512: P * 1024 + (sc + 1) * 512]
            wv_ = 32 * i + u - 1
            if wv_ > 0:
                aex.wait(te, v=wv_)
            last = None
            for tj in range(2):
                tt = g * 2 + tj
                last = te.matmul(k.bank(2 * reg + tj),
                                 Ksb[:, tt * 128:(tt + 1) * 128], qv,
                                 start=True, stop=True)
            pmm.inc(last)
        put("pmm", i, pmm)

        hc1 = h * 2
        for u in range(32):
            sc = u % 2
            g = u // 2
            reg = u % 2
            if i >= 1 and not ATT0_NO_POST and u in (10, 16, 22, 28):
                emit_e3q(i - 1, (u - 10) // 6)
            pmm.wait(se, v=at("pmm", i) - 32 + u + 1)
            dstE = E1 if sc == 0 else E2
            if i >= 2 and u < 2:
                vdf.wait(se, v=at("vdf", i - 2))
                psu.wait(se, v=at("psu", i - 2))
                if ("ps3", i - 2) in snap:
                    pav.wait(se, v=at("pav", i - 2))
                    ps3.wait(se, v=at("ps3", i - 2))
            aex.inc(se.activation(
                dstE[:, eP + g * 1024: eP + (g + 1) * 1024],
                k.bank(2 * reg, nb=2), AF.Exp,
                bias=biasB[:, hc1 + sc:hc1 + sc + 1],
                scale=float(1.0 / np.sqrt(128.0))))
        put("aexe", i, aex)

        if i >= 1 and not ATT0_NO_POST:
            emit_post(i - 1)
        if sb == 0 and h > 0:
            if ("pav", i - 1) in snap:
                pav.wait(sy, v=at("pav", i - 1))
            ld.inc(sy.dma_start(out=Vt[:],
                                in_=v0_ap[h * 128:(h + 1) * 128, :]), 16)
            put("ldv", h, ld)

        aex.wait(te)
        if i >= 2:
            vrm.wait(te, v=at("vrm", i - 2))
        first = True
        last = None
        for tt in range(NTL):
            last = te.matmul(k.bank(RB, rows=1), k.ones_col[:],
                             E1[:, eP + tt * 512:eP + (tt + 1) * 512],
                             start=first, stop=False)
            first = False
            last = te.matmul(k.bank(RB, rows=1, row0=32), k.ones_col[:],
                             E2[:, eP + tt * 512:eP + (tt + 1) * 512],
                             start=False, stop=(tt == NTL - 1))
        psu.inc(last)
        put("psu", i, psu)
        psu.wait(se, v=at("psu", i))
        k.chain("se", se.copy(sr, k.ps[0:1, 512 * RB:512 * RB + 512]))
        k.chain("se", se.activation(rr, sr, AF.Ln))
        k.chain("se", se.activation(rb[0:1, P * 1024:P * 1024 + 512], rr,
                                    AF.Exp, scale=-1.0))
        k.chain("se", se.copy(sr, k.ps[32:33, 512 * RB:512 * RB + 512]))
        k.chain("se", se.activation(rr, sr, AF.Ln))
        asr.inc(se.activation(rb2f, rr, AF.Exp, scale=-1.0))
        put("asr", i, asr)
        asr.wait(ve, v=at("asr", i))
        vsr.inc(ve.tensor_scalar(rb[0:1, P * 1024 + 512:P * 1024 + 1024], rb2f,
                                 lam_sb[:, h:h + 1], None, op0=ALU.mult))
        put("vsr", i, vsr)
        vsr.wait(te, v=at("vsr", i))
        pb12.inc(te.matmul(k.bank(RB), k.ones_row[:],
                           rb[0:1, P * 1024:P * 1024 + 512],
                           start=True, stop=True))
        pb12.wait(se)
        aev.inc(se.copy(Rs[:, P * 1536:P * 1536 + 512], k.bank(RB)))
        put("aevR1", i, aev)
        aev.wait(te, v=at("aevR1", i))
        pb12.inc(te.matmul(k.bank(RB), k.ones_row[:],
                           rb[0:1, P * 1024 + 512:P * 1024 + 1024],
                           start=True, stop=True))
        pb12.wait(se)
        aev.inc(se.copy(Rs[:, P * 1536 + 512:P * 1536 + 1024], k.bank(RB)))
        put("aevR2", i, aev)
        put("pb12", i, pb12)

        aev.wait(ve, v=at("aevR2", i))
        R1v = Rs[:, P * 1536:P * 1536 + 512]
        R2v = Rs[:, P * 1536 + 512:P * 1536 + 1024]
        for q_ in range(4):
            for tt in range(q_ * 8, (q_ + 1) * 8):
                sl = slice(eP + tt * 512, eP + (tt + 1) * 512)
                wsl = slice((tt - q_ * 8) * 512, (tt - q_ * 8 + 1) * 512)
                k.chain("ve", ve.tensor_tensor(Wq[:, wsl], E1[:, sl], R1v,
                                               ALU.mult))
            for tt in range(q_ * 8, (q_ + 1) * 8):
                sl = slice(eP + tt * 512, eP + (tt + 1) * 512)
                k.chain("ve", ve.tensor_tensor(E1[:, sl], E2[:, sl], R2v,
                                               ALU.mult))
            for tt in range(q_ * 8, (q_ + 1) * 8):
                sl = slice(eP + tt * 512, eP + (tt + 1) * 512)
                wsl = slice((tt - q_ * 8) * 512, (tt - q_ * 8 + 1) * 512)
                last = ve.tensor_tensor(E2[:, sl], Wq[:, wsl], E1[:, sl],
                                        ALU.subtract)
                if tt != (q_ + 1) * 8 - 1:
                    k.chain("ve", last)
            vdf.inc(last)
            put("vdfq", (i, q_), vdf)
        put("vdf", i, vdf)

    if not ATT0_NO_POST:
        for q_ in range(4):
            emit_e3q(NITL - 1, q_)
        emit_post(NITL - 1)

    rdm.wait(sy)
    lf = k.sem(f"{tag}_lf")
    vvf = k.sem(f"{tag}_vvf")
    afc = k.sem(f"{tag}_afc")
    for cch in range(8):
        csl = slice(cch * 512, (cch + 1) * 512)
        if cch >= 1:
            sst.wait(sy, v=at("fst", cch - 1))
        lf.inc(sy.dma_start(out=rr, in_=rms_ap[0:1, csl]), 16)
        lf.wait(ve)
        k.chain("ve", ve.tensor_scalar(sr, rr, 1.0 / D, None, op0=ALU.mult))
        vvf.inc(ve.tensor_copy(rr, sr))
        vvf.wait(se)
        afc.inc(se.activation(sr, rr, AF.Sqrt, bias=k.constf(EPS_RMS, rows=1)))
        afc.wait(ve)
        k.chain("ve", ve.reciprocal(rr, sr))
        vvf.inc(ve.tensor_scalar(sr, rr, mlap, None, op0=ALU.mult))
        vvf.wait(sy)
        sst.inc(sy.dma_start(out=factor_ap[0:1, csl], in_=sr), 16)
        snap[("fst", cch)] = sst.v
    sst.wait(sy)
    _les.close()


def emit_wo0(k: KB, tag, s0t_ap, factor_ap, wo_ap, ns_ap, s0_ap, n_tok=NT,
             d_out=D, use_ns=True, use_factor=True):
    nc, te, ve, se, sy, ge = k.nc, k.te, k.ve, k.se, k.sy, k.ge
    wl = k.sem(f"{tag}_wl")
    pl = k.sem(f"{tag}_pl")
    fl = k.sem(f"{tag}_fl")
    vsc = k.sem(f"{tag}_vsc")
    pmm = k.sem(f"{tag}_pmm")
    aev = k.sem(f"{tag}_aev")
    pfb = k.sem(f"{tag}_pfb")
    sst = k.sem(f"{tag}_sst")
    NOC = d_out // 128
    NMAC = n_tok // 2048
    snap = {}
    with (
        nc.sbuf_tensor(f"{tag}_wo", [128, NCH * d_out], BF16) as wo_sb,
        nc.sbuf_tensor(f"{tag}_pan", [128, NCH * 2048], BF16) as pan,
        nc.sbuf_tensor(f"{tag}_sc", [128, 2 * NCH * 512], BF16) as scl,
        nc.sbuf_tensor(f"{tag}_fb", [128, 2048], BF16) as fb,
        nc.sbuf_tensor(f"{tag}_fr", [1, 2048], F32) as fr_h,
        nc.sbuf_tensor(f"{tag}_frb", [1, 2048], BF16) as frb_h,
        nc.sbuf_tensor(f"{tag}_ns", [128, 16], F32) as ns_sb,
        nc.sbuf_tensor(f"{tag}_ev", [128, 2 * 512], BF16) as evt,
    ):
        fr, frb = fr_h[:], frb_h[:]
        for fc in range(NCH):
            wl.inc(ge.dma_start(out=wo_sb[:, fc * d_out:(fc + 1) * d_out],
                                in_=wo_ap[fc * 128:(fc + 1) * 128, :]), 16)
        if use_ns:
            wl.inc(ge.dma_start(out=ns_sb[:], in_=ns_ap), 16)
        nit = 0
        noc_it = 0
        for mac in range(NMAC):
            msl = slice(mac * 2048, (mac + 1) * 2048)
            if mac > 0:
                pmm.wait(sy, v=snap[("pmm_mac", mac - 1)])
            for fc in range(NCH):
                pl.inc(sy.dma_start(out=pan[:, fc * 2048:(fc + 1) * 2048],
                                    in_=s0t_ap[fc * 128:(fc + 1) * 128, msl]), 16)
            if use_factor:
                fl.inc(sy.dma_start(out=fr, in_=factor_ap[0:1, msl]), 16)
                fl.wait(ve)
                if mac > 0:
                    vsc.wait(ve, v=snap[("vsc_mac", mac - 1)])
                k.chain("ve", ve.tensor_copy(frb, fr))
                vsc.inc(ve.tensor_copy(frb[:, 0:1], fr[:, 0:1]))
                vsc.wait(te)
                lastb = None
                for j in range(4):
                    lastb = te.matmul(k.bank(4 + j), k.ones_row[:],
                                      frb[:, j * 512:(j + 1) * 512],
                                      start=True, stop=True)
                pfb.inc(lastb)
                pfb.wait(ve)
                for j in range(4):
                    k.chain("ve", ve.tensor_copy(fb[:, j * 512:(j + 1) * 512],
                                                 k.bank(4 + j)))
            for nb in range(4):
                nbs = slice(nb * 512, (nb + 1) * 512)
                sb_ = nit % 2
                ssl0 = sb_ * NCH * 512
                pl.wait(ve)
                if nit >= 2:
                    pmm.wait(ve, v=snap[("pmm_nit", nit - 2)])
                last = None
                for fc in range(NCH):
                    src = pan[:, fc * 2048 + nb * 512: fc * 2048 + (nb + 1) * 512]
                    dst = scl[:, ssl0 + fc * 512:ssl0 + (fc + 1) * 512]
                    if use_ns:
                        k.chain("ve", ve.tensor_scalar(
                            dst, src, ns_sb[:, fc:fc + 1], None, op0=ALU.mult))
                        if use_factor:
                            last = ve.tensor_tensor(dst, dst, fb[:, nbs], ALU.mult)
                    elif use_factor:
                        last = ve.tensor_tensor(dst, src, fb[:, nbs], ALU.mult)
                    else:
                        last = ve.tensor_copy(dst, src)
                    if fc < NCH - 1:
                        k.chain("ve", last)
                vsc.inc(last)
                snap[("vsc_nit", nit)] = vsc.v
                vsc.wait(te)
                wl.wait(te)
                for oc in range(NOC):
                    bank = noc_it % 2
                    if noc_it >= 2:
                        aev.wait(te, v=noc_it - 1)
                    last = None
                    for fc in range(NCH):
                        last = te.matmul(
                            k.bank(bank),
                            wo_sb[:, fc * d_out + oc * 128:
                                  fc * d_out + (oc + 1) * 128],
                            scl[:, ssl0 + fc * 512:ssl0 + (fc + 1) * 512],
                            start=(fc == 0), stop=(fc == NCH - 1))
                    pmm.inc(last)
                    pmm.wait(se)
                    if noc_it >= 2:
                        sst.wait(se, v=16 * (noc_it - 1))
                    aev.inc(se.copy(evt[:, bank * 512:(bank + 1) * 512],
                                    k.bank(bank)))
                    aev.wait(sy)
                    sst.inc(sy.dma_start(
                        out=s0_ap[oc * 128:(oc + 1) * 128,
                                  mac * 2048 + nb * 512:
                                  mac * 2048 + (nb + 1) * 512],
                        in_=evt[:, bank * 512:(bank + 1) * 512]), 16)
                    noc_it += 1
                snap[("pmm_nit", nit)] = pmm.v
                nit += 1
            snap[("pmm_mac", mac)] = pmm.v
            snap[("vsc_mac", mac)] = vsc.v
        sst.wait(sy)


def emit_ffn0(k: KB, tag, xn_ap, wT_ap, a_ap, b_ap, rt_ap, rep_ap, id_ap,
              g0_ap):
    nc, te, ve, se, sy, ge = k.nc, k.te, k.ve, k.se, k.sy, k.ge
    wl = k.sem(f"{tag}_wl")
    pl = k.sem(f"{tag}_pl")
    prt = k.sem(f"{tag}_prt")
    vrt = k.sem(f"{tag}_vrt")
    art = k.sem(f"{tag}_art")
    ptr = k.sem(f"{tag}_ptr")
    aev = k.sem(f"{tag}_aev")
    prp = k.sem(f"{tag}_prp")
    pax = k.sem(f"{tag}_pax")
    vax = k.sem(f"{tag}_vax")
    pmm = k.sem(f"{tag}_pmm")
    asl = k.sem(f"{tag}_asl")
    sst = k.sem(f"{tag}_sst")
    snap = {}
    with (
        nc.sbuf_tensor(f"{tag}_xn", [128, NCH * 2048], BF16) as xp,
        nc.sbuf_tensor(f"{tag}_wt", [128, NCH * D], BF16) as wt,
        nc.sbuf_tensor(f"{tag}_A", [128, NCH * 128], BF16) as A_sb,
        nc.sbuf_tensor(f"{tag}_B", [128, D], BF16) as B_sb,
        nc.sbuf_tensor(f"{tag}_B2", [128, D], BF16) as B2,
        nc.sbuf_tensor(f"{tag}_rt", [128, NCH * 8], BF16) as rt_sb,
        nc.sbuf_tensor(f"{tag}_rep", [8, 128], BF16) as rep_sb,
        nc.sbuf_tensor(f"{tag}_id", [128, 128], BF16) as id_sb,
        nc.sbuf_tensor(f"{tag}_wfm", [8, 2048], BF16) as wfm,
        nc.sbuf_tensor(f"{tag}_wrp", [128, 2048], BF16) as wrp,
        nc.sbuf_tensor(f"{tag}_axw", [128, 2048], BF16) as axw,
        nc.sbuf_tensor(f"{tag}_r1", [128, 64], F32) as r1_h,
        nc.sbuf_tensor(f"{tag}_rb1", [128, 8], BF16) as rb1_h,
        nc.sbuf_tensor(f"{tag}_ev", [128, 2 * 512], BF16) as evt,
    ):
        r1 = r1_h[:]
        rb1 = rb1_h[:]
        for fc in range(NCH):
            wl.inc(ge.dma_start(out=wt[:, fc * D:(fc + 1) * D],
                                in_=wT_ap[fc * 128:(fc + 1) * 128, :]), 16)
        wl.inc(ge.dma_start(out=A_sb[:].rearrange("p (c f) -> p c f", c=NCH),
                            in_=a_ap.rearrange("(c p) f -> p c f", p=128)), 16)
        wl.inc(ge.dma_start(out=B_sb[:], in_=b_ap), 16)
        wl.inc(ge.dma_start(out=rt_sb[:].rearrange("p (c f) -> p c f", c=NCH),
                            in_=rt_ap.rearrange("(c p) f -> p c f", p=128)), 16)
        wl.inc(ge.dma_start(out=rep_sb[:], in_=rep_ap), 16)
        wl.inc(ge.dma_start(out=id_sb[:], in_=id_ap), 16)
        wl.wait(ve)
        vB2 = k.sem(f"{tag}_vB2")
        vB2.inc(ve.tensor_scalar(B2[:], B_sb[:], 2.0, None, op0=ALU.mult))
        vB2.wait(te)

        for mac in range(NT // 2048):
            msl = slice(mac * 2048, (mac + 1) * 2048)
            if mac > 0:
                if ("pmm_mac", mac - 1) in snap:
                    pmm.wait(sy, v=snap[("pmm_mac", mac - 1)])
                if ("pax_mac", mac - 1) in snap:
                    pax.wait(sy, v=snap[("pax_mac", mac - 1)])
                prt.wait(sy, v=snap[("prt_mac", mac - 1)] if
                         ("prt_mac", mac - 1) in snap else prt.v)
            for fc in range(NCH):
                pl.inc(sy.dma_start(out=xp[:, fc * 2048:(fc + 1) * 2048],
                                    in_=xn_ap[fc * 128:(fc + 1) * 128, msl]), 16)
            pl.wait(te)
            wl.wait(te)
            for tt in range(16):
                if mac + tt > 0:
                    prev = (mac, tt - 1) if tt else (mac - 1, 15)
                    vrt.wait(te, v=snap[("vrt", prev)])
                    art.wait(te, v=snap[("art", prev)])
                last = None
                for fc in range(NCH):
                    last = te.matmul(k.bank(0)[:, 0:8],
                                     xp[:, fc * 2048 + tt * 128:
                                        fc * 2048 + (tt + 1) * 128],
                                     rt_sb[:, fc * 8:(fc + 1) * 8],
                                     start=(fc == 0), stop=(fc == NCH - 1))
                prt.inc(last)
                prt.wait(ve)
                e_t = r1[:, 0:8]
                probs = r1[:, 8:16]
                eq1 = r1[:, 16:24]
                pm = r1[:, 24:32]
                eq2 = r1[:, 32:40]
                wsel = r1[:, 40:48]
                eq1s = r1[:, 56:64]
                mx = r1[:, 48:49]
                mxn = r1[:, 49:50]
                sm = r1[:, 50:51]
                rs = r1[:, 51:52]
                m1 = r1[:, 52:53]
                m2 = r1[:, 53:54]
                ws = r1[:, 54:55]
                rw = r1[:, 55:56]
                k.chain("ve", ve.reduce_max(mx, k.bank(0)[:, 0:8], axis=AXX))
                vrt.inc(ve.tensor_scalar(mxn, mx, -1.0, None, op0=ALU.mult))
                vrt.wait(se)
                art.inc(se.activation(e_t, k.bank(0)[:, 0:8], AF.Exp, bias=mxn,
                                      accum_out=sm))
                snap[("art", (mac, tt))] = art.v
                art.wait(ve)
                k.chain("ve", ve.reciprocal(rs, sm))
                k.chain("ve", ve.tensor_scalar(probs, e_t, rs, None, op0=ALU.mult))
                k.chain("ve", ve.reduce_max(m1, probs, axis=AXX))
                k.chain("ve", ve.tensor_scalar(eq1, probs, m1, None,
                                               op0=ALU.is_equal))
                k.chain("ve", ve.tensor_scalar(eq1s, eq1, -2.0, None, op0=ALU.mult))
                k.chain("ve", ve.tensor_tensor(pm, probs, eq1s, ALU.add))
                k.chain("ve", ve.reduce_max(m2, pm, axis=AXX))
                k.chain("ve", ve.tensor_scalar(eq2, probs, m2, None,
                                               op0=ALU.is_equal))
                k.chain("ve", ve.tensor_tensor(pm, eq1, eq2, ALU.add))
                k.chain("ve", ve.tensor_tensor(wsel, probs, pm, ALU.mult))
                k.chain("ve", ve.tensor_tensor(ws, m1, m2, ALU.add))
                k.chain("ve", ve.reciprocal(rw, ws))
                vrt.inc(ve.tensor_scalar(rb1, wsel, rw, None, op0=ALU.mult))
                snap[("vrt", (mac, tt))] = vrt.v
                vrt.wait(te)
                if mac + tt > 0:
                    aev.wait(te, v=snap[("aevt", (mac, tt - 1) if tt
                                         else (mac - 1, 15))])
                ptr.inc(te.matmul(k.bank(1, rows=8)[:, 0:128], rb1, id_sb[:],
                                  start=True, stop=True))
                ptr.wait(se)
                aev.inc(se.copy(wfm[:, tt * 128:(tt + 1) * 128],
                                k.bank(1, rows=8)[:, 0:128]))
                snap[("aevt", (mac, tt))] = aev.v
            snap[("prt_mac", mac)] = prt.v
            if FFN0_STAGE < 2:
                continue
            aev.wait(te)
            if mac > 0:
                vax.wait(se, v=snap[("vax_mac", mac - 1)])
            for nb in range(4):
                if nb > 0:
                    aev.wait(te)
                prp.inc(te.matmul(k.bank(2), rep_sb[:],
                                  wfm[:, nb * 512:(nb + 1) * 512],
                                  start=True, stop=True))
                prp.wait(se)
                aev.inc(se.copy(wrp[:, nb * 512:(nb + 1) * 512], k.bank(2)))
            aev.wait(ve)
            if mac > 0 and ("pmm_mac", mac - 1) in snap:
                pmm.wait(ve, v=snap[("pmm_mac", mac - 1)])
            for nb in range(4):
                vax.wait(te)
                last = None
                for fc in range(NCH):
                    last = te.matmul(k.bank(3), A_sb[:, fc * 128:(fc + 1) * 128],
                                     xp[:, fc * 2048 + nb * 512:
                                        fc * 2048 + (nb + 1) * 512],
                                     start=(fc == 0), stop=(fc == NCH - 1))
                pax.inc(last)
                pax.wait(ve)
                vax.inc(ve.tensor_tensor(axw[:, nb * 512:(nb + 1) * 512],
                                         k.bank(3), wrp[:, nb * 512:(nb + 1) * 512],
                                         ALU.mult))
            snap[("vax_mac", mac)] = vax.v
            snap[("pax_mac", mac)] = pax.v
            if FFN0_STAGE < 3:
                continue
            vax.wait(te)
            vB2.wait(te)
            bit = mac * NCH * 4
            for oc in range(NCH):
                for nb in range(4):
                    bank = 4 + bit % 2
                    if bit >= 2:
                        asl.wait(te, v=snap[("aslG", bit - 2)])
                    last = None
                    for fc in range(NCH):
                        last = te.matmul(
                            k.bank(bank),
                            wt[:, fc * D + oc * 128: fc * D + (oc + 1) * 128],
                            xp[:, fc * 2048 + nb * 512:
                               fc * 2048 + (nb + 1) * 512],
                            start=(fc == 0), stop=False)
                    last = te.matmul(k.bank(bank), B2[:, oc * 128:(oc + 1) * 128],
                                     axw[:, nb * 512:(nb + 1) * 512],
                                     start=False, stop=True)
                    pmm.inc(last)
                    pmm.wait(se)
                    if bit >= 2:
                        sst.wait(se, v=snap[("sstG", bit - 2)])
                    asl.inc(se.activation(evt[:, (bit % 2) * 512:
                                              (bit % 2 + 1) * 512],
                                          k.bank(bank), AF.Silu))
                    snap[("aslG", bit)] = asl.v
                    asl.wait(sy)
                    sst.inc(sy.dma_start(
                        out=g0_ap[oc * 128:(oc + 1) * 128,
                                  mac * 2048 + nb * 512:
                                  mac * 2048 + (nb + 1) * 512],
                        in_=evt[:, (bit % 2) * 512:(bit % 2 + 1) * 512]), 16)
                    snap[("sstG", bit)] = sst.v
                    bit += 1
            snap[("pmm_mac", mac)] = pmm.v
        sst.wait(sy)


def emit_combine(k: KB, tag, x_ap, s_ap, g_ap, xout_ap, n_tok=NT,
                 g_silu=False):
    nc, te, ve, se, sy, ge = k.nc, k.te, k.ve, k.se, k.sy, k.ge
    ld = k.sem(f"{tag}_ld")
    vc = k.sem(f"{tag}_vc")
    agl = k.sem(f"{tag}_agl")
    sst = k.sem(f"{tag}_sst")
    gdt = F32 if g_silu else BF16
    with (
        nc.sbuf_tensor(f"{tag}_x", [128, 2 * n_tok], F32) as xc,
        nc.sbuf_tensor(f"{tag}_s", [128, 2 * n_tok], BF16) as sc,
        nc.sbuf_tensor(f"{tag}_g", [128, 2 * n_tok], gdt) as gc,
        nc.sbuf_tensor(f"{tag}_gs", [128, n_tok], BF16) as gs,
        nc.sbuf_tensor(f"{tag}_sg", [128, n_tok], BF16) as sg,
        nc.sbuf_tensor(f"{tag}_o", [128, 2 * n_tok], F32) as oc_,
    ):
        for c in range(NCH):
            buf = c % 2
            sl = slice(buf * n_tok, (buf + 1) * n_tok)
            if c >= 2:
                vc.wait(sy, v=c - 1)
                if g_silu:
                    agl.wait(sy, v=c - 1)
            ld.inc(sy.dma_start(out=xc[:, sl],
                                in_=x_ap[c * 128:(c + 1) * 128, :]), 16)
            ld.inc(sy.dma_start(out=sc[:, sl],
                                in_=s_ap[c * 128:(c + 1) * 128, :]), 16)
            ld.inc(sy.dma_start(out=gc[:, sl],
                                in_=g_ap[c * 128:(c + 1) * 128, :]), 16)
            if g_silu:
                ld.wait(se)
                if c >= 1:
                    vc.wait(se, v=c)
                agl.inc(se.activation(gs[:], gc[:, sl], AF.Silu))
                agl.wait(ve)
            ld.wait(ve)
            if c >= 2:
                sst.wait(ve, v=16 * (c - 1))
            gsrc = gs[:] if g_silu else gc[:, sl]
            k.chain("ve", ve.tensor_tensor(sg[:], sc[:, sl], gsrc, ALU.mult))
            vc.inc(ve.tensor_tensor(oc_[:, sl], xc[:, sl], sg[:], ALU.add))
            vc.wait(sy)
            sst.inc(sy.dma_start(out=xout_ap[c * 128:(c + 1) * 128, :],
                                 in_=oc_[:, sl]), 16)
        sst.wait(sy)


def emit_qkv1(k: KB, tag, xn_ap, wq_ap, wk_ap, wv_ap, cos_ap, sin_ap, psw_ap,
              q1_ap, k1_ap, v1_ap, bnd_ap):
    nc, te, ve, se, sy, ge = k.nc, k.te, k.ve, k.se, k.sy, k.ge
    xl = k.sem(f"{tag}_xl")
    wl = k.sem(f"{tag}_wl")
    smm = k.sem(f"{tag}_mm")
    sev = k.sem(f"{tag}_ev")
    psw = k.sem(f"{tag}_sw")
    vro = k.sem(f"{tag}_ro")
    pnm = k.sem(f"{tag}_nm")
    vmx = k.sem(f"{tag}_mx")
    vsq = k.sem(f"{tag}_sq")
    sst = k.sem(f"{tag}_st")
    snap = {}
    with (
        nc.sbuf_tensor(f"{tag}_xn", [128, NCH * NT], BF16) as xn_sb,
        nc.sbuf_tensor(f"{tag}_wb", [128, 2 * NCH * 512], BF16) as wb,
        nc.sbuf_tensor(f"{tag}_cos", [128, NT], F32) as cos_sb,
        nc.sbuf_tensor(f"{tag}_sin", [128, NT], F32) as sin_sb,
        nc.sbuf_tensor(f"{tag}_psw", [128, 128], BF16) as psw_sb,
        nc.sbuf_tensor(f"{tag}_ho", [128, 2], BF16) as ho_sb,
        nc.sbuf_tensor(f"{tag}_qe", [128, 2 * 512], BF16) as qe,
        nc.sbuf_tensor(f"{tag}_u1", [128, 512], F32) as u1_h,
        nc.sbuf_tensor(f"{tag}_u2", [128, 512], F32) as u2_h,
        nc.sbuf_tensor(f"{tag}_rq", [128, 2 * 512], BF16) as rq,
        nc.sbuf_tensor(f"{tag}_sq2", [128, 2 * 512], BF16) as sq2,
        nc.sbuf_tensor(f"{tag}_mxr", [2, 512], F32) as mxr_h,
    ):
        u1, u2, mxr = u1_h[:], u2_h[:], mxr_h[:]
        init2 = k.sem(f"{tag}_in2")
        k.ve.memset(ho_sb[:], 0.0)
        k.ve.memset(ho_sb[0:64, 0:1], 1.0)
        init2.inc(k.ve.memset(ho_sb[64:128, 1:2], 1.0))
        init2.wait(te)
        for c in range(NCH):
            xl.inc(sy.dma_start(out=xn_sb[:, c * NT:(c + 1) * NT],
                                in_=xn_ap[c * 128:(c + 1) * 128, :]), 16)
        xl.inc(sy.dma_start(out=cos_sb[:], in_=cos_ap), 16)
        xl.inc(sy.dma_start(out=sin_sb[:], in_=sin_ap), 16)
        xl.inc(ge.dma_start(out=psw_sb[:], in_=psw_ap), 16)
        xl.wait(te)

        blocks = [("q", wq_ap, q1_ap, 4, 4, 0), ("k", wk_ap, k1_ap, 1, 8, 64)]
        it = 0
        for name, w_ap, dst_ap, nfb, ntb, nslot0 in blocks:
            for fb in range(nfb):
                bi = it
                wbuf = (len(snap) * 0 + fb + (0 if name == "q" else 4)) % 2
                if ("wld", wbuf) in snap:
                    smm.wait(ge, v=snap[("wld", wbuf)])
                src = w_ap[:, fb * 512:(fb + 1) * 512]
                src = src.rearrange("(dc p) f -> p dc f", p=128)
                dst3 = wb[:, wbuf * NCH * 512:(wbuf + 1) * NCH * 512].rearrange(
                    "p (dc f) -> p dc f", dc=NCH)
                wl.inc(ge.dma_start(out=dst3, in_=src), 16)
                wseen = wl.v
                for fc in range(4):
                    for tb in range(ntb):
                        buf = it % 2
                        bank = it % 2
                        wl.wait(te, v=wseen)
                        if it >= 2:
                            sev.wait(te, v=snap[("sev", it - 2)])
                        last = None
                        for dc in range(NCH):
                            lhsT = wb[:, wbuf * NCH * 512 + dc * 512 + fc * 128:
                                      wbuf * NCH * 512 + dc * 512 + (fc + 1) * 128]
                            rhs = xn_sb[:, dc * NT + tb * 512:
                                        dc * NT + (tb + 1) * 512]
                            last = te.matmul(k.bank(bank), lhsT, rhs,
                                             start=(dc == 0), stop=(dc == NCH - 1))
                        smm.inc(last)
                        snap[("wld", wbuf)] = smm.v
                        smm.wait(se, v=smm.v)
                        if it >= 2:
                            psw.wait(se, v=snap[("psw", it - 2)])
                            k.sem("chain_ve").wait(se, v=snap[("vroU2", it - 2)])
                        sev.inc(se.copy(qe[:, buf * 512:(buf + 1) * 512],
                                        k.bank(bank)))
                        snap[("sev", it)] = sev.v
                        sev.wait(te, v=snap[("sev", it)])
                        if it >= 2:
                            k.sem("chain_ve").wait(te, v=snap[("vroU2", it - 2)])
                        psw.inc(te.matmul(k.bank(2 + bank), psw_sb[:],
                                          qe[:, buf * 512:(buf + 1) * 512],
                                          start=True, stop=True))
                        snap[("psw", it)] = psw.v
                        csl = slice(tb * 512, (tb + 1) * 512)
                        psw.wait(ve, v=snap[("psw", it)])
                        k.chain("ve", ve.tensor_tensor(
                            u1, qe[:, buf * 512:(buf + 1) * 512],
                            cos_sb[:, csl], ALU.mult))
                        k.chain("ve", ve.tensor_tensor(u2, k.bank(2 + bank),
                                                       sin_sb[:, csl], ALU.mult))
                        chv = k.sem("chain_ve")
                        snap[("vroU2", it)] = chv.v
                        if it >= 2:
                            sst.wait(ve, v=snap[("sstq", it - 2)])
                        k.chain("ve", ve.tensor_tensor(
                            rq[:, buf * 512:(buf + 1) * 512], u1, u2, ALU.add))
                        snap[("vro", it)] = chv.v
                        if it >= 1:
                            pnm.wait(ve, v=snap[("pnm", it - 1)])
                        vsq.inc(ve.tensor_tensor(
                            sq2[:, buf * 512:(buf + 1) * 512],
                            rq[:, buf * 512:(buf + 1) * 512],
                            rq[:, buf * 512:(buf + 1) * 512], ALU.mult))
                        vsq.wait(te, v=vsq.v)
                        if it >= 2:
                            vmx.wait(te, v=snap[("vmx", it - 2)])
                        pnm.inc(te.matmul(k.bank(4 + bank, rows=2), ho_sb[:],
                                          sq2[:, buf * 512:(buf + 1) * 512],
                                          start=True, stop=True))
                        snap[("pnm", it)] = pnm.v
                        pnm.wait(ve, v=snap[("pnm", it)])
                        slot = nslot0 + (fb * 4 + fc) * ntb + tb
                        vmx.inc(ve.reduce_max(mxr[:, slot:slot + 1],
                                              k.bank(4 + bank, rows=2), axis=AXX))
                        snap[("vmx", it)] = vmx.v
                        k.sem("chain_ve").wait(sy, v=snap[("vro", it)])
                        sst.inc(sy.dma_start(
                            out=dst_ap[fb * 512 + fc * 128:
                                       fb * 512 + (fc + 1) * 128,
                                       tb * 512:(tb + 1) * 512],
                            in_=rq[:, buf * 512:(buf + 1) * 512]), 16)
                        snap[("sstq", it)] = sst.v
                        it += 1
        vmx.wait(sy)
        sst.inc(sy.dma_start(out=bnd_ap, in_=mxr), 16)

        wbuf = 0
        smm.wait(ge, v=smm.v)
        src = wv_ap.rearrange("(dc p) f -> p dc f", p=128)
        dst3 = wb[:, 0:NCH * 512].rearrange("p (dc f) -> p dc f", dc=NCH)
        wl.inc(ge.dma_start(out=dst3, in_=src), 16)
        st_v = {}
        for tt in range(NTL):
            bank = tt % 2
            buf = tt % 2
            wl.wait(te)
            if tt >= 2:
                sev.wait(te, v=snap[("sevv", tt - 2)])
            last = None
            for dc in range(NCH):
                lhsT = xn_sb[:, dc * NT + tt * 128: dc * NT + (tt + 1) * 128]
                rhs = wb[:, dc * 512:(dc + 1) * 512]
                last = te.matmul(k.bank(bank), lhsT, rhs,
                                 start=(dc == 0), stop=(dc == NCH - 1))
            smm.inc(last)
            smm.wait(se, v=smm.v)
            if tt >= 2:
                sst.wait(se, v=st_v[tt - 2])
            sev.inc(se.copy(qe[:, buf * 512:(buf + 1) * 512], k.bank(bank)))
            snap[("sevv", tt)] = sev.v
            sev.wait(sy, v=snap[("sevv", tt)])
            for g in range(8):
                sst.inc(sy.dma_start(
                    out=v1_ap[g * 128:(g + 1) * 128, tt * 64:(tt + 1) * 64],
                    in_=qe[:, buf * 512 + g * 64: buf * 512 + (g + 1) * 64]), 16)
            st_v[tt] = sst.v
        sst.wait(sy)


def emit_att1(k: KB, tag, q1_ap, k1_ap, v1_ap, bnd_ap, cm_ap, a1_ap):
    nc, te, ve, se, sy, ge = k.nc, k.te, k.ve, k.se, k.sy, k.ge
    ld = k.sem(f"{tag}_ld")
    lq = k.sem(f"{tag}_lq")
    pmm = k.sem(f"{tag}_pmm")
    aex = k.sem(f"{tag}_aex")
    vmk = k.sem(f"{tag}_vmk")
    pav = k.sem(f"{tag}_pav")
    vs3 = k.sem(f"{tag}_vs3")
    pb3 = k.sem(f"{tag}_pb3")
    aev = k.sem(f"{tag}_aev")
    vout = k.sem(f"{tag}_vout")
    vms = k.sem(f"{tag}_vms")
    sst = k.sem(f"{tag}_sst")
    vpr = k.sem(f"{tag}_vpr")
    snap = {}

    def put(n, i, sem):
        snap[(n, i)] = sem.v

    def at(n, i):
        return snap[(n, i)]

    from contextlib import ExitStack as _ES
    _les = _ES()

    def _sb(name, shape, dt):
        return _les.enter_context(nc.sbuf_tensor(name, shape, dt))

    E1 = _sb(f"{tag}_E1", [128, 2 * 16384], BF16)
    MB = _sb(f"{tag}_MB", [128, 2 * 4096], BF16)
    Kp = _sb(f"{tag}_K", [128, NT], BF16)
    V65 = _sb(f"{tag}_V", [128, 2 * 32 * 65], BF16)
    qt = _sb(f"{tag}_q", [128, 2 * 512], BF16)
    CM = _sb(f"{tag}_cm", [128, 4 * 8 * 512], BF16)
    Rs = _sb(f"{tag}_Rs", [128, 512], BF16)
    ob = _sb(f"{tag}_ob", [128, 512], BF16)
    s3r = _sb(f"{tag}_s3", [1, 512], F32)[:]
    r3r = _sb(f"{tag}_r3", [1, 512], F32)[:]
    b3r = _sb(f"{tag}_b3", [1, 512], BF16)[:]
    pr = _sb(f"{tag}_pr", [1, 1280], F32)[:]
    prb = _sb(f"{tag}_prb", [1, 64], BF16)[:]
    biasB = _sb(f"{tag}_bias", [128, 32], F32)[:]

    bnd = pr[:, 0:1024]
    lq.inc(sy.dma_start(out=pr[:, 0:512], in_=bnd_ap[0:1, :]), 16)
    lq.inc(sy.dma_start(out=pr[:, 512:1024], in_=bnd_ap[1:2, :]), 16)
    lq.inc(ge.dma_start(out=CM[:], in_=cm_ap), 16)
    lq.wait(ve)
    bq = pr[:, 1024 + 0:1024 + 32]
    bk = pr[:, 1024 + 32:1024 + 40]
    scr40 = pr[:, 1064:1104]
    bk4 = pr[:, 1104:1136]
    prod = pr[:, 1136:1168]
    for h in range(32):
        col = (h % 2) * 512 + (h // 2) * 4
        k.chain("ve", ve.reduce_max(bq[:, h:h + 1], pr[:, col:col + 4], axis=AXX))
    for g in range(8):
        col = (g % 2) * 512 + 64 + (g // 2) * 8
        k.chain("ve", ve.reduce_max(bk[:, g:g + 1], pr[:, col:col + 8], axis=AXX))
    vpr.inc(ve.tensor_copy(scr40, pr[:, 1024:1064]))
    vpr.wait(se)
    apr = k.sem(f"{tag}_apr")
    apr.inc(se.activation(pr[:, 1024:1064], scr40, AF.Sqrt))
    apr.wait(ve)
    bk4v = bk4.rearrange("p (a b) -> p a b", b=4)
    for j in range(4):
        k.chain("ve", ve.tensor_copy(bk4v[:, :, j:j + 1],
                                     bk.rearrange("p (a b) -> p a b", b=1)))
    k.chain("ve", ve.tensor_tensor(prod, bq, bk4, ALU.mult))
    vpr.inc(ve.tensor_scalar(prb[:, 0:32], prod, -1.0 / 8.0, None, op0=ALU.mult))
    vpr.wait(te)
    ppr = k.sem(f"{tag}_ppr")
    ppr.inc(te.matmul(k.bank(0)[:, 0:32], k.ones_row[:], prb[:, 0:32],
                      start=True, stop=True))
    ppr.wait(ve)
    vpr.inc(ve.tensor_copy(biasB, k.bank(0)[:, 0:32]))
    vpr.wait(te)
    vpr.wait(se)
    vpr.wait(sy)
    vpr.wait(ge)
    if ATT1_STAGE == 0:
        _les.close()
        return

    def emit_post(j):
        jp, sb, hq8 = j // 32, (j // 8) % 4, j % 8
        gi = hq8 // 4
        hq = jp * 8 + hq8
        P = j % 2
        eP = 16384 * P
        mP = 4096 * P
        nt = 8 * sb + 8
        aex.wait(te, v=at("aexe", j))
        vmk.wait(te, v=at("vmk", j))
        if j >= 1:
            vout.wait(te, v=at("vout", j - 1))
        ld.wait(te, v=at("ldv", j // 32))
        last = None
        for jj in range(nt):
            own = jj < nt // 2
            c65 = jj if own else 16 + (jj - nt // 2)
            near = (jj >= nt // 2 - 4 and own) or (jj >= nt - 4)
            if near:
                m = (jj - (nt // 2 - 4)) if own else (4 + jj - (nt - 4))
                rhs = MB[:, mP + m * 512: mP + (m + 1) * 512]
            else:
                rhs = E1[:, eP + jj * 512: eP + (jj + 1) * 512]
            last = te.matmul(k.bank(6)[0:65, :],
                             V65[:, gi * 2080 + c65 * 65: gi * 2080 + (c65 + 1) * 65],
                             rhs, start=(jj == 0), stop=(jj == nt - 1))
        pav.inc(last)
        put("pav", j, pav)
        pav.wait(ve, v=at("pav", j))
        if j >= 1:
            pb3.wait(ve, v=at("pb3", j - 1))
        k.chain("ve", ve.tensor_copy(s3r, k.ps[64:65, 512 * 6:512 * 7]))
        k.chain("ve", ve.reciprocal(r3r, s3r))
        vs3.inc(ve.tensor_copy(b3r, r3r))
        put("vs3", j, vs3)
        vs3.wait(te, v=at("vs3", j))
        if j >= 1:
            aev.wait(te, v=at("aevR3", j - 1))
        pb3.inc(te.matmul(k.bank(7, rows=64), k.ones_row[0:1, 0:64], b3r,
                          start=True, stop=True))
        put("pb3", j, pb3)
        pb3.wait(se, v=at("pb3", j))
        aev.inc(se.copy(Rs[0:64, :], k.bank(7, rows=64)))
        put("aevR3", j, aev)
        aev.wait(ve, v=at("aevR3", j))
        if j >= 1:
            sst.wait(ve, v=at("sst", j - 1))
        vout.inc(ve.tensor_tensor(ob[0:64, :], k.bank(6)[0:64, :], Rs[0:64, :],
                                  ALU.mult))
        put("vout", j, vout)
        vout.wait(sy, v=at("vout", j))
        sst.inc(sy.dma_start(
            out=a1_ap[hq * 64:(hq + 1) * 64, sb * 512:(sb + 1) * 512],
            in_=ob[0:64, :]), 16)
        put("sst", j, sst)

    NIT = min(128, ATT1_IT_LIMIT) if ATT1_IT_LIMIT else 128
    for i in range(NIT):
        jp, sb, hq8 = i // 32, (i // 8) % 4, i % 8
        gi = hq8 // 4
        hq = jp * 8 + hq8
        P = i % 2
        eP = 16384 * P
        mP = 4096 * P
        nt = 8 * sb + 8
        if sb == 0 and hq8 == 0:
            if jp > 0:
                pmm.wait(sy, v=at("pmm", i - 1))
            ld.inc(sy.dma_start(out=Kp[:],
                                in_=k1_ap[jp * 128:(jp + 1) * 128, :]), 16)
        if i >= 2:
            pmm.wait(sy, v=at("pmm", i - 1))
        lq.inc(sy.dma_start(
            out=qt[gi * 64:(gi + 1) * 64, P * 512:(P + 1) * 512],
            in_=q1_ap[hq * 64:(hq + 1) * 64, sb * 512:(sb + 1) * 512]), 16)

        ld.wait(te)
        lq.wait(te)
        nun = nt // 2
        gu0 = snap.get(("gun", i - 1), 0)
        for u in range(nun):
            G = gu0 + u
            reg = G % 3
            wv_ = G - 2
            if wv_ > 0:
                aex.wait(te, v=wv_)
            last = None
            for tj in range(2):
                jj = u * 2 + tj
                own = jj < nt // 2
                col = jj * 128 if own else 2048 + (jj - nt // 2) * 128
                last = te.matmul(
                    k.bank(2 * reg + tj),
                    Kp[gi * 64:(gi + 1) * 64, col:col + 128],
                    qt[gi * 64:(gi + 1) * 64, P * 512:(P + 1) * 512],
                    start=True, stop=True)
            pmm.inc(last)
        put("pmm", i, pmm)
        snap[("gun", i)] = gu0 + nun

        for u in range(nun):
            G = gu0 + u
            reg = G % 3
            pmm.wait(se, v=at("pmm", i) - nun + u + 1)
            if i >= 2 and u == 0:
                if ("pav", i - 2) in snap:
                    pav.wait(se, v=at("pav", i - 2))
                vmk.wait(se, v=at("vmk", i - 2))
            aex.inc(se.activation(
                E1[:, eP + u * 1024: eP + (u + 1) * 1024],
                k.bank(2 * reg, nb=2), AF.Exp,
                bias=biasB[:, hq:hq + 1], scale=float(1.0 / 8.0)))
        put("aexe", i, aex)

        if i >= 1 and ATT1_STAGE >= 2:
            emit_post(i - 1)
        if sb == 0 and hq8 == 0:
            if jp > 0 and ("pav", i - 1) in snap:
                pav.wait(ve, v=at("pav", i - 1))
            vms.inc(ve.memset(V65[:], 1.0))
            vms.wait(sy)
            for gi2 in range(2):
                g = jp * 2 + gi2
                dst = V65[:, gi2 * 2080:(gi2 + 1) * 2080].rearrange(
                    "p (c d) -> p c d", d=65)[:, :, 0:64]
                ld.inc(sy.dma_start(out=dst, in_=v1_ap[g * 128:(g + 1) * 128, :]
                                    .rearrange("p (c d) -> p c d", d=64)), 16)
            put("ldv", jp, ld)

        aex.wait(ve, v=at("aexe", i))
        if i >= 2 and ("pav", i - 2) in snap:
            pav.wait(ve, v=at("pav", i - 2))
        for m in range(8):
            own = m < 4
            jj = (nt // 2 - 4 + m) if own else (nt - 8 + m)
            last = ve.tensor_tensor(
                MB[:, mP + m * 512: mP + (m + 1) * 512],
                E1[:, eP + jj * 512: eP + (jj + 1) * 512],
                CM[:, (sb * 8 + m) * 512:(sb * 8 + m + 1) * 512], ALU.mult)
            if m < 7:
                k.chain("ve", last)
        vmk.inc(last)
        put("vmk", i, vmk)

    if ATT1_STAGE >= 2:
        emit_post(NIT - 1)
    sst.wait(sy)
    _les.close()


def emit_ffn1(k: KB, tag, xn_ap, rt_ap, wsg_ap, wgu_ap, wsd_ap, wegu_ap,
              wed_ap, id_ap, g1p_ap, w1fm_ap):
    nc, te, ve, se, sy, ge = k.nc, k.te, k.ve, k.se, k.sy, k.ge
    wl = k.sem(f"{tag}_wl")
    pl = k.sem(f"{tag}_pl")
    prt = k.sem(f"{tag}_prt")
    vrt = k.sem(f"{tag}_vrt")
    art = k.sem(f"{tag}_art")
    ptr = k.sem(f"{tag}_ptr")
    aev = k.sem(f"{tag}_aev")
    psg = k.sem(f"{tag}_psg")
    pgu = k.sem(f"{tag}_pgu")
    ahg = k.sem(f"{tag}_ahg")
    vh = k.sem(f"{tag}_vh")
    pdn = k.sem(f"{tag}_pdn")
    adn = k.sem(f"{tag}_adn")
    sst = k.sem(f"{tag}_sst")
    rdm = k.sem(f"{tag}_rdm")
    lfm = k.sem(f"{tag}_lfm")
    vsg = k.sem(f"{tag}_vsg")
    asg = k.sem(f"{tag}_asg")
    snap = {}
    from contextlib import ExitStack as _ES
    _les = _ES()

    def _sb(name, shape, dt):
        return _les.enter_context(nc.sbuf_tensor(name, shape, dt))

    xp = _sb(f"{tag}_xn", [128, NCH * 2048], BF16)
    hbuf = _sb(f"{tag}_h", [128, 45056], BF16)
    wg = _sb(f"{tag}_wg", [128, 2 * NCH * 128], BF16)
    wu = _sb(f"{tag}_wu", [128, 2 * NCH * 128], BF16)
    wd = _sb(f"{tag}_wd", [128, 2 * 5632], BF16)
    rt_sb = _sb(f"{tag}_rt", [128, NCH * 8], BF16)
    wsg_sb = _sb(f"{tag}_ws", [128, NCH], BF16)
    id_sb = _sb(f"{tag}_id", [128, 128], BF16)
    wfm = _sb(f"{tag}_wfm", [8, 128], BF16)
    sgB = _sb(f"{tag}_sgB", [128, 2048], BF16)
    ceB = sgB
    scr = _sb(f"{tag}_scr", [128, 2048], BF16)
    r1 = _sb(f"{tag}_r1", [128, 64], F32)[:]
    rb1 = _sb(f"{tag}_rb1", [128, 8], BF16)[:]
    srow = _sb(f"{tag}_srow", [1, 512], F32)[:]
    srb = _sb(f"{tag}_srb", [1, 512], BF16)[:]
    evf = _sb(f"{tag}_evf", [128, 2 * 512], F32)

    for fc in range(NCH):
        pl.inc(sy.dma_start(out=xp[:, fc * 2048:(fc + 1) * 2048],
                            in_=xn_ap[fc * 128:(fc + 1) * 128, 0:2048]), 16)
    wl.inc(ge.dma_start(out=rt_sb[:].rearrange("p (c f) -> p c f", c=NCH),
                        in_=rt_ap.rearrange("(c p) f -> p c f", p=128)), 16)
    wl.inc(ge.dma_start(out=wsg_sb[:], in_=wsg_ap), 16)
    wl.inc(ge.dma_start(out=id_sb[:], in_=id_ap), 16)
    wl.wait(te)
    pl.wait(te)

    for tt in range(16):
        if tt > 0:
            vrt.wait(te, v=snap[("vrt", tt - 1)])
            art.wait(te, v=snap[("art", tt - 1)])
        last = None
        for fc in range(NCH):
            last = te.matmul(k.bank(0)[:, 0:8],
                             xp[:, fc * 2048 + tt * 128:
                                fc * 2048 + (tt + 1) * 128],
                             rt_sb[:, fc * 8:(fc + 1) * 8],
                             start=(fc == 0), stop=(fc == NCH - 1))
        prt.inc(last)
        prt.wait(ve)
        e_t = r1[:, 0:8]
        probs = r1[:, 8:16]
        eq1 = r1[:, 16:24]
        pm = r1[:, 24:32]
        eq2 = r1[:, 32:40]
        eq1s = r1[:, 56:64]
        mx = r1[:, 48:49]
        mxn = r1[:, 49:50]
        sm = r1[:, 50:51]
        rs = r1[:, 51:52]
        m1 = r1[:, 52:53]
        m2 = r1[:, 53:54]
        k.chain("ve", ve.reduce_max(mx, k.bank(0)[:, 0:8], axis=AXX))
        vrt.inc(ve.tensor_scalar(mxn, mx, -1.0, None, op0=ALU.mult))
        vrt.wait(se)
        art.inc(se.activation(e_t, k.bank(0)[:, 0:8], AF.Exp, bias=mxn,
                              accum_out=sm))
        snap[("art", tt)] = art.v
        art.wait(ve)
        k.chain("ve", ve.reciprocal(rs, sm))
        k.chain("ve", ve.tensor_scalar(probs, e_t, rs, None, op0=ALU.mult))
        k.chain("ve", ve.reduce_max(m1, probs, axis=AXX))
        k.chain("ve", ve.tensor_scalar(eq1, probs, m1, None, op0=ALU.is_equal))
        k.chain("ve", ve.tensor_scalar(eq1s, eq1, -2.0, None, op0=ALU.mult))
        k.chain("ve", ve.tensor_tensor(pm, probs, eq1s, ALU.add))
        k.chain("ve", ve.reduce_max(m2, pm, axis=AXX))
        k.chain("ve", ve.tensor_scalar(eq2, probs, m2, None, op0=ALU.is_equal))
        k.chain("ve", ve.tensor_tensor(pm, eq1, eq2, ALU.add))
        vrt.inc(ve.tensor_tensor(rb1, probs, pm, ALU.mult))
        snap[("vrt", tt)] = vrt.v
        vrt.wait(te)
        if tt > 0:
            aev.wait(te, v=snap[("aevt", tt - 1)])
        ptr.inc(te.matmul(k.bank(1, rows=8)[:, 0:128], rb1, id_sb[:],
                          start=True, stop=True))
        ptr.wait(se)
        if tt > 0:
            sst.wait(se, v=snap[("wfmst", tt - 1)])
        aev.inc(se.copy(wfm[:], k.bank(1, rows=8)[:, 0:128]))
        snap[("aevt", tt)] = aev.v
        aev.wait(sy)
        sst.inc(sy.dma_start(out=w1fm_ap[:, tt * 128:(tt + 1) * 128],
                             in_=wfm[:]), 16)
        snap[("wfmst", tt)] = sst.v

    pbj = k.sem(f"{tag}_pbj")
    for j in range(4):
        if j > 0:
            vsg.wait(te, v=snap[("vsgj", j - 1)])
        last = None
        for fc in range(NCH):
            last = te.matmul(k.bank(2, rows=1), wsg_sb[:, fc:fc + 1],
                             xp[:, fc * 2048 + j * 512:fc * 2048 + j * 512 + 512],
                             start=(fc == 0), stop=(fc == NCH - 1))
        psg.inc(last)
        psg.wait(ve)
        if j > 0:
            asg.wait(ve, v=snap[("asgj", j - 1)])
        vsg.inc(ve.tensor_copy(srow, k.bank(2, rows=1)))
        snap[("vsgj", j)] = vsg.v
        vsg.wait(se)
        if j > 0:
            pbj.wait(se, v=snap[("pbjj", j - 1)])
        asg.inc(se.activation(srb, srow, AF.Sigmoid))
        snap[("asgj", j)] = asg.v
        asg.wait(te)
        if j > 0:
            aev.wait(te, v=snap[("aevj", j - 1)])
        pbj.inc(te.matmul(k.bank(3), k.ones_row[:], srb,
                          start=True, stop=True))
        snap[("pbjj", j)] = pbj.v
        pbj.wait(se)
        aev.inc(se.copy(sgB[:, j * 512:(j + 1) * 512], k.bank(3)))
        snap[("aevj", j)] = aev.v
    aev.wait(ve)
    aev.wait(te)

    state = {"git": 0, "dit": 0}

    def mlp_half(gu_ap, up_off, nip, dn_ap, scale_sb, r, accum, ekey):
        git0 = state["git"]
        tsl = slice(r * 1024, (r + 1) * 1024)
        for ip in range(nip):
            git = state["git"]
            wbuf = git % 2
            if git >= 2:
                pgu.wait(ge, v=snap[("pgu", git - 2)])
            for wdst, col0 in ((wg, ip * 128), (wu, up_off + ip * 128)):
                src = gu_ap[:, col0:col0 + 128]
                wl.inc(ge.dma_start(
                    out=wdst[:, wbuf * NCH * 128:(wbuf + 1) * NCH * 128]
                    .rearrange("p (c f) -> p c f", c=NCH),
                    in_=src.rearrange("(c p) f -> p c f", p=128)), 16)
            wseen = wl.v
            wl.wait(te, v=wseen)
            gb = (git % 2) * 4
            if git >= 2:
                vh.wait(te, v=snap[("vh", git - 2)])
                ahg.wait(te, v=snap[("ahg", git - 2)])
            if ip == 0 and ("hwar", ekey) in snap:
                pdn.wait(ve, v=snap[("hwar", ekey)])
            for bank0, wsrc in ((gb, wg), (gb + 2, wu)):
                last = None
                for th in range(2):
                    for fc in range(NCH):
                        last = te.matmul(
                            k.bank(bank0 + th),
                            wsrc[:, wbuf * NCH * 128 + fc * 128:
                                 wbuf * NCH * 128 + (fc + 1) * 128],
                            xp[:, fc * 2048 + r * 1024 + th * 512:
                               fc * 2048 + r * 1024 + (th + 1) * 512],
                            start=(fc == 0), stop=(fc == NCH - 1))
                pgu.inc(last)
            snap[("pgu", git)] = pgu.v
            pgu.wait(se, v=snap[("pgu", git)])
            hsl = slice(ip * 1024, (ip + 1) * 1024)
            if ("chm", git - 1) in snap:
                k.sem("chain_ve").wait(se, v=snap[("chm", git - 1)])
            ahg.inc(se.activation(scr[:, 0:1024], k.bank(gb, nb=2), AF.Silu))
            snap[("ahg", git)] = ahg.v
            ahg.wait(ve)
            k.chain("ve", ve.tensor_tensor(scr[:, 1024:2048], scr[:, 0:1024],
                                           k.bank(gb + 2, nb=2), ALU.mult))
            snap[("chm", git)] = k.sem("chain_ve").v
            vh.inc(ve.tensor_tensor(hbuf[:, hsl], scr[:, 1024:2048],
                                    scale_sb[:, tsl], ALU.mult))
            snap[("vh", git)] = vh.v
            state["git"] = git + 1
        vh.wait(te)

        def wd_load(oc):
            dwb = oc % 2
            if ("wdw", dwb) in snap:
                pdn.wait(ge, v=snap[("wdw", dwb)])
            wl.inc(ge.dma_start(
                out=wd[:, dwb * 5632: dwb * 5632 + nip * 128]
                .rearrange("p (c f) -> p c f", c=nip),
                in_=dn_ap[:, oc * 128:(oc + 1) * 128]
                .rearrange("(c p) f -> p c f", p=128)), 16)
            snap[("wds", oc)] = wl.v

        wd_load(0)
        for oc in range(NCH):
            dwb = oc % 2
            if oc + 1 < NCH:
                wd_load(oc + 1)
            wl.wait(te, v=snap[("wds", oc)])
            for th in range(2):
                dit = state["dit"]
                bank = 4 + dit % 2
                if dit >= 2:
                    adn.wait(te, v=snap[("adn", dit - 2)])
                last = None
                for ip in range(nip):
                    last = te.matmul(
                        k.bank(bank),
                        wd[:, dwb * 5632 + ip * 128:dwb * 5632 + (ip + 1) * 128],
                        hbuf[:, ip * 1024 + th * 512: ip * 1024 + th * 512 + 512],
                        start=(ip == 0), stop=(ip == nip - 1))
                pdn.inc(last)
                snap[("wdw", dwb)] = pdn.v
                pdn.wait(se, v=pdn.v)
                if dit >= 2:
                    sst.wait(se, v=snap[("dst", dit - 2)])
                    rdm.wait(se, v=snap[("drm", dit - 2)])
                adn.inc(se.copy(evf[:, (dit % 2) * 512:(dit % 2 + 1) * 512],
                                k.bank(bank)))
                snap[("adn", dit)] = adn.v
                reg = (oc, r * 2 + th)
                dst = g1p_ap[oc * 128:(oc + 1) * 128,
                             r * 1024 + th * 512: r * 1024 + (th + 1) * 512]
                srcv = evf[:, (dit % 2) * 512:(dit % 2 + 1) * 512]
                if accum:
                    adn.wait(ge, v=snap[("adn", dit)])
                    if ("g1w", reg) in snap:
                        rdm.wait(ge, v=snap[("g1w", reg)])
                    rdm.inc(ge.dma_start(out=dst, in_=srcv, accum_op=ALU.add), 16)
                    snap[("g1w", reg)] = rdm.v
                    snap[("drm", dit)] = rdm.v
                    snap[("dst", dit)] = sst.v
                else:
                    adn.wait(sy, v=snap[("adn", dit)])
                    sst.inc(sy.dma_start(out=dst, in_=srcv), 16)
                    snap[("dst", dit)] = sst.v
                    snap[("drm", dit)] = rdm.v
                state["dit"] = dit + 1
        snap[("hwar", ekey + 1)] = pdn.v

    snap[("hwar", 0)] = 0
    ek = 0
    for r in range(2):
        mlp_half(wgu_ap, 5632, 44, wsd_ap, sgB, r, False, ek)
        ek += 1
    sst.wait(ge)
    for e in range(8):
        vh.wait(te)
        aev.wait(te)
        for j in range(4):
            pbj.wait(sy)
            lfm.inc(sy.dma_start(out=srb, in_=w1fm_ap[e:e + 1,
                                                      j * 512:(j + 1) * 512]), 16)
            lfm.wait(te)
            pbj.inc(te.matmul(k.bank(3), k.ones_row[:], srb,
                              start=True, stop=True))
            pbj.wait(se)
            aev.inc(se.copy(ceB[:, j * 512:(j + 1) * 512], k.bank(3)))
            aev.wait(te)
        aev.wait(ve)
        for r in range(2):
            mlp_half(wegu_ap[e * 2048:(e + 1) * 2048, :], 1408, 11,
                     wed_ap[e * 1408:(e + 1) * 1408, :], ceB, r, True, ek)
            ek += 1
    rdm.wait(ge)
    sst.wait(sy)
    _les.close()


def _host_perm(parity):
    own = np.arange(parity, NTL, 2)
    other = np.arange(1 - parity, NTL, 2)
    tiles = np.concatenate([own, other])
    return (tiles[:, None] * 128 + np.arange(128)[None, :]).reshape(-1)


def build_program(dbg=(), phases=None):
    k = KB(dbg=dbg, phases=phases)
    k.constf(EPS_LN, rows=1)
    k.constf(EPS_RMS, rows=1)
    k.wait_init()
    x0 = k.din("x0", [D, NT], F32)
    if k.want("ln0"):
        xn = k.dram("xn0", [D, NT], BF16)
        emit_ln(k, "ln0", x0, xn)
        k.barrier()
    if k.want("qkv0"):
        xn = k.dram("xn0", [D, NT], BF16)
        wq = k.din("wq0", [D, 4096], F32)
        wk = k.din("wk0", [D, 4096], F32)
        wv = k.din("wv0", [D, D], F32)
        q0 = k.dram("q0", [4096, NT], BF16)
        k0 = k.dram("k0", [4096, NT], BF16)
        v0 = k.dram("v0", [D, NT], BF16)
        bnd = k.dram("bnd0", [1, 512], F32)
        emit_qkv0(k, "qkv0", xn, wq, wk, wv, q0, k0, v0, bnd)
        k.barrier()
    if k.want("att0"):
        q0 = k.dram("q0", [4096, NT], BF16)
        k0 = k.dram("k0", [4096, NT], BF16)
        v0 = k.dram("v0", [D, NT], BF16)
        bnd = k.dram("bnd0", [1, 512], F32)
        lam = k.din("lam0", [1, 16], F32)
        s0t = k.dram("s0t", [D, NT], BF16)
        factor = k.dram("factor0", [1, NT], F32)
        rms0 = k.dram("rms0", [1, NT], F32)
        emit_att0(k, "att0", q0, k0, v0, bnd, lam, s0t, factor, rms0)
        k.barrier()
    if k.want("wo0"):
        s0t = k.dram("s0t", [D, NT], BF16)
        factor = k.dram("factor0", [1, NT], F32)
        wo = k.din("wo0w", [D, D], F32)
        ns = k.din("ns0", [128, 16], F32)
        s0 = k.dram("s0", [D, NT], BF16)
        emit_wo0(k, "wo0", s0t, factor, wo, ns, s0)
        k.barrier()
    if k.want("ffn0"):
        xn = k.dram("xn0", [D, NT], BF16)
        wT = k.din("w0ffnT", [D, D], F32)
        aL = k.din("a0l", [D, 128], F32)
        bL = k.din("b0l", [128, D], F32)
        rt = k.din("rt0", [D, 8], F32)
        rep = k.din("rep", [8, 128], F32)
        ident = k.din("ident", [128, 128], F32)
        g0 = k.dram("g0", [D, NT], BF16)
        emit_ffn0(k, "ffn0", xn, wT, aL, bL, rt, rep, ident, g0)
        k.barrier()
    if k.want("cmb0"):
        x0_ = k.din("x0", [D, NT], F32)
        s0 = k.dram("s0", [D, NT], BF16)
        g0 = k.dram("g0", [D, NT], BF16)
        x1 = k.dram("x1", [D, NT], F32)
        emit_combine(k, "cmb0", x0_, s0, g0, x1)
        k.barrier()
    if k.want("ln1"):
        x1 = k.dram("x1", [D, NT], F32)
        xn1 = k.dram("xn1", [D, NT], BF16)
        emit_ln(k, "ln1", x1, xn1)
        k.barrier()
    if k.want("qkv1"):
        xn1 = k.dram("xn1", [D, NT], BF16)
        wq1 = k.din("wq1T", [D, D], F32)
        wk1 = k.din("wk1T", [D, 512], F32)
        wv1 = k.din("wv1T", [D, 512], F32)
        cosT = k.din("cos1", [128, NT], F32)
        sinT = k.din("sin1", [128, NT], F32)
        pswi = k.din("psw", [128, 128], F32)
        q1 = k.dram("q1", [D, NO], BF16)
        k1 = k.dram("k1", [512, NT], BF16)
        v1 = k.dram("v1", [1024, 2048], BF16)
        bnd1 = k.dram("bnd1", [2, 512], F32)
        emit_qkv1(k, "qkv1", xn1, wq1, wk1, wv1, cosT, sinT, pswi,
                  q1, k1, v1, bnd1)
        k.barrier()
    if k.want("att1"):
        q1 = k.dram("q1", [D, NO], BF16)
        k1 = k.dram("k1", [512, NT], BF16)
        v1 = k.dram("v1", [1024, 2048], BF16)
        bnd1 = k.dram("bnd1", [2, 512], F32)
        cm = k.din("cmask", [128, 4 * 8 * 512], F32)
        a1 = k.dram("a1", [D, NO], BF16)
        emit_att1(k, "att1", q1, k1, v1, bnd1, cm, a1)
        k.barrier()
    if k.want("wo1"):
        a1 = k.dram("a1", [D, NO], BF16)
        wo1 = k.din("wo1T", [D, D], F32)
        s1 = k.dram("s1", [D, NO], BF16)
        emit_wo0(k, "wo1", a1, None, wo1, None, s1, n_tok=NO,
                 use_ns=False, use_factor=False)
        k.barrier()
    if k.want("ffn1"):
        xn1 = k.dram("xn1", [D, NT], BF16)
        rt1 = k.din("rt1T", [D, 8], F32)
        wsg1 = k.din("wsg1T", [128, NCH], F32)
        wgu1 = k.din("wgu1", [D, 11264], F32)
        wsd1 = k.din("wsd1", [5632, D], F32)
        wegu1 = k.din("wegu1", [8 * D, 2816], F32)
        wed1 = k.din("wed1", [8 * 1408, D], F32)
        ident = k.din("ident", [128, 128], F32)
        g1p = k.dram("g1p", [D, NO], F32)
        w1fm = k.dram("w1fm", [8, 2048], BF16)
        emit_ffn1(k, "ffn1", xn1, rt1, wsg1, wgu1, wsd1, wegu1, wed1, ident,
                  g1p, w1fm)
        k.barrier()
    if k.want("cmb1"):
        x1 = k.dram("x1", [D, NT], F32)
        s1 = k.dram("s1", [D, NO], BF16)
        g1p = k.dram("g1p", [D, NO], F32)
        x2 = k.dout("out", [D, NO], F32)
        emit_combine(k, "cmb1", x1[:, 0:NO], s1, g1p, x2, n_tok=NO,
                     g_silu=True)
        k.barrier()
    return k


def _stage_weights(inputs):
    f32 = np.float32
    A = np.asarray(inputs["L0_ffn_lora_A"], f32)
    B = np.asarray(inputs["L0_ffn_lora_B"], f32)
    rep = np.zeros((8, 128), f32)
    for e in range(8):
        rep[e, e * 16:(e + 1) * 16] = 1.0
    psw = np.zeros((128, 128), f32)
    psw[np.arange(128), np.arange(128) ^ 1] = 1.0
    return {
        "wq0": np.asarray(inputs["L0_attn_Wq"], f32),
        "wk0": np.asarray(inputs["L0_attn_Wk"], f32),
        "wv0": np.asarray(inputs["L0_attn_Wv"], f32),
        "wo0w": np.asarray(inputs["L0_attn_Wo"], f32),
        "ns0": np.ascontiguousarray(
            np.asarray(inputs["L0_attn_norm_scale"], f32).reshape(16, 128).T),
        "lam0": np.asarray(inputs["L0_attn_lam"], f32).reshape(1, 16),
        "w0ffnT": np.ascontiguousarray(np.asarray(inputs["L0_ffn_weight"], f32).T),
        "a0l": np.ascontiguousarray(np.transpose(A, (2, 0, 1)).reshape(2048, 128)),
        "b0l": np.ascontiguousarray(np.transpose(B, (0, 2, 1)).reshape(128, 2048)),
        "rt0": np.ascontiguousarray(np.asarray(inputs["L0_ffn_router_w"], f32).T),
        "rep": rep,
        "ident": np.eye(128, dtype=f32),
        "psw": psw,
        "wq1T": np.ascontiguousarray(np.asarray(inputs["L1_attn_wq"], f32).T),
        "wk1T": np.ascontiguousarray(np.asarray(inputs["L1_attn_wk"], f32).T),
        "wv1T": np.ascontiguousarray(np.asarray(inputs["L1_attn_wv"], f32).T),
        "wo1T": np.ascontiguousarray(np.asarray(inputs["L1_attn_wo"], f32).T),
        "rt1T": np.ascontiguousarray(np.asarray(inputs["L1_ffn_W_router"], f32).T),
        "wsg1T": np.ascontiguousarray(
            np.asarray(inputs["L1_ffn_W_sgate"], f32).reshape(16, 128).T),
        "wgu1": np.asarray(inputs["L1_ffn_Ws_gate_up"], f32),
        "wsd1": np.asarray(inputs["L1_ffn_Ws_down"], f32),
        "wegu1": np.asarray(inputs["L1_ffn_We_gate_up"], f32).reshape(8 * 2048, 2816),
        "wed1": np.asarray(inputs["L1_ffn_We_down"], f32).reshape(8 * 1408, 2048),
    }


def _stage_percore(inputs, core):
    f32 = np.float32
    p, parity = core // 2, core % 2
    perm = _host_perm(parity)
    x = np.asarray(inputs["L0_attn_x"][p], f32)
    st = {"x0": np.ascontiguousarray(x.T[:, perm])}
    pos = perm.astype(np.float64)
    r = np.arange(128)
    theta = 10000.0 ** (-(2.0 * ((r % 64) // 2)) / 64.0)
    ang = pos[None, :] * theta[:, None]
    st["cos1"] = np.cos(ang).astype(f32)
    sgn = np.where(r % 2 == 0, -1.0, 1.0)
    st["sin1"] = (sgn[:, None] * np.sin(ang)).astype(f32)
    own_t = np.arange(parity, 32, 2)
    oth_t = np.arange(1 - parity, 32, 2)
    cm = np.zeros((4, 8, 128, 512), f32)
    for sb in range(4):
        for m in range(8):
            own = m < 4
            j = 4 * sb + (m if own else m - 4)
            keyg = (own_t if own else oth_t)[j]
            kpos = keyg * 128 + np.arange(128)
            for c in range(4):
                qg = own_t[4 * sb + c]
                qpos = qg * 128 + np.arange(128)
                cm[sb, m, :, c * 128:(c + 1) * 128] = (
                    kpos[:, None] <= qpos[None, :]).astype(f32)
    st["cmask"] = np.ascontiguousarray(
        cm.reshape(32, 128, 512).transpose(1, 0, 2).reshape(128, 32 * 512))
    return st, perm


def kernel(**inputs):
    k = build_program()
    names = set(k.inputs)
    w = _stage_weights(inputs)
    in_maps = []
    perms = []
    for core in range(8):
        st, perm = _stage_percore(inputs, core)
        st.update(w)
        in_maps.append({n: st[n] for n in names})
        perms.append(perm)
    res = run_bass_kernel_spmd(k.nc, in_maps, core_ids=list(range(8)))
    k.close()
    out = np.zeros((4, NT, D), np.float32)
    for core in range(8):
        p = core // 2
        o = np.asarray(res.results[core]["out"], np.float32)
        out[p, perms[core][:NO], :] = o.T
    return out


# version 10 (final)
# speedup vs baseline: 1.2189x; 1.0259x over previous
import numpy as np
import ml_dtypes
from contextlib import ExitStack

import concourse.bass as bass
import concourse.mybir as mybir
from concourse.bass_utils import run_bass_kernel_spmd

F32 = mybir.dt.float32
BF16 = mybir.dt.bfloat16
AF = mybir.ActivationFunctionType
ALU = mybir.AluOpType
AXX = mybir.AxisListType.X

D = 2048
NT = 4096
NO = 2048
NCH = D // 128
NTL = NT // 128
EPS_LN = 1e-5
EPS_RMS = 1e-6

SAFE_CHAIN = True
ATT0_IT_LIMIT = None
ATT0_NO_POST = False
FFN0_STAGE = 3
ATT1_IT_LIMIT = None
ATT1_STAGE = 2


class PhysSem:
    def __init__(self, nc, es, name):
        self.h = es.enter_context(nc.semaphore(name))
        self.total = 0


class Sem:

    def __init__(self, phys):
        self.p = phys
        self.base = phys.total
        self.v = 0

    def inc(self, instr, n=1):
        instr.then_inc(self.p.h, n)
        self.p.total += n
        self.v += n
        return instr

    def wait(self, eng, v=None):
        eng.wait_ge(self.p.h, self.base + (self.v if v is None else v))


class KB:

    def __init__(self, dbg=(), phases=None):
        self.es = ExitStack()
        self.nc = bass.Bass("TRN2", target_bir_lowering=False)
        self.dbg = set(dbg)
        self.phases = phases
        nc = self.nc
        self.te, self.ve, self.se = nc.tensor, nc.vector, nc.scalar
        self.ge, self.sy = nc.gpsimd, nc.sync
        self.sems = {}
        self._phys = {}
        self.drams = {}
        self.inputs = set()
        self._chain_n = 0
        self.ps = self.es.enter_context(nc.psum_tensor("ps_arena", [128, 4096], F32))
        self._consts = {}
        self._const_pool = self.es.enter_context(
            nc.sbuf_tensor("const_pool", [128, 64], F32))
        self._const_n = 0
        init = self.sem("init")
        self.ones_col = self.es.enter_context(nc.sbuf_tensor("ones_col", [128, 1], BF16))
        self.ones_row = self.es.enter_context(nc.sbuf_tensor("ones_row", [1, 128], BF16))
        self.ve.memset(self.ones_col[:], 1.0)
        init.inc(self.ve.memset(self.ones_row[:], 1.0))

    def constf(self, val, rows=128):
        key = (float(val), rows)
        if key not in self._consts:
            assert not getattr(self, "_init_done", False), \
                "register consts before wait_init()"
            col = self._const_n
            self._const_n += 1
            ap = self._const_pool[0:rows, col:col + 1]
            init = self.sem("init")
            init.inc(self.ve.memset(self._const_pool[:, col:col + 1], float(val)))
            self._consts[key] = ap
        return self._consts[key]

    def wait_init(self):
        self._init_done = True
        init = self.sem("init")
        for eng in (self.te, self.ve, self.se, self.ge, self.sy):
            init.wait(eng)

    def sem(self, name):
        if name in self.sems:
            return self.sems[name]
        if name.startswith("chain") or name == "init":
            role = name
        else:
            role = name.split("_", 1)[-1] if "_" in name else name
        if role not in self._phys:
            self._phys[role] = PhysSem(self.nc, self.es, role)
        self.sems[name] = Sem(self._phys[role])
        return self.sems[name]

    def bank(self, b, nb=1, rows=128, row0=0):
        return self.ps[row0:row0 + rows, 512 * b:512 * b + 512 * nb]

    def chain(self, eng_name, instr):
        if SAFE_CHAIN:
            s = self.sem(f"chain_{eng_name}")
            s.inc(instr)
            eng = getattr(self, eng_name)
            s.wait(eng)
        return instr

    def din(self, name, shape, dtype=F32):
        if name not in self.drams:
            self.inputs.add(name)
            self.drams[name] = self.nc.dram_tensor(
                name, list(shape), dtype, kind="ExternalInput").ap()
        return self.drams[name]

    def dout(self, name, shape, dtype=F32):
        if name not in self.drams:
            self.drams[name] = self.nc.dram_tensor(
                name, list(shape), dtype, kind="ExternalOutput").ap()
        return self.drams[name]

    def dram(self, name, shape, dtype=BF16):
        if name not in self.drams:
            if name in self.dbg:
                return self.dout(name, shape, dtype)
            self.drams[name] = self.nc.dram_tensor(
                name, list(shape), dtype, kind="Internal").ap()
        return self.drams[name]

    def want(self, phase):
        return self.phases is None or phase in self.phases

    def barrier(self):
        self.nc.all_engine_barrier()

    def close(self):
        self.es.close()


def emit_ln(k: KB, tag, x_ap, xn_ap):
    nc, te, ve, se, sy, ge = k.nc, k.te, k.ve, k.se, k.sy, k.ge
    ld = k.sem(f"{tag}_ld")
    va = k.sem(f"{tag}_va")
    sa = k.sem(f"{tag}_sa")
    pa = k.sem(f"{tag}_pa")
    ev = k.sem(f"{tag}_ev")
    bc = k.sem(f"{tag}_bc")
    ld2 = k.sem(f"{tag}_ld2")
    vb = k.sem(f"{tag}_vb")
    st = k.sem(f"{tag}_st")
    with (
        nc.sbuf_tensor(f"{tag}_xc", [128, 2 * NT], F32) as xc,
        nc.sbuf_tensor(f"{tag}_xb", [128, 2 * NT], BF16) as xb,
        nc.sbuf_tensor(f"{tag}_xq", [128, 2 * NT], BF16) as xq,
        nc.sbuf_tensor(f"{tag}_s0", [1, NT], F32) as sums_h,
        nc.sbuf_tensor(f"{tag}_s1", [1, NT], F32) as sumsq_h,
        nc.sbuf_tensor(f"{tag}_s2", [1, NT], F32) as mrow_h,
        nc.sbuf_tensor(f"{tag}_s3", [1, NT], F32) as v4_h,
        nc.sbuf_tensor(f"{tag}_mB", [128, NT], F32) as mB,
        nc.sbuf_tensor(f"{tag}_rB", [128, NT], F32) as rB,
        nc.sbuf_tensor(f"{tag}_t1", [128, NT], F32) as t1,
        nc.sbuf_tensor(f"{tag}_xnb", [128, 2 * NT], BF16) as xnb,
    ):
        sums, sumsq, mrow, v4 = sums_h[:], sumsq_h[:], mrow_h[:], v4_h[:]

        for c in range(NCH):
            buf = c % 2
            sl = slice(buf * NT, (buf + 1) * NT)
            if c >= 2:
                va.wait(sy, v=c - 1)
                sa.wait(sy, v=c - 1)
            ld.inc(sy.dma_start(out=xc[:, sl], in_=x_ap[c * 128:(c + 1) * 128, :]), 16)
            ld.wait(ve)
            if c >= 2:
                pa.wait(ve, v=c - 1)
            va.inc(ve.tensor_copy(xb[:, sl], xc[:, sl]))
            ld.wait(se)
            if c >= 2:
                pa.wait(se, v=c - 1)
            sa.inc(se.activation(xq[:, sl], xc[:, sl], AF.Square))
            va.wait(te)
            sa.wait(te)
            last = None
            for j in range(8):
                last = te.matmul(k.bank(j, rows=1, row0=0), k.ones_col[:],
                                 xb[:, buf * NT + j * 512: buf * NT + (j + 1) * 512],
                                 start=(c == 0), stop=(c == NCH - 1))
            for j in range(8):
                last = te.matmul(k.bank(j, rows=1, row0=32), k.ones_col[:],
                                 xq[:, buf * NT + j * 512: buf * NT + (j + 1) * 512],
                                 start=(c == 0), stop=(c == NCH - 1))
            pa.inc(last)

        pa.wait(ve)
        k.chain("ve", ve.tensor_copy(sums, k.ps[0:1, 0:4096]))
        k.chain("ve", ve.tensor_copy(sumsq, k.ps[32:33, 0:4096]))
        k.chain("ve", ve.tensor_scalar(mrow, sums, 1.0 / D, None, op0=ALU.mult))
        ev.inc(ve.tensor_scalar(v4, sumsq, 1.0 / D, None, op0=ALU.mult))
        ev.wait(se)
        sq = k.sem(f"{tag}_sq")
        sq.inc(se.activation(sums, mrow, AF.Square))
        sq.wait(ve)
        ev.inc(ve.tensor_tensor(sumsq, v4, sums, ALU.subtract))
        ev.wait(se)
        sq.inc(se.activation(v4, sumsq, AF.Sqrt, bias=k.constf(EPS_LN, rows=1)))
        sq.wait(ve)
        k.chain("ve", ve.reciprocal(sums, v4))
        mrow_bf = xb[0:1, 0:NT]
        rrow_bf = xb[0:1, NT:2 * NT]
        k.chain("ve", ve.tensor_copy(mrow_bf, mrow))
        ev.inc(ve.tensor_copy(rrow_bf, sums))
        evb = k.sem(f"{tag}_evb")
        ev.wait(te)
        for r in range(2):
            src = mrow_bf if r == 0 else rrow_bf
            dst = mB if r == 0 else rB
            if r == 1:
                evb.wait(te, v=8)
            mmw = k.sem(f"{tag}_bb")
            for j in range(8):
                mmw.inc(te.matmul(k.bank(j), k.ones_row[:],
                                  src[:, j * 512:(j + 1) * 512],
                                  start=True, stop=True))
            mmw.wait(ve)
            for j in range(8):
                evb.inc(ve.tensor_copy(dst[:, j * 512:(j + 1) * 512], k.bank(j)))
        evb.wait(sy)
        evb.wait(ve)
        for c in range(NCH):
            buf = c % 2
            sl = slice(buf * NT, (buf + 1) * NT)
            if c >= 2:
                vb.wait(sy, v=c - 1)
            ld2.inc(sy.dma_start(out=xc[:, sl], in_=x_ap[c * 128:(c + 1) * 128, :]), 16)
            ld2.wait(ve)
            if c >= 2:
                st.wait(ve, v=16 * (c - 1))
            k.chain("ve", ve.tensor_tensor(t1[:], xc[:, sl], mB[:], ALU.subtract))
            vb.inc(ve.tensor_tensor(xnb[:, sl], t1[:], rB[:], ALU.mult))
            vb.wait(sy)
            st.inc(sy.dma_start(out=xn_ap[c * 128:(c + 1) * 128, :], in_=xnb[:, sl]), 16)
        st.wait(sy)


def emit_qkv0(k: KB, tag, xn_ap, wq_ap, wk_ap, wv_ap, q0_ap, k0_ap, v0_ap,
              bnd_ap):
    nc, te, ve, se, sy, ge = k.nc, k.te, k.ve, k.se, k.sy, k.ge
    xl = k.sem(f"{tag}_xl")
    wl = k.sem(f"{tag}_wl")
    smm = k.sem(f"{tag}_mm")
    sev = k.sem(f"{tag}_ev")
    ssq = k.sem(f"{tag}_sq")
    snm = k.sem(f"{tag}_nm")
    smx = k.sem(f"{tag}_mx")
    sst = k.sem(f"{tag}_st")
    with (
        nc.sbuf_tensor(f"{tag}_xn", [128, NCH * NT], BF16) as xn_sb,
        nc.sbuf_tensor(f"{tag}_wb", [128, 2 * NCH * 512], BF16) as wb,
        nc.sbuf_tensor(f"{tag}_evt", [128, 4 * 512], BF16) as evt,
        nc.sbuf_tensor(f"{tag}_sqt", [128, 4 * 512], BF16) as sqt,
        nc.sbuf_tensor(f"{tag}_mxr", [1, 512], F32) as mxr,
    ):
        for c in range(NCH):
            xl.inc(sy.dma_start(out=xn_sb[:, c * NT:(c + 1) * NT],
                                in_=xn_ap[c * 128:(c + 1) * 128, :]), 16)
        xl.wait(te)

        it = 0
        nqk_blocks = [(wq_ap, q0_ap, fb) for fb in range(8)] + \
                     [(wk_ap, k0_ap, fb) for fb in range(8)]
        for bi, (w_ap, dst_ap, fb) in enumerate(nqk_blocks):
            wbuf = bi % 2
            if bi >= 2:
                smm.wait(ge, v=(bi - 1) * 32)
            src = w_ap[:, fb * 512:(fb + 1) * 512]
            src = src.rearrange("(dc p) f -> p dc f", p=128)
            dst3 = wb[:, wbuf * NCH * 512:(wbuf + 1) * NCH * 512].rearrange(
                "p (dc f) -> p dc f", dc=NCH)
            wl.inc(ge.dma_start(out=dst3, in_=src), 16)
            for fc in range(4):
                for tb in range(8):
                    bank = it % 4
                    ebuf = it % 4
                    wl.wait(te, v=(bi + 1) * 16)
                    if it >= 4:
                        sev.wait(te, v=it - 3)
                    last = None
                    for dc in range(NCH):
                        lhsT = wb[:, wbuf * NCH * 512 + dc * 512 + fc * 128:
                                  wbuf * NCH * 512 + dc * 512 + fc * 128 + 128]
                        rhs = xn_sb[:, dc * NT + tb * 512: dc * NT + (tb + 1) * 512]
                        last = te.matmul(k.bank(bank), lhsT, rhs,
                                         start=(dc == 0), stop=(dc == NCH - 1))
                    smm.inc(last)
                    smm.wait(se, v=it + 1)
                    if it >= 4:
                        sst.wait(se, v=16 * (it - 3))
                        ssq.wait(se, v=it - 3)
                    sev.inc(se.copy(evt[:, ebuf * 512:(ebuf + 1) * 512], k.bank(bank)))
                    sev.wait(ve, v=it + 1)
                    if it >= 4:
                        snm.wait(ve, v=it - 3)
                    ssq.inc(ve.tensor_tensor(sqt[:, ebuf * 512:(ebuf + 1) * 512],
                                             evt[:, ebuf * 512:(ebuf + 1) * 512],
                                             evt[:, ebuf * 512:(ebuf + 1) * 512],
                                             ALU.mult))
                    ssq.wait(te, v=it + 1)
                    if it >= 2:
                        smx.wait(te, v=it - 1)
                    snm.inc(te.matmul(k.bank(4 + it % 2, rows=1), k.ones_col[:],
                                      sqt[:, ebuf * 512:(ebuf + 1) * 512],
                                      start=True, stop=True))
                    snm.wait(ve, v=it + 1)
                    chunk_id = bi * 4 + fc
                    smx.inc(ve.reduce_max(mxr[:, chunk_id * 8 + tb:
                                              chunk_id * 8 + tb + 1],
                                          k.bank(4 + it % 2, rows=1), axis=AXX))
                    sev.wait(sy, v=it + 1)
                    sst.inc(sy.dma_start(
                        out=dst_ap[fb * 512 + fc * 128: fb * 512 + (fc + 1) * 128,
                                   tb * 512:(tb + 1) * 512],
                        in_=evt[:, ebuf * 512:(ebuf + 1) * 512]), 16)
                    it += 1
        smx.wait(sy)
        sst.inc(sy.dma_start(out=bnd_ap, in_=mxr[:]), 16)

        st_snap = {}
        for vb in range(4):
            wbuf = vb % 2
            if vb >= 2:
                smm.wait(ge, v=len(nqk_blocks) * 32 + (vb - 1) * 32)
            else:
                smm.wait(ge, v=len(nqk_blocks) * 32)
            src = wv_ap[:, vb * 512:(vb + 1) * 512]
            src = src.rearrange("(dc p) f -> p dc f", p=128)
            dst3 = wb[:, wbuf * NCH * 512:(wbuf + 1) * NCH * 512].rearrange(
                "p (dc f) -> p dc f", dc=NCH)
            wl.inc(ge.dma_start(out=dst3, in_=src), 16)
            for tt in range(NTL):
                vit = vb * NTL + tt
                bank = it % 4
                ebuf = it % 4
                wl.wait(te, v=(len(nqk_blocks) + vb + 1) * 16)
                if it >= 4:
                    sev.wait(te, v=it - 3)
                last = None
                for dc in range(NCH):
                    lhsT = xn_sb[:, dc * NT + tt * 128: dc * NT + (tt + 1) * 128]
                    rhs = wb[:, wbuf * NCH * 512 + dc * 512:
                             wbuf * NCH * 512 + (dc + 1) * 512]
                    last = te.matmul(k.bank(bank), lhsT, rhs,
                                     start=(dc == 0), stop=(dc == NCH - 1))
                smm.inc(last)
                smm.wait(se, v=it + 1)
                if vit >= 4:
                    sst.wait(se, v=st_snap[vit - 4])
                else:
                    sst.wait(se, v=512 * 16 + 16)
                sev.inc(se.copy(evt[:, ebuf * 512:(ebuf + 1) * 512], k.bank(bank)))
                sev.wait(sy, v=it + 1)
                for hh in range(4):
                    h = vb * 4 + hh
                    sst.inc(sy.dma_start(
                        out=v0_ap[h * 128:(h + 1) * 128, tt * 128:(tt + 1) * 128],
                        in_=evt[:, ebuf * 512 + hh * 128:
                                ebuf * 512 + (hh + 1) * 128]), 16)
                st_snap[vit] = sst.v
                it += 1
        sst.wait(sy)


def emit_att0(k: KB, tag, q0_ap, k0_ap, v0_ap, bnd_ap, lam_ap, s0t_ap,
              factor_ap, rms_ap):
    nc, te, ve, se, sy, ge = k.nc, k.te, k.ve, k.se, k.sy, k.ge
    NH, SB = 16, 8
    NIT = NH * SB
    ld = k.sem(f"{tag}_ld")
    lq = k.sem(f"{tag}_lq")
    pmm = k.sem(f"{tag}_pmm")
    aex = k.sem(f"{tag}_aex")
    psu = k.sem(f"{tag}_psu")
    vsr = k.sem(f"{tag}_vsr")
    pb12 = k.sem(f"{tag}_pb12")
    aev = k.sem(f"{tag}_aev")
    vdf = k.sem(f"{tag}_vdf")
    ae3 = k.sem(f"{tag}_ae3")
    pav = k.sem(f"{tag}_pav")
    ps3 = k.sem(f"{tag}_ps3")
    vs3 = k.sem(f"{tag}_vs3")
    pb3 = k.sem(f"{tag}_pb3")
    rdm = k.sem(f"{tag}_rdm")
    vout = k.sem(f"{tag}_vout")
    vsq = k.sem(f"{tag}_vsq")
    prm = k.sem(f"{tag}_prm")
    vrm = k.sem(f"{tag}_vrm")
    sst = k.sem(f"{tag}_sst")
    vpr = k.sem(f"{tag}_vpr")
    asr = k.sem(f"{tag}_asr")
    snap = {}

    def put(name, i, sem):
        snap[(name, i)] = sem.v

    def at(name, i):
        return snap[(name, i)]

    from contextlib import ExitStack as _ES
    _les = _ES()

    def _sb(name, shape, dt):
        return _les.enter_context(nc.sbuf_tensor(name, shape, dt))

    E1 = _sb(f"{tag}_E1", [128, 2 * 16384], BF16)
    E2 = _sb(f"{tag}_E2", [128, 2 * 16384], BF16)
    Wq = _sb(f"{tag}_Wq", [128, 4096], BF16)
    K1 = _sb(f"{tag}_K1", [128, NT], BF16)
    K2 = _sb(f"{tag}_K2", [128, NT], BF16)
    Vt = _sb(f"{tag}_V", [128, NT], BF16)
    qt = _sb(f"{tag}_q", [128, 4 * 512], BF16)
    Rs = _sb(f"{tag}_Rs", [128, 2 * 1536], BF16)
    ob = _sb(f"{tag}_ob", [128, 2 * 512], BF16)
    sq2b = _sb(f"{tag}_sq2", [128, 2 * 512], BF16)
    sr = _sb(f"{tag}_sr", [1, 512], F32)[:]
    rr = _sb(f"{tag}_rr", [1, 512], F32)[:]
    rb = _sb(f"{tag}_rb", [1, 2 * 1024], BF16)
    rb2f = _sb(f"{tag}_r2f", [1, 512], F32)[:]
    s3r = _sb(f"{tag}_s3", [1, 2 * 512], F32)
    b3r = _sb(f"{tag}_b3", [1, 2 * 512], BF16)
    st1 = _sb(f"{tag}_st1", [1, 512], F32)[:]
    pr = _sb(f"{tag}_pr", [1, 640], F32)[:]
    prb = _sb(f"{tag}_prb", [1, 64], BF16)[:]
    biasB = _sb(f"{tag}_bias", [128, 64], F32)[:]

    bnd = pr[:, 0:512]
    n2 = pr[:, 512:576]
    nrm = pr[:, 576:640]
    lq.inc(sy.dma_start(out=bnd, in_=bnd_ap), 16)
    lam_sb = biasB[0:1, 48:64]
    lq.inc(sy.dma_start(out=lam_sb, in_=lam_ap), 16)
    lq.wait(ve)
    for cid in range(64):
        k.chain("ve", ve.reduce_max(n2[:, cid:cid + 1],
                                    bnd[:, cid * 8:(cid + 1) * 8], axis=AXX))
    vpr.inc(ve.tensor_copy(nrm, n2))
    vpr.wait(se)
    aprep = k.sem(f"{tag}_aprep")
    aprep.inc(se.activation(nrm, n2, AF.Sqrt))
    aprep.wait(ve)
    k.chain("ve", ve.tensor_tensor(pr[:, 0:32], nrm[:, 0:32], nrm[:, 32:64],
                                   ALU.mult))
    k.chain("ve", ve.tensor_scalar(pr[:, 0:32], pr[:, 0:32],
                                   -1.0 / float(np.sqrt(128.0)), None,
                                   op0=ALU.mult))
    k.chain("ve", ve.tensor_scalar(pr[:, 64:80], lam_sb, -1.0, None,
                                   op0=ALU.mult))
    vpr.inc(ve.tensor_copy(pr[:, 80:96], pr[:, 64:80]))
    vpr.wait(se)
    aprep.inc(se.activation(pr[:, 64:80], pr[:, 80:96], AF.Relu))
    aprep.wait(ve)
    k.chain("ve", ve.tensor_scalar(pr[:, 32:48], pr[:, 64:80], -1.0, -1.0,
                                   op0=ALU.mult, op1=ALU.add))
    mlap = biasB[0:1, 62:63]
    k.chain("ve", ve.reduce_sum(pr[:, 96:97], lam_sb, axis=AXX))
    k.chain("ve", ve.tensor_scalar(mlap, pr[:, 96:97], -1.0 / 16.0,
                                   1.0, op0=ALU.mult, op1=ALU.add))
    vpr.inc(ve.tensor_copy(prb[:, 0:48], pr[:, 0:48]))
    vpr.wait(te)
    pprep = k.sem(f"{tag}_pprep")
    pprep.inc(te.matmul(k.bank(0)[:, 0:48], k.ones_row[:], prb[:, 0:48],
                        start=True, stop=True))
    pprep.wait(ve)
    vpr.inc(ve.tensor_copy(biasB[:, 0:48], k.bank(0)[:, 0:48]))
    vpr.inc(ve.memset(st1, 0.0))
    vpr.wait(ge)
    for sb_ in range(8):
        rdm.inc(ge.dma_start(out=rms_ap[0:1, sb_ * 512:(sb_ + 1) * 512],
                             in_=st1), 16)
    put("rdm", -1, rdm)
    rdm.wait(ve)
    vpr.wait(te)
    vpr.wait(se)
    vpr.wait(sy)

    def emit_e3q(j, q_):
        h = j // SB
        P = j % 2
        eP = 16384 * P
        vdf.wait(se, v=at("vdfq", (j, q_)))
        qs = slice(eP + q_ * 4096, eP + (q_ + 1) * 4096)
        ae3.inc(se.activation(E1[:, qs], E2[:, qs], AF.Exp,
                              bias=biasB[:, 32 + h:33 + h]))
        put(("ae3q", q_), j, ae3)
        if q_ == 3:
            put("ae3", j, ae3)

    def emit_post(j):
        h, sb = j // SB, j % SB
        P = j % 2
        eP = 16384 * P
        AVB = 4 + P
        RB = 6 + P
        if j >= 2:
            vout.wait(te, v=at("vout", j - 2))
        ld.wait(te, v=at("ldv", j // SB))
        last = None
        for tt in range(NTL):
            if tt % 8 == 0:
                ae3.wait(te, v=at(("ae3q", tt // 8), j))
            last = te.matmul(k.bank(AVB), Vt[:, tt * 128:(tt + 1) * 128],
                             E1[:, eP + tt * 512:eP + (tt + 1) * 512],
                             start=(tt == 0), stop=(tt == NTL - 1))
        pav.inc(last)
        put("pav", j, pav)
        aev.wait(te, v=at("aevR2", j))
        last = None
        for tt in range(NTL):
            last = te.matmul(k.bank(RB, rows=1, row0=64), k.ones_col[:],
                             E1[:, eP + tt * 512:eP + (tt + 1) * 512],
                             start=(tt == 0), stop=(tt == NTL - 1))
        ps3.inc(last)
        put("ps3", j, ps3)
        ps3.wait(se, v=at("ps3", j))
        k.chain("se", se.copy(s3r[:, P * 512:(P + 1) * 512],
                              k.ps[64:65, 512 * RB:512 * RB + 512]))
        k.chain("se", se.activation(rr, s3r[:, P * 512:(P + 1) * 512], AF.Ln))
        vs3.inc(se.activation(b3r[:, P * 512:(P + 1) * 512], rr, AF.Exp,
                              scale=-1.0))
        put("vs3", j, vs3)
        vs3.wait(te, v=at("vs3", j))
        pb3.inc(te.matmul(k.bank(RB), k.ones_row[:],
                          b3r[:, P * 512:(P + 1) * 512],
                          start=True, stop=True))
        put("pb3", j, pb3)
        pb3.wait(se, v=at("pb3", j))
        aev.inc(se.copy(Rs[:, P * 1536 + 1024:P * 1536 + 1536], k.bank(RB)))
        put("aevR3", j, aev)
        pav.wait(ve, v=at("pav", j))
        aev.wait(ve, v=at("aevR3", j))
        if j >= 2:
            sst.wait(ve, v=at("sst", j - 2))
        vout.inc(ve.tensor_tensor(ob[:, 512 * P:512 * (P + 1)], k.bank(AVB),
                                  Rs[:, P * 1536 + 1024:P * 1536 + 1536],
                                  ALU.mult))
        put("vout", j, vout)
        if j >= 2:
            prm.wait(ve, v=at("prm", j - 2))
        vsq.inc(ve.tensor_tensor(sq2b[:, 512 * P:512 * (P + 1)],
                                 ob[:, 512 * P:512 * (P + 1)],
                                 ob[:, 512 * P:512 * (P + 1)], ALU.mult))
        put("vsq", j, vsq)
        vsq.wait(te, v=at("vsq", j))
        aev.wait(te, v=at("aevR3", j))
        prm.inc(te.matmul(k.bank(RB, rows=1, row0=32), k.ones_col[:],
                          sq2b[:, 512 * P:512 * (P + 1)],
                          start=True, stop=True))
        put("prm", j, prm)
        prm.wait(ve, v=at("prm", j))
        rdm.wait(ve, v=at("rdm", j - 1))
        vrm.inc(ve.tensor_copy(st1, k.ps[32:33, 512 * RB:512 * RB + 512]))
        put("vrm", j, vrm)
        vrm.wait(ge, v=at("vrm", j))
        rdm.inc(ge.dma_start(out=rms_ap[0:1, sb * 512:(sb + 1) * 512],
                             in_=st1, accum_op=ALU.add), 16)
        put("rdm", j, rdm)
        vout.wait(sy, v=at("vout", j))
        sst.inc(sy.dma_start(
            out=s0t_ap[h * 128:(h + 1) * 128, sb * 512:(sb + 1) * 512],
            in_=ob[:, 512 * P:512 * (P + 1)]), 16)
        put("sst", j, sst)

    NITL = min(NIT, ATT0_IT_LIMIT) if ATT0_IT_LIMIT else NIT
    for i in range(NITL):
        h, sb = i // SB, i % SB
        P = i % 2
        eP = 16384 * P
        RB = 6 + P
        if sb == 0:
            if h > 0:
                pmm.wait(sy, v=at("pmm", i - 1))
            ld.inc(sy.dma_start(out=K1[:],
                                in_=k0_ap[h * 256:h * 256 + 128, :]), 16)
            ld.inc(sy.dma_start(out=K2[:],
                                in_=k0_ap[h * 256 + 128:h * 256 + 256, :]), 16)
            if h == 0:
                ld.inc(sy.dma_start(out=Vt[:], in_=v0_ap[0:128, :]), 16)
                put("ldv", 0, ld)
        if i >= 2:
            pmm.wait(sy, v=at("pmm", i - 1))
        lq.inc(sy.dma_start(
            out=qt[:, P * 1024:P * 1024 + 512],
            in_=q0_ap[h * 256:h * 256 + 128, sb * 512:(sb + 1) * 512]), 16)
        lq.inc(sy.dma_start(
            out=qt[:, P * 1024 + 512:P * 1024 + 1024],
            in_=q0_ap[h * 256 + 128:h * 256 + 256, sb * 512:(sb + 1) * 512]),
            16)

        ld.wait(te)
        lq.wait(te)
        for u in range(32):
            sc = u % 2
            g = u // 2
            reg = u % 2
            Ksb = K1 if sc == 0 else K2
            qv = qt[:, P * 1024 + sc * 512: P * 1024 + (sc + 1) * 512]
            wv_ = 32 * i + u - 1
            if wv_ > 0:
                aex.wait(te, v=wv_)
            last = None
            for tj in range(2):
                tt = g * 2 + tj
                last = te.matmul(k.bank(2 * reg + tj),
                                 Ksb[:, tt * 128:(tt + 1) * 128], qv,
                                 start=True, stop=True)
            pmm.inc(last)
        put("pmm", i, pmm)

        hc1 = h * 2
        for u in range(32):
            sc = u % 2
            g = u // 2
            reg = u % 2
            if i >= 1 and not ATT0_NO_POST and u in (10, 16, 22, 28):
                emit_e3q(i - 1, (u - 10) // 6)
            pmm.wait(se, v=at("pmm", i) - 32 + u + 1)
            dstE = E1 if sc == 0 else E2
            if i >= 2 and u < 2:
                vdf.wait(se, v=at("vdf", i - 2))
                psu.wait(se, v=at("psu", i - 2))
                if ("ps3", i - 2) in snap:
                    pav.wait(se, v=at("pav", i - 2))
                    ps3.wait(se, v=at("ps3", i - 2))
            aex.inc(se.activation(
                dstE[:, eP + g * 1024: eP + (g + 1) * 1024],
                k.bank(2 * reg, nb=2), AF.Exp,
                bias=biasB[:, hc1 + sc:hc1 + sc + 1],
                scale=float(1.0 / np.sqrt(128.0))))
        put("aexe", i, aex)

        if i >= 1 and not ATT0_NO_POST:
            emit_post(i - 1)
        if sb == 0 and h > 0:
            if ("pav", i - 1) in snap:
                pav.wait(sy, v=at("pav", i - 1))
            ld.inc(sy.dma_start(out=Vt[:],
                                in_=v0_ap[h * 128:(h + 1) * 128, :]), 16)
            put("ldv", h, ld)

        aex.wait(te)
        if i >= 2:
            vrm.wait(te, v=at("vrm", i - 2))
        first = True
        last = None
        for tt in range(NTL):
            last = te.matmul(k.bank(RB, rows=1), k.ones_col[:],
                             E1[:, eP + tt * 512:eP + (tt + 1) * 512],
                             start=first, stop=False)
            first = False
            last = te.matmul(k.bank(RB, rows=1, row0=32), k.ones_col[:],
                             E2[:, eP + tt * 512:eP + (tt + 1) * 512],
                             start=False, stop=(tt == NTL - 1))
        psu.inc(last)
        put("psu", i, psu)
        psu.wait(se, v=at("psu", i))
        k.chain("se", se.copy(sr, k.ps[0:1, 512 * RB:512 * RB + 512]))
        k.chain("se", se.activation(rr, sr, AF.Ln))
        k.chain("se", se.activation(rb[0:1, P * 1024:P * 1024 + 512], rr,
                                    AF.Exp, scale=-1.0))
        k.chain("se", se.copy(sr, k.ps[32:33, 512 * RB:512 * RB + 512]))
        k.chain("se", se.activation(rr, sr, AF.Ln))
        asr.inc(se.activation(rb2f, rr, AF.Exp, scale=-1.0))
        put("asr", i, asr)
        asr.wait(ve, v=at("asr", i))
        vsr.inc(ve.tensor_scalar(rb[0:1, P * 1024 + 512:P * 1024 + 1024], rb2f,
                                 lam_sb[:, h:h + 1], None, op0=ALU.mult))
        put("vsr", i, vsr)
        vsr.wait(te, v=at("vsr", i))
        pb12.inc(te.matmul(k.bank(RB), k.ones_row[:],
                           rb[0:1, P * 1024:P * 1024 + 512],
                           start=True, stop=True))
        pb12.wait(se)
        aev.inc(se.copy(Rs[:, P * 1536:P * 1536 + 512], k.bank(RB)))
        put("aevR1", i, aev)
        aev.wait(te, v=at("aevR1", i))
        pb12.inc(te.matmul(k.bank(RB), k.ones_row[:],
                           rb[0:1, P * 1024 + 512:P * 1024 + 1024],
                           start=True, stop=True))
        pb12.wait(se)
        aev.inc(se.copy(Rs[:, P * 1536 + 512:P * 1536 + 1024], k.bank(RB)))
        put("aevR2", i, aev)
        put("pb12", i, pb12)

        aev.wait(ve, v=at("aevR2", i))
        R1v = Rs[:, P * 1536:P * 1536 + 512]
        R2v = Rs[:, P * 1536 + 512:P * 1536 + 1024]
        for q_ in range(4):
            for tt in range(q_ * 8, (q_ + 1) * 8):
                sl = slice(eP + tt * 512, eP + (tt + 1) * 512)
                wsl = slice((tt - q_ * 8) * 512, (tt - q_ * 8 + 1) * 512)
                k.chain("ve", ve.tensor_tensor(Wq[:, wsl], E1[:, sl], R1v,
                                               ALU.mult))
            for tt in range(q_ * 8, (q_ + 1) * 8):
                sl = slice(eP + tt * 512, eP + (tt + 1) * 512)
                k.chain("ve", ve.tensor_tensor(E1[:, sl], E2[:, sl], R2v,
                                               ALU.mult))
            for tt in range(q_ * 8, (q_ + 1) * 8):
                sl = slice(eP + tt * 512, eP + (tt + 1) * 512)
                wsl = slice((tt - q_ * 8) * 512, (tt - q_ * 8 + 1) * 512)
                last = ve.tensor_tensor(E2[:, sl], Wq[:, wsl], E1[:, sl],
                                        ALU.subtract)
                if tt != (q_ + 1) * 8 - 1:
                    k.chain("ve", last)
            vdf.inc(last)
            put("vdfq", (i, q_), vdf)
        put("vdf", i, vdf)

    if not ATT0_NO_POST:
        for q_ in range(4):
            emit_e3q(NITL - 1, q_)
        emit_post(NITL - 1)

    rdm.wait(sy)
    lf = k.sem(f"{tag}_lf")
    vvf = k.sem(f"{tag}_vvf")
    afc = k.sem(f"{tag}_afc")
    for cch in range(8):
        csl = slice(cch * 512, (cch + 1) * 512)
        if cch >= 1:
            sst.wait(sy, v=at("fst", cch - 1))
        lf.inc(sy.dma_start(out=rr, in_=rms_ap[0:1, csl]), 16)
        lf.wait(ve)
        k.chain("ve", ve.tensor_scalar(sr, rr, 1.0 / D, None, op0=ALU.mult))
        vvf.inc(ve.tensor_copy(rr, sr))
        vvf.wait(se)
        afc.inc(se.activation(sr, rr, AF.Sqrt, bias=k.constf(EPS_RMS, rows=1)))
        afc.wait(ve)
        k.chain("ve", ve.reciprocal(rr, sr))
        vvf.inc(ve.tensor_scalar(sr, rr, mlap, None, op0=ALU.mult))
        vvf.wait(sy)
        sst.inc(sy.dma_start(out=factor_ap[0:1, csl], in_=sr), 16)
        snap[("fst", cch)] = sst.v
    sst.wait(sy)
    _les.close()


def emit_wo0(k: KB, tag, s0t_ap, factor_ap, wo_ap, ns_ap, s0_ap, n_tok=NT,
             d_out=D, use_ns=True, use_factor=True):
    nc, te, ve, se, sy, ge = k.nc, k.te, k.ve, k.se, k.sy, k.ge
    wl = k.sem(f"{tag}_wl")
    pl = k.sem(f"{tag}_pl")
    fl = k.sem(f"{tag}_fl")
    vsc = k.sem(f"{tag}_vsc")
    pmm = k.sem(f"{tag}_pmm")
    aev = k.sem(f"{tag}_aev")
    pfb = k.sem(f"{tag}_pfb")
    sst = k.sem(f"{tag}_sst")
    NOC = d_out // 128
    NMAC = n_tok // 2048
    snap = {}
    with (
        nc.sbuf_tensor(f"{tag}_wo", [128, NCH * d_out], BF16) as wo_sb,
        nc.sbuf_tensor(f"{tag}_pan", [128, NCH * 2048], BF16) as pan,
        nc.sbuf_tensor(f"{tag}_sc", [128, 2 * NCH * 512], BF16) as scl,
        nc.sbuf_tensor(f"{tag}_fb", [128, 2048], BF16) as fb,
        nc.sbuf_tensor(f"{tag}_fr", [1, 2048], F32) as fr_h,
        nc.sbuf_tensor(f"{tag}_frb", [1, 2048], BF16) as frb_h,
        nc.sbuf_tensor(f"{tag}_ns", [128, 16], F32) as ns_sb,
        nc.sbuf_tensor(f"{tag}_ev", [128, 2 * 512], BF16) as evt,
    ):
        fr, frb = fr_h[:], frb_h[:]
        for fc in range(NCH):
            wl.inc(ge.dma_start(out=wo_sb[:, fc * d_out:(fc + 1) * d_out],
                                in_=wo_ap[fc * 128:(fc + 1) * 128, :]), 16)
        if use_ns:
            wl.inc(ge.dma_start(out=ns_sb[:], in_=ns_ap), 16)
        nit = 0
        noc_it = 0
        for mac in range(NMAC):
            msl = slice(mac * 2048, (mac + 1) * 2048)
            if mac > 0:
                pmm.wait(sy, v=snap[("pmm_mac", mac - 1)])
            for fc in range(NCH):
                pl.inc(sy.dma_start(out=pan[:, fc * 2048:(fc + 1) * 2048],
                                    in_=s0t_ap[fc * 128:(fc + 1) * 128, msl]), 16)
            if use_factor:
                fl.inc(sy.dma_start(out=fr, in_=factor_ap[0:1, msl]), 16)
                fl.wait(ve)
                if mac > 0:
                    vsc.wait(ve, v=snap[("vsc_mac", mac - 1)])
                k.chain("ve", ve.tensor_copy(frb, fr))
                vsc.inc(ve.tensor_copy(frb[:, 0:1], fr[:, 0:1]))
                vsc.wait(te)
                lastb = None
                for j in range(4):
                    lastb = te.matmul(k.bank(4 + j), k.ones_row[:],
                                      frb[:, j * 512:(j + 1) * 512],
                                      start=True, stop=True)
                pfb.inc(lastb)
                pfb.wait(ve)
                for j in range(4):
                    k.chain("ve", ve.tensor_copy(fb[:, j * 512:(j + 1) * 512],
                                                 k.bank(4 + j)))
            for nb in range(4):
                nbs = slice(nb * 512, (nb + 1) * 512)
                sb_ = nit % 2
                ssl0 = sb_ * NCH * 512
                pl.wait(ve)
                if nit >= 2:
                    pmm.wait(ve, v=snap[("pmm_nit", nit - 2)])
                last = None
                for fc in range(NCH):
                    src = pan[:, fc * 2048 + nb * 512: fc * 2048 + (nb + 1) * 512]
                    dst = scl[:, ssl0 + fc * 512:ssl0 + (fc + 1) * 512]
                    if use_ns:
                        k.chain("ve", ve.tensor_scalar(
                            dst, src, ns_sb[:, fc:fc + 1], None, op0=ALU.mult))
                        if use_factor:
                            last = ve.tensor_tensor(dst, dst, fb[:, nbs], ALU.mult)
                    elif use_factor:
                        last = ve.tensor_tensor(dst, src, fb[:, nbs], ALU.mult)
                    else:
                        last = ve.tensor_copy(dst, src)
                    if fc < NCH - 1:
                        k.chain("ve", last)
                vsc.inc(last)
                snap[("vsc_nit", nit)] = vsc.v
                vsc.wait(te)
                wl.wait(te)
                for oc in range(NOC):
                    bank = noc_it % 2
                    if noc_it >= 2:
                        aev.wait(te, v=noc_it - 1)
                    last = None
                    for fc in range(NCH):
                        last = te.matmul(
                            k.bank(bank),
                            wo_sb[:, fc * d_out + oc * 128:
                                  fc * d_out + (oc + 1) * 128],
                            scl[:, ssl0 + fc * 512:ssl0 + (fc + 1) * 512],
                            start=(fc == 0), stop=(fc == NCH - 1))
                    pmm.inc(last)
                    pmm.wait(se)
                    if noc_it >= 2:
                        sst.wait(se, v=16 * (noc_it - 1))
                    aev.inc(se.copy(evt[:, bank * 512:(bank + 1) * 512],
                                    k.bank(bank)))
                    aev.wait(sy)
                    sst.inc(sy.dma_start(
                        out=s0_ap[oc * 128:(oc + 1) * 128,
                                  mac * 2048 + nb * 512:
                                  mac * 2048 + (nb + 1) * 512],
                        in_=evt[:, bank * 512:(bank + 1) * 512]), 16)
                    noc_it += 1
                snap[("pmm_nit", nit)] = pmm.v
                nit += 1
            snap[("pmm_mac", mac)] = pmm.v
            snap[("vsc_mac", mac)] = vsc.v
        sst.wait(sy)


def emit_ffn0(k: KB, tag, xn_ap, wT_ap, a_ap, b_ap, rt_ap, rep_ap, id_ap,
              g0_ap):
    nc, te, ve, se, sy, ge = k.nc, k.te, k.ve, k.se, k.sy, k.ge
    wl = k.sem(f"{tag}_wl")
    pl = k.sem(f"{tag}_pl")
    prt = k.sem(f"{tag}_prt")
    vrt = k.sem(f"{tag}_vrt")
    art = k.sem(f"{tag}_art")
    ptr = k.sem(f"{tag}_ptr")
    aev = k.sem(f"{tag}_aev")
    prp = k.sem(f"{tag}_prp")
    pax = k.sem(f"{tag}_pax")
    vax = k.sem(f"{tag}_vax")
    pmm = k.sem(f"{tag}_pmm")
    asl = k.sem(f"{tag}_asl")
    sst = k.sem(f"{tag}_sst")
    snap = {}
    with (
        nc.sbuf_tensor(f"{tag}_xn", [128, NCH * 2048], BF16) as xp,
        nc.sbuf_tensor(f"{tag}_wt", [128, NCH * D], BF16) as wt,
        nc.sbuf_tensor(f"{tag}_A", [128, NCH * 128], BF16) as A_sb,
        nc.sbuf_tensor(f"{tag}_B", [128, D], BF16) as B_sb,
        nc.sbuf_tensor(f"{tag}_B2", [128, D], BF16) as B2,
        nc.sbuf_tensor(f"{tag}_rt", [128, NCH * 8], BF16) as rt_sb,
        nc.sbuf_tensor(f"{tag}_rep", [8, 128], BF16) as rep_sb,
        nc.sbuf_tensor(f"{tag}_id", [128, 128], BF16) as id_sb,
        nc.sbuf_tensor(f"{tag}_wfm", [8, 2048], BF16) as wfm,
        nc.sbuf_tensor(f"{tag}_wrp", [128, 2048], BF16) as wrp,
        nc.sbuf_tensor(f"{tag}_axw", [128, 2048], BF16) as axw,
        nc.sbuf_tensor(f"{tag}_r1", [128, 64], F32) as r1_h,
        nc.sbuf_tensor(f"{tag}_rb1", [128, 8], BF16) as rb1_h,
        nc.sbuf_tensor(f"{tag}_ev", [128, 2 * 512], BF16) as evt,
    ):
        r1 = r1_h[:]
        rb1 = rb1_h[:]
        for fc in range(NCH):
            wl.inc(ge.dma_start(out=wt[:, fc * D:(fc + 1) * D],
                                in_=wT_ap[fc * 128:(fc + 1) * 128, :]), 16)
        wl.inc(ge.dma_start(out=A_sb[:].rearrange("p (c f) -> p c f", c=NCH),
                            in_=a_ap.rearrange("(c p) f -> p c f", p=128)), 16)
        wl.inc(ge.dma_start(out=B_sb[:], in_=b_ap), 16)
        wl.inc(ge.dma_start(out=rt_sb[:].rearrange("p (c f) -> p c f", c=NCH),
                            in_=rt_ap.rearrange("(c p) f -> p c f", p=128)), 16)
        wl.inc(ge.dma_start(out=rep_sb[:], in_=rep_ap), 16)
        wl.inc(ge.dma_start(out=id_sb[:], in_=id_ap), 16)
        wl.wait(ve)
        vB2 = k.sem(f"{tag}_vB2")
        vB2.inc(ve.tensor_scalar(B2[:], B_sb[:], 2.0, None, op0=ALU.mult))
        vB2.wait(te)

        for mac in range(NT // 2048):
            msl = slice(mac * 2048, (mac + 1) * 2048)
            if mac > 0:
                if ("pmm_mac", mac - 1) in snap:
                    pmm.wait(sy, v=snap[("pmm_mac", mac - 1)])
                if ("pax_mac", mac - 1) in snap:
                    pax.wait(sy, v=snap[("pax_mac", mac - 1)])
                prt.wait(sy, v=snap[("prt_mac", mac - 1)] if
                         ("prt_mac", mac - 1) in snap else prt.v)
            for fc in range(NCH):
                pl.inc(sy.dma_start(out=xp[:, fc * 2048:(fc + 1) * 2048],
                                    in_=xn_ap[fc * 128:(fc + 1) * 128, msl]), 16)
            pl.wait(te)
            wl.wait(te)
            for tt in range(16):
                if mac + tt > 0:
                    prev = (mac, tt - 1) if tt else (mac - 1, 15)
                    vrt.wait(te, v=snap[("vrt", prev)])
                    art.wait(te, v=snap[("art", prev)])
                last = None
                for fc in range(NCH):
                    last = te.matmul(k.bank(0)[:, 0:8],
                                     xp[:, fc * 2048 + tt * 128:
                                        fc * 2048 + (tt + 1) * 128],
                                     rt_sb[:, fc * 8:(fc + 1) * 8],
                                     start=(fc == 0), stop=(fc == NCH - 1))
                prt.inc(last)
                prt.wait(ve)
                e_t = r1[:, 0:8]
                probs = r1[:, 8:16]
                eq1 = r1[:, 16:24]
                pm = r1[:, 24:32]
                eq2 = r1[:, 32:40]
                wsel = r1[:, 40:48]
                eq1s = r1[:, 56:64]
                mx = r1[:, 48:49]
                mxn = r1[:, 49:50]
                sm = r1[:, 50:51]
                rs = r1[:, 51:52]
                m1 = r1[:, 52:53]
                m2 = r1[:, 53:54]
                ws = r1[:, 54:55]
                rw = r1[:, 55:56]
                k.chain("ve", ve.reduce_max(mx, k.bank(0)[:, 0:8], axis=AXX))
                vrt.inc(ve.tensor_scalar(mxn, mx, -1.0, None, op0=ALU.mult))
                vrt.wait(se)
                art.inc(se.activation(e_t, k.bank(0)[:, 0:8], AF.Exp, bias=mxn,
                                      accum_out=sm))
                snap[("art", (mac, tt))] = art.v
                art.wait(ve)
                k.chain("ve", ve.reciprocal(rs, sm))
                k.chain("ve", ve.tensor_scalar(probs, e_t, rs, None, op0=ALU.mult))
                k.chain("ve", ve.reduce_max(m1, probs, axis=AXX))
                k.chain("ve", ve.tensor_scalar(eq1, probs, m1, None,
                                               op0=ALU.is_equal))
                k.chain("ve", ve.tensor_scalar(eq1s, eq1, -2.0, None, op0=ALU.mult))
                k.chain("ve", ve.tensor_tensor(pm, probs, eq1s, ALU.add))
                k.chain("ve", ve.reduce_max(m2, pm, axis=AXX))
                k.chain("ve", ve.tensor_scalar(eq2, probs, m2, None,
                                               op0=ALU.is_equal))
                k.chain("ve", ve.tensor_tensor(pm, eq1, eq2, ALU.add))
                k.chain("ve", ve.tensor_tensor(wsel, probs, pm, ALU.mult))
                k.chain("ve", ve.tensor_tensor(ws, m1, m2, ALU.add))
                k.chain("ve", ve.reciprocal(rw, ws))
                vrt.inc(ve.tensor_scalar(rb1, wsel, rw, None, op0=ALU.mult))
                snap[("vrt", (mac, tt))] = vrt.v
                vrt.wait(te)
                if mac + tt > 0:
                    aev.wait(te, v=snap[("aevt", (mac, tt - 1) if tt
                                         else (mac - 1, 15))])
                ptr.inc(te.matmul(k.bank(1, rows=8)[:, 0:128], rb1, id_sb[:],
                                  start=True, stop=True))
                ptr.wait(se)
                aev.inc(se.copy(wfm[:, tt * 128:(tt + 1) * 128],
                                k.bank(1, rows=8)[:, 0:128]))
                snap[("aevt", (mac, tt))] = aev.v
            snap[("prt_mac", mac)] = prt.v
            if FFN0_STAGE < 2:
                continue
            aev.wait(te)
            if mac > 0:
                vax.wait(se, v=snap[("vax_mac", mac - 1)])
            for nb in range(4):
                if nb > 0:
                    aev.wait(te)
                prp.inc(te.matmul(k.bank(2), rep_sb[:],
                                  wfm[:, nb * 512:(nb + 1) * 512],
                                  start=True, stop=True))
                prp.wait(se)
                aev.inc(se.copy(wrp[:, nb * 512:(nb + 1) * 512], k.bank(2)))
            aev.wait(ve)
            if mac > 0 and ("pmm_mac", mac - 1) in snap:
                pmm.wait(ve, v=snap[("pmm_mac", mac - 1)])
            for nb in range(4):
                vax.wait(te)
                last = None
                for fc in range(NCH):
                    last = te.matmul(k.bank(3), A_sb[:, fc * 128:(fc + 1) * 128],
                                     xp[:, fc * 2048 + nb * 512:
                                        fc * 2048 + (nb + 1) * 512],
                                     start=(fc == 0), stop=(fc == NCH - 1))
                pax.inc(last)
                pax.wait(ve)
                vax.inc(ve.tensor_tensor(axw[:, nb * 512:(nb + 1) * 512],
                                         k.bank(3), wrp[:, nb * 512:(nb + 1) * 512],
                                         ALU.mult))
            snap[("vax_mac", mac)] = vax.v
            snap[("pax_mac", mac)] = pax.v
            if FFN0_STAGE < 3:
                continue
            vax.wait(te)
            vB2.wait(te)
            bit = mac * NCH * 4
            for oc in range(NCH):
                for nb in range(4):
                    bank = 4 + bit % 2
                    if bit >= 2:
                        asl.wait(te, v=snap[("aslG", bit - 2)])
                    last = None
                    for fc in range(NCH):
                        last = te.matmul(
                            k.bank(bank),
                            wt[:, fc * D + oc * 128: fc * D + (oc + 1) * 128],
                            xp[:, fc * 2048 + nb * 512:
                               fc * 2048 + (nb + 1) * 512],
                            start=(fc == 0), stop=False)
                    last = te.matmul(k.bank(bank), B2[:, oc * 128:(oc + 1) * 128],
                                     axw[:, nb * 512:(nb + 1) * 512],
                                     start=False, stop=True)
                    pmm.inc(last)
                    pmm.wait(se)
                    if bit >= 2:
                        sst.wait(se, v=snap[("sstG", bit - 2)])
                    asl.inc(se.activation(evt[:, (bit % 2) * 512:
                                              (bit % 2 + 1) * 512],
                                          k.bank(bank), AF.Silu))
                    snap[("aslG", bit)] = asl.v
                    asl.wait(sy)
                    sst.inc(sy.dma_start(
                        out=g0_ap[oc * 128:(oc + 1) * 128,
                                  mac * 2048 + nb * 512:
                                  mac * 2048 + (nb + 1) * 512],
                        in_=evt[:, (bit % 2) * 512:(bit % 2 + 1) * 512]), 16)
                    snap[("sstG", bit)] = sst.v
                    bit += 1
            snap[("pmm_mac", mac)] = pmm.v
        sst.wait(sy)


def emit_combine(k: KB, tag, x_ap, s_ap, g_ap, xout_ap, n_tok=NT,
                 g_silu=False):
    nc, te, ve, se, sy, ge = k.nc, k.te, k.ve, k.se, k.sy, k.ge
    ld = k.sem(f"{tag}_ld")
    vc = k.sem(f"{tag}_vc")
    agl = k.sem(f"{tag}_agl")
    sst = k.sem(f"{tag}_sst")
    gdt = F32 if g_silu else BF16
    with (
        nc.sbuf_tensor(f"{tag}_x", [128, 2 * n_tok], F32) as xc,
        nc.sbuf_tensor(f"{tag}_s", [128, 2 * n_tok], BF16) as sc,
        nc.sbuf_tensor(f"{tag}_g", [128, 2 * n_tok], gdt) as gc,
        nc.sbuf_tensor(f"{tag}_gs", [128, n_tok], BF16) as gs,
        nc.sbuf_tensor(f"{tag}_sg", [128, n_tok], BF16) as sg,
        nc.sbuf_tensor(f"{tag}_o", [128, 2 * n_tok], F32) as oc_,
    ):
        for c in range(NCH):
            buf = c % 2
            sl = slice(buf * n_tok, (buf + 1) * n_tok)
            if c >= 2:
                vc.wait(sy, v=c - 1)
                if g_silu:
                    agl.wait(sy, v=c - 1)
            ld.inc(sy.dma_start(out=xc[:, sl],
                                in_=x_ap[c * 128:(c + 1) * 128, :]), 16)
            ld.inc(sy.dma_start(out=sc[:, sl],
                                in_=s_ap[c * 128:(c + 1) * 128, :]), 16)
            ld.inc(sy.dma_start(out=gc[:, sl],
                                in_=g_ap[c * 128:(c + 1) * 128, :]), 16)
            if g_silu:
                ld.wait(se)
                if c >= 1:
                    vc.wait(se, v=c)
                agl.inc(se.activation(gs[:], gc[:, sl], AF.Silu))
                agl.wait(ve)
            ld.wait(ve)
            if c >= 2:
                sst.wait(ve, v=16 * (c - 1))
            gsrc = gs[:] if g_silu else gc[:, sl]
            k.chain("ve", ve.tensor_tensor(sg[:], sc[:, sl], gsrc, ALU.mult))
            vc.inc(ve.tensor_tensor(oc_[:, sl], xc[:, sl], sg[:], ALU.add))
            vc.wait(sy)
            sst.inc(sy.dma_start(out=xout_ap[c * 128:(c + 1) * 128, :],
                                 in_=oc_[:, sl]), 16)
        sst.wait(sy)


def emit_qkv1(k: KB, tag, xn_ap, wq_ap, wk_ap, wv_ap, cos_ap, sin_ap, psw_ap,
              q1_ap, k1_ap, v1_ap, bnd_ap):
    nc, te, ve, se, sy, ge = k.nc, k.te, k.ve, k.se, k.sy, k.ge
    xl = k.sem(f"{tag}_xl")
    wl = k.sem(f"{tag}_wl")
    smm = k.sem(f"{tag}_mm")
    sev = k.sem(f"{tag}_ev")
    psw = k.sem(f"{tag}_sw")
    vro = k.sem(f"{tag}_ro")
    pnm = k.sem(f"{tag}_nm")
    vmx = k.sem(f"{tag}_mx")
    vsq = k.sem(f"{tag}_sq")
    sst = k.sem(f"{tag}_st")
    snap = {}
    with (
        nc.sbuf_tensor(f"{tag}_xn", [128, NCH * NT], BF16) as xn_sb,
        nc.sbuf_tensor(f"{tag}_wb", [128, 2 * NCH * 512], BF16) as wb,
        nc.sbuf_tensor(f"{tag}_cos", [128, NT], F32) as cos_sb,
        nc.sbuf_tensor(f"{tag}_sin", [128, NT], F32) as sin_sb,
        nc.sbuf_tensor(f"{tag}_psw", [128, 128], BF16) as psw_sb,
        nc.sbuf_tensor(f"{tag}_ho", [128, 2], BF16) as ho_sb,
        nc.sbuf_tensor(f"{tag}_qe", [128, 2 * 512], BF16) as qe,
        nc.sbuf_tensor(f"{tag}_u1", [128, 512], F32) as u1_h,
        nc.sbuf_tensor(f"{tag}_u2", [128, 512], F32) as u2_h,
        nc.sbuf_tensor(f"{tag}_rq", [128, 2 * 512], BF16) as rq,
        nc.sbuf_tensor(f"{tag}_sq2", [128, 2 * 512], BF16) as sq2,
        nc.sbuf_tensor(f"{tag}_mxr", [2, 512], F32) as mxr_h,
    ):
        u1, u2, mxr = u1_h[:], u2_h[:], mxr_h[:]
        init2 = k.sem(f"{tag}_in2")
        k.ve.memset(ho_sb[:], 0.0)
        k.ve.memset(ho_sb[0:64, 0:1], 1.0)
        init2.inc(k.ve.memset(ho_sb[64:128, 1:2], 1.0))
        init2.wait(te)
        for c in range(NCH):
            xl.inc(sy.dma_start(out=xn_sb[:, c * NT:(c + 1) * NT],
                                in_=xn_ap[c * 128:(c + 1) * 128, :]), 16)
        xl.inc(sy.dma_start(out=cos_sb[:], in_=cos_ap), 16)
        xl.inc(sy.dma_start(out=sin_sb[:], in_=sin_ap), 16)
        xl.inc(ge.dma_start(out=psw_sb[:], in_=psw_ap), 16)
        xl.wait(te)

        blocks = [("q", wq_ap, q1_ap, 4, 4, 0), ("k", wk_ap, k1_ap, 1, 8, 64)]
        it = 0
        for name, w_ap, dst_ap, nfb, ntb, nslot0 in blocks:
            for fb in range(nfb):
                bi = it
                wbuf = (len(snap) * 0 + fb + (0 if name == "q" else 4)) % 2
                if ("wld", wbuf) in snap:
                    smm.wait(ge, v=snap[("wld", wbuf)])
                src = w_ap[:, fb * 512:(fb + 1) * 512]
                src = src.rearrange("(dc p) f -> p dc f", p=128)
                dst3 = wb[:, wbuf * NCH * 512:(wbuf + 1) * NCH * 512].rearrange(
                    "p (dc f) -> p dc f", dc=NCH)
                wl.inc(ge.dma_start(out=dst3, in_=src), 16)
                wseen = wl.v
                for fc in range(4):
                    for tb in range(ntb):
                        buf = it % 2
                        bank = it % 2
                        wl.wait(te, v=wseen)
                        if it >= 2:
                            sev.wait(te, v=snap[("sev", it - 2)])
                        last = None
                        for dc in range(NCH):
                            lhsT = wb[:, wbuf * NCH * 512 + dc * 512 + fc * 128:
                                      wbuf * NCH * 512 + dc * 512 + (fc + 1) * 128]
                            rhs = xn_sb[:, dc * NT + tb * 512:
                                        dc * NT + (tb + 1) * 512]
                            last = te.matmul(k.bank(bank), lhsT, rhs,
                                             start=(dc == 0), stop=(dc == NCH - 1))
                        smm.inc(last)
                        snap[("wld", wbuf)] = smm.v
                        smm.wait(se, v=smm.v)
                        if it >= 2:
                            psw.wait(se, v=snap[("psw", it - 2)])
                            k.sem("chain_ve").wait(se, v=snap[("vroU2", it - 2)])
                        sev.inc(se.copy(qe[:, buf * 512:(buf + 1) * 512],
                                        k.bank(bank)))
                        snap[("sev", it)] = sev.v
                        sev.wait(te, v=snap[("sev", it)])
                        if it >= 2:
                            k.sem("chain_ve").wait(te, v=snap[("vroU2", it - 2)])
                        psw.inc(te.matmul(k.bank(2 + bank), psw_sb[:],
                                          qe[:, buf * 512:(buf + 1) * 512],
                                          start=True, stop=True))
                        snap[("psw", it)] = psw.v
                        csl = slice(tb * 512, (tb + 1) * 512)
                        psw.wait(ve, v=snap[("psw", it)])
                        k.chain("ve", ve.tensor_tensor(
                            u1, qe[:, buf * 512:(buf + 1) * 512],
                            cos_sb[:, csl], ALU.mult))
                        k.chain("ve", ve.tensor_tensor(u2, k.bank(2 + bank),
                                                       sin_sb[:, csl], ALU.mult))
                        chv = k.sem("chain_ve")
                        snap[("vroU2", it)] = chv.v
                        if it >= 2:
                            sst.wait(ve, v=snap[("sstq", it - 2)])
                        k.chain("ve", ve.tensor_tensor(
                            rq[:, buf * 512:(buf + 1) * 512], u1, u2, ALU.add))
                        snap[("vro", it)] = chv.v
                        if it >= 1:
                            pnm.wait(ve, v=snap[("pnm", it - 1)])
                        vsq.inc(ve.tensor_tensor(
                            sq2[:, buf * 512:(buf + 1) * 512],
                            rq[:, buf * 512:(buf + 1) * 512],
                            rq[:, buf * 512:(buf + 1) * 512], ALU.mult))
                        vsq.wait(te, v=vsq.v)
                        if it >= 2:
                            vmx.wait(te, v=snap[("vmx", it - 2)])
                        pnm.inc(te.matmul(k.bank(4 + bank, rows=2), ho_sb[:],
                                          sq2[:, buf * 512:(buf + 1) * 512],
                                          start=True, stop=True))
                        snap[("pnm", it)] = pnm.v
                        pnm.wait(ve, v=snap[("pnm", it)])
                        slot = nslot0 + (fb * 4 + fc) * ntb + tb
                        vmx.inc(ve.reduce_max(mxr[:, slot:slot + 1],
                                              k.bank(4 + bank, rows=2), axis=AXX))
                        snap[("vmx", it)] = vmx.v
                        k.sem("chain_ve").wait(sy, v=snap[("vro", it)])
                        sst.inc(sy.dma_start(
                            out=dst_ap[fb * 512 + fc * 128:
                                       fb * 512 + (fc + 1) * 128,
                                       tb * 512:(tb + 1) * 512],
                            in_=rq[:, buf * 512:(buf + 1) * 512]), 16)
                        snap[("sstq", it)] = sst.v
                        it += 1
        vmx.wait(sy)
        sst.inc(sy.dma_start(out=bnd_ap, in_=mxr), 16)

        wbuf = 0
        smm.wait(ge, v=smm.v)
        src = wv_ap.rearrange("(dc p) f -> p dc f", p=128)
        dst3 = wb[:, 0:NCH * 512].rearrange("p (dc f) -> p dc f", dc=NCH)
        wl.inc(ge.dma_start(out=dst3, in_=src), 16)
        st_v = {}
        for tt in range(NTL):
            bank = tt % 2
            buf = tt % 2
            wl.wait(te)
            if tt >= 2:
                sev.wait(te, v=snap[("sevv", tt - 2)])
            last = None
            for dc in range(NCH):
                lhsT = xn_sb[:, dc * NT + tt * 128: dc * NT + (tt + 1) * 128]
                rhs = wb[:, dc * 512:(dc + 1) * 512]
                last = te.matmul(k.bank(bank), lhsT, rhs,
                                 start=(dc == 0), stop=(dc == NCH - 1))
            smm.inc(last)
            smm.wait(se, v=smm.v)
            if tt >= 2:
                sst.wait(se, v=st_v[tt - 2])
            sev.inc(se.copy(qe[:, buf * 512:(buf + 1) * 512], k.bank(bank)))
            snap[("sevv", tt)] = sev.v
            sev.wait(sy, v=snap[("sevv", tt)])
            for g in range(8):
                sst.inc(sy.dma_start(
                    out=v1_ap[g * 128:(g + 1) * 128, tt * 64:(tt + 1) * 64],
                    in_=qe[:, buf * 512 + g * 64: buf * 512 + (g + 1) * 64]), 16)
            st_v[tt] = sst.v
        sst.wait(sy)


def emit_att1(k: KB, tag, q1_ap, k1_ap, v1_ap, bnd_ap, cm_ap, a1_ap):
    nc, te, ve, se, sy, ge = k.nc, k.te, k.ve, k.se, k.sy, k.ge
    ld = k.sem(f"{tag}_ld")
    lq = k.sem(f"{tag}_lq")
    pmm = k.sem(f"{tag}_pmm")
    aex = k.sem(f"{tag}_aex")
    vmk = k.sem(f"{tag}_vmk")
    pav = k.sem(f"{tag}_pav")
    vs3 = k.sem(f"{tag}_vs3")
    pb3 = k.sem(f"{tag}_pb3")
    aev = k.sem(f"{tag}_aev")
    vout = k.sem(f"{tag}_vout")
    vms = k.sem(f"{tag}_vms")
    sst = k.sem(f"{tag}_sst")
    vpr = k.sem(f"{tag}_vpr")
    snap = {}

    def put(n, i, sem):
        snap[(n, i)] = sem.v

    def at(n, i):
        return snap[(n, i)]

    from contextlib import ExitStack as _ES
    _les = _ES()

    def _sb(name, shape, dt):
        return _les.enter_context(nc.sbuf_tensor(name, shape, dt))

    E1 = _sb(f"{tag}_E1", [128, 2 * 16384], BF16)
    MB = _sb(f"{tag}_MB", [128, 2 * 4096], BF16)
    Kp = _sb(f"{tag}_K", [128, NT], BF16)
    V65 = _sb(f"{tag}_V", [128, 2 * 32 * 65], BF16)
    qt = _sb(f"{tag}_q", [128, 2 * 512], BF16)
    CM = _sb(f"{tag}_cm", [128, 4 * 8 * 512], BF16)
    Rs = _sb(f"{tag}_Rs", [128, 512], BF16)
    ob = _sb(f"{tag}_ob", [128, 512], BF16)
    s3r = _sb(f"{tag}_s3", [1, 512], F32)[:]
    r3r = _sb(f"{tag}_r3", [1, 512], F32)[:]
    b3r = _sb(f"{tag}_b3", [1, 512], BF16)[:]
    pr = _sb(f"{tag}_pr", [1, 1280], F32)[:]
    prb = _sb(f"{tag}_prb", [1, 64], BF16)[:]
    biasB = _sb(f"{tag}_bias", [128, 32], F32)[:]

    bnd = pr[:, 0:1024]
    lq.inc(sy.dma_start(out=pr[:, 0:512], in_=bnd_ap[0:1, :]), 16)
    lq.inc(sy.dma_start(out=pr[:, 512:1024], in_=bnd_ap[1:2, :]), 16)
    lq.inc(ge.dma_start(out=CM[:], in_=cm_ap), 16)
    lq.wait(ve)
    bq = pr[:, 1024 + 0:1024 + 32]
    bk = pr[:, 1024 + 32:1024 + 40]
    scr40 = pr[:, 1064:1104]
    bk4 = pr[:, 1104:1136]
    prod = pr[:, 1136:1168]
    for h in range(32):
        col = (h % 2) * 512 + (h // 2) * 4
        k.chain("ve", ve.reduce_max(bq[:, h:h + 1], pr[:, col:col + 4], axis=AXX))
    for g in range(8):
        col = (g % 2) * 512 + 64 + (g // 2) * 8
        k.chain("ve", ve.reduce_max(bk[:, g:g + 1], pr[:, col:col + 8], axis=AXX))
    vpr.inc(ve.tensor_copy(scr40, pr[:, 1024:1064]))
    vpr.wait(se)
    apr = k.sem(f"{tag}_apr")
    apr.inc(se.activation(pr[:, 1024:1064], scr40, AF.Sqrt))
    apr.wait(ve)
    bk4v = bk4.rearrange("p (a b) -> p a b", b=4)
    for j in range(4):
        k.chain("ve", ve.tensor_copy(bk4v[:, :, j:j + 1],
                                     bk.rearrange("p (a b) -> p a b", b=1)))
    k.chain("ve", ve.tensor_tensor(prod, bq, bk4, ALU.mult))
    vpr.inc(ve.tensor_scalar(prb[:, 0:32], prod, -1.0 / 8.0, None, op0=ALU.mult))
    vpr.wait(te)
    ppr = k.sem(f"{tag}_ppr")
    ppr.inc(te.matmul(k.bank(0)[:, 0:32], k.ones_row[:], prb[:, 0:32],
                      start=True, stop=True))
    ppr.wait(ve)
    vpr.inc(ve.tensor_copy(biasB, k.bank(0)[:, 0:32]))
    vpr.wait(te)
    vpr.wait(se)
    vpr.wait(sy)
    vpr.wait(ge)
    if ATT1_STAGE == 0:
        _les.close()
        return

    def emit_post(j):
        jp, sb, hq8 = j // 32, (j // 8) % 4, j % 8
        gi = hq8 // 4
        hq = jp * 8 + hq8
        P = j % 2
        eP = 16384 * P
        mP = 4096 * P
        nt = 8 * sb + 8
        aex.wait(te, v=at("aexe", j))
        vmk.wait(te, v=at("vmk", j))
        if j >= 1:
            vout.wait(te, v=at("vout", j - 1))
        ld.wait(te, v=at("ldv", j // 32))
        last = None
        for jj in range(nt):
            own = jj < nt // 2
            c65 = jj if own else 16 + (jj - nt // 2)
            near = (jj >= nt // 2 - 4 and own) or (jj >= nt - 4)
            if near:
                m = (jj - (nt // 2 - 4)) if own else (4 + jj - (nt - 4))
                rhs = MB[:, mP + m * 512: mP + (m + 1) * 512]
            else:
                rhs = E1[:, eP + jj * 512: eP + (jj + 1) * 512]
            last = te.matmul(k.bank(6)[0:65, :],
                             V65[:, gi * 2080 + c65 * 65: gi * 2080 + (c65 + 1) * 65],
                             rhs, start=(jj == 0), stop=(jj == nt - 1))
        pav.inc(last)
        put("pav", j, pav)
        pav.wait(ve, v=at("pav", j))
        if j >= 1:
            pb3.wait(ve, v=at("pb3", j - 1))
        k.chain("ve", ve.tensor_copy(s3r, k.ps[64:65, 512 * 6:512 * 7]))
        k.chain("ve", ve.reciprocal(r3r, s3r))
        vs3.inc(ve.tensor_copy(b3r, r3r))
        put("vs3", j, vs3)
        vs3.wait(te, v=at("vs3", j))
        if j >= 1:
            aev.wait(te, v=at("aevR3", j - 1))
        pb3.inc(te.matmul(k.bank(7, rows=64), k.ones_row[0:1, 0:64], b3r,
                          start=True, stop=True))
        put("pb3", j, pb3)
        pb3.wait(se, v=at("pb3", j))
        aev.inc(se.copy(Rs[0:64, :], k.bank(7, rows=64)))
        put("aevR3", j, aev)
        aev.wait(ve, v=at("aevR3", j))
        if j >= 1:
            sst.wait(ve, v=at("sst", j - 1))
        vout.inc(ve.tensor_tensor(ob[0:64, :], k.bank(6)[0:64, :], Rs[0:64, :],
                                  ALU.mult))
        put("vout", j, vout)
        vout.wait(sy, v=at("vout", j))
        sst.inc(sy.dma_start(
            out=a1_ap[hq * 64:(hq + 1) * 64, sb * 512:(sb + 1) * 512],
            in_=ob[0:64, :]), 16)
        put("sst", j, sst)

    NIT = min(128, ATT1_IT_LIMIT) if ATT1_IT_LIMIT else 128
    for i in range(NIT):
        jp, sb, hq8 = i // 32, (i // 8) % 4, i % 8
        gi = hq8 // 4
        hq = jp * 8 + hq8
        P = i % 2
        eP = 16384 * P
        mP = 4096 * P
        nt = 8 * sb + 8
        if sb == 0 and hq8 == 0:
            if jp > 0:
                pmm.wait(sy, v=at("pmm", i - 1))
            ld.inc(sy.dma_start(out=Kp[:],
                                in_=k1_ap[jp * 128:(jp + 1) * 128, :]), 16)
        if i >= 2:
            pmm.wait(sy, v=at("pmm", i - 1))
        lq.inc(sy.dma_start(
            out=qt[gi * 64:(gi + 1) * 64, P * 512:(P + 1) * 512],
            in_=q1_ap[hq * 64:(hq + 1) * 64, sb * 512:(sb + 1) * 512]), 16)

        ld.wait(te)
        lq.wait(te)
        nun = nt // 2
        gu0 = snap.get(("gun", i - 1), 0)
        for u in range(nun):
            G = gu0 + u
            reg = G % 3
            wv_ = G - 2
            if wv_ > 0:
                aex.wait(te, v=wv_)
            last = None
            for tj in range(2):
                jj = u * 2 + tj
                own = jj < nt // 2
                col = jj * 128 if own else 2048 + (jj - nt // 2) * 128
                last = te.matmul(
                    k.bank(2 * reg + tj),
                    Kp[gi * 64:(gi + 1) * 64, col:col + 128],
                    qt[gi * 64:(gi + 1) * 64, P * 512:(P + 1) * 512],
                    start=True, stop=True)
            pmm.inc(last)
        put("pmm", i, pmm)
        snap[("gun", i)] = gu0 + nun

        for u in range(nun):
            G = gu0 + u
            reg = G % 3
            pmm.wait(se, v=at("pmm", i) - nun + u + 1)
            if i >= 2 and u == 0:
                if ("pav", i - 2) in snap:
                    pav.wait(se, v=at("pav", i - 2))
                vmk.wait(se, v=at("vmk", i - 2))
            aex.inc(se.activation(
                E1[:, eP + u * 1024: eP + (u + 1) * 1024],
                k.bank(2 * reg, nb=2), AF.Exp,
                bias=biasB[:, hq:hq + 1], scale=float(1.0 / 8.0)))
        put("aexe", i, aex)

        if i >= 1 and ATT1_STAGE >= 2:
            emit_post(i - 1)
        if sb == 0 and hq8 == 0:
            if jp > 0 and ("pav", i - 1) in snap:
                pav.wait(ve, v=at("pav", i - 1))
            vms.inc(ve.memset(V65[:], 1.0))
            vms.wait(sy)
            for gi2 in range(2):
                g = jp * 2 + gi2
                dst = V65[:, gi2 * 2080:(gi2 + 1) * 2080].rearrange(
                    "p (c d) -> p c d", d=65)[:, :, 0:64]
                ld.inc(sy.dma_start(out=dst, in_=v1_ap[g * 128:(g + 1) * 128, :]
                                    .rearrange("p (c d) -> p c d", d=64)), 16)
            put("ldv", jp, ld)

        aex.wait(ve, v=at("aexe", i))
        if i >= 2 and ("pav", i - 2) in snap:
            pav.wait(ve, v=at("pav", i - 2))
        for m in range(8):
            own = m < 4
            jj = (nt // 2 - 4 + m) if own else (nt - 8 + m)
            last = ve.tensor_tensor(
                MB[:, mP + m * 512: mP + (m + 1) * 512],
                E1[:, eP + jj * 512: eP + (jj + 1) * 512],
                CM[:, (sb * 8 + m) * 512:(sb * 8 + m + 1) * 512], ALU.mult)
            if m < 7:
                k.chain("ve", last)
        vmk.inc(last)
        put("vmk", i, vmk)

    if ATT1_STAGE >= 2:
        emit_post(NIT - 1)
    sst.wait(sy)
    _les.close()


def emit_ffn1(k: KB, tag, xn_ap, rt_ap, wsg_ap, wgu_ap, wsd_ap, wegu_ap,
              wed_ap, id_ap, g1p_ap, w1fm_ap):
    nc, te, ve, se, sy, ge = k.nc, k.te, k.ve, k.se, k.sy, k.ge
    wl = k.sem(f"{tag}_wl")
    pl = k.sem(f"{tag}_pl")
    prt = k.sem(f"{tag}_prt")
    vrt = k.sem(f"{tag}_vrt")
    art = k.sem(f"{tag}_art")
    ptr = k.sem(f"{tag}_ptr")
    aev = k.sem(f"{tag}_aev")
    psg = k.sem(f"{tag}_psg")
    pgu = k.sem(f"{tag}_pgu")
    ahg = k.sem(f"{tag}_ahg")
    vh = k.sem(f"{tag}_vh")
    pdn = k.sem(f"{tag}_pdn")
    adn = k.sem(f"{tag}_adn")
    sst = k.sem(f"{tag}_sst")
    rdm = k.sem(f"{tag}_rdm")
    lfm = k.sem(f"{tag}_lfm")
    vsg = k.sem(f"{tag}_vsg")
    asg = k.sem(f"{tag}_asg")
    snap = {}
    from contextlib import ExitStack as _ES
    _les = _ES()

    def _sb(name, shape, dt):
        return _les.enter_context(nc.sbuf_tensor(name, shape, dt))

    xp = _sb(f"{tag}_xn", [128, NCH * 2048], BF16)
    hbuf = _sb(f"{tag}_h", [128, 45056], BF16)
    wg = _sb(f"{tag}_wg", [128, 2 * NCH * 128], BF16)
    wu = _sb(f"{tag}_wu", [128, 2 * NCH * 128], BF16)
    wd = _sb(f"{tag}_wd", [128, 2 * 5632], BF16)
    rt_sb = _sb(f"{tag}_rt", [128, NCH * 8], BF16)
    wsg_sb = _sb(f"{tag}_ws", [128, NCH], BF16)
    id_sb = _sb(f"{tag}_id", [128, 128], BF16)
    wfm = _sb(f"{tag}_wfm", [8, 128], BF16)
    sgB = _sb(f"{tag}_sgB", [128, 2048], BF16)
    ceB = sgB
    scr = _sb(f"{tag}_scr", [128, 2048], BF16)
    r1 = _sb(f"{tag}_r1", [128, 64], F32)[:]
    rb1 = _sb(f"{tag}_rb1", [128, 8], BF16)[:]
    srow = _sb(f"{tag}_srow", [1, 512], F32)[:]
    srb = _sb(f"{tag}_srb", [1, 512], BF16)[:]
    evf = _sb(f"{tag}_evf", [128, 2 * 512], F32)

    for fc in range(NCH):
        pl.inc(sy.dma_start(out=xp[:, fc * 2048:(fc + 1) * 2048],
                            in_=xn_ap[fc * 128:(fc + 1) * 128, 0:2048]), 16)
    wl.inc(ge.dma_start(out=rt_sb[:].rearrange("p (c f) -> p c f", c=NCH),
                        in_=rt_ap.rearrange("(c p) f -> p c f", p=128)), 16)
    wl.inc(ge.dma_start(out=wsg_sb[:], in_=wsg_ap), 16)
    wl.inc(ge.dma_start(out=id_sb[:], in_=id_ap), 16)
    wl.wait(te)
    pl.wait(te)

    for tt in range(16):
        if tt > 0:
            vrt.wait(te, v=snap[("vrt", tt - 1)])
            art.wait(te, v=snap[("art", tt - 1)])
        last = None
        for fc in range(NCH):
            last = te.matmul(k.bank(0)[:, 0:8],
                             xp[:, fc * 2048 + tt * 128:
                                fc * 2048 + (tt + 1) * 128],
                             rt_sb[:, fc * 8:(fc + 1) * 8],
                             start=(fc == 0), stop=(fc == NCH - 1))
        prt.inc(last)
        prt.wait(ve)
        e_t = r1[:, 0:8]
        probs = r1[:, 8:16]
        eq1 = r1[:, 16:24]
        pm = r1[:, 24:32]
        eq2 = r1[:, 32:40]
        eq1s = r1[:, 56:64]
        mx = r1[:, 48:49]
        mxn = r1[:, 49:50]
        sm = r1[:, 50:51]
        rs = r1[:, 51:52]
        m1 = r1[:, 52:53]
        m2 = r1[:, 53:54]
        k.chain("ve", ve.reduce_max(mx, k.bank(0)[:, 0:8], axis=AXX))
        vrt.inc(ve.tensor_scalar(mxn, mx, -1.0, None, op0=ALU.mult))
        vrt.wait(se)
        art.inc(se.activation(e_t, k.bank(0)[:, 0:8], AF.Exp, bias=mxn,
                              accum_out=sm))
        snap[("art", tt)] = art.v
        art.wait(ve)
        k.chain("ve", ve.reciprocal(rs, sm))
        k.chain("ve", ve.tensor_scalar(probs, e_t, rs, None, op0=ALU.mult))
        k.chain("ve", ve.reduce_max(m1, probs, axis=AXX))
        k.chain("ve", ve.tensor_scalar(eq1, probs, m1, None, op0=ALU.is_equal))
        k.chain("ve", ve.tensor_scalar(eq1s, eq1, -2.0, None, op0=ALU.mult))
        k.chain("ve", ve.tensor_tensor(pm, probs, eq1s, ALU.add))
        k.chain("ve", ve.reduce_max(m2, pm, axis=AXX))
        k.chain("ve", ve.tensor_scalar(eq2, probs, m2, None, op0=ALU.is_equal))
        k.chain("ve", ve.tensor_tensor(pm, eq1, eq2, ALU.add))
        vrt.inc(ve.tensor_tensor(rb1, probs, pm, ALU.mult))
        snap[("vrt", tt)] = vrt.v
        vrt.wait(te)
        if tt > 0:
            aev.wait(te, v=snap[("aevt", tt - 1)])
        ptr.inc(te.matmul(k.bank(1, rows=8)[:, 0:128], rb1, id_sb[:],
                          start=True, stop=True))
        ptr.wait(se)
        if tt > 0:
            sst.wait(se, v=snap[("wfmst", tt - 1)])
        aev.inc(se.copy(wfm[:], k.bank(1, rows=8)[:, 0:128]))
        snap[("aevt", tt)] = aev.v
        aev.wait(sy)
        sst.inc(sy.dma_start(out=w1fm_ap[:, tt * 128:(tt + 1) * 128],
                             in_=wfm[:]), 16)
        snap[("wfmst", tt)] = sst.v

    pbj = k.sem(f"{tag}_pbj")
    for j in range(4):
        if j > 0:
            vsg.wait(te, v=snap[("vsgj", j - 1)])
        last = None
        for fc in range(NCH):
            last = te.matmul(k.bank(2, rows=1), wsg_sb[:, fc:fc + 1],
                             xp[:, fc * 2048 + j * 512:fc * 2048 + j * 512 + 512],
                             start=(fc == 0), stop=(fc == NCH - 1))
        psg.inc(last)
        psg.wait(ve)
        if j > 0:
            asg.wait(ve, v=snap[("asgj", j - 1)])
        vsg.inc(ve.tensor_copy(srow, k.bank(2, rows=1)))
        snap[("vsgj", j)] = vsg.v
        vsg.wait(se)
        if j > 0:
            pbj.wait(se, v=snap[("pbjj", j - 1)])
        asg.inc(se.activation(srb, srow, AF.Sigmoid))
        snap[("asgj", j)] = asg.v
        asg.wait(te)
        if j > 0:
            aev.wait(te, v=snap[("aevj", j - 1)])
        pbj.inc(te.matmul(k.bank(3), k.ones_row[:], srb,
                          start=True, stop=True))
        snap[("pbjj", j)] = pbj.v
        pbj.wait(se)
        aev.inc(se.copy(sgB[:, j * 512:(j + 1) * 512], k.bank(3)))
        snap[("aevj", j)] = aev.v
    aev.wait(ve)
    aev.wait(te)

    state = {"git": 0, "dit": 0}

    def mlp_half(gu_ap, up_off, nip, dn_ap, scale_sb, r, accum, ekey):
        git0 = state["git"]
        tsl = slice(r * 1024, (r + 1) * 1024)
        for ip in range(nip):
            git = state["git"]
            wbuf = git % 2
            if git >= 2:
                pgu.wait(ge, v=snap[("pgu", git - 2)])
            for wdst, col0 in ((wg, ip * 128), (wu, up_off + ip * 128)):
                src = gu_ap[:, col0:col0 + 128]
                wl.inc(ge.dma_start(
                    out=wdst[:, wbuf * NCH * 128:(wbuf + 1) * NCH * 128]
                    .rearrange("p (c f) -> p c f", c=NCH),
                    in_=src.rearrange("(c p) f -> p c f", p=128)), 16)
            wseen = wl.v
            wl.wait(te, v=wseen)
            gb = (git % 2) * 4
            if git >= 2:
                vh.wait(te, v=snap[("vh", git - 2)])
                ahg.wait(te, v=snap[("ahg", git - 2)])
            if ip == 0 and ("hwar", ekey) in snap:
                pdn.wait(ve, v=snap[("hwar", ekey)])
            for bank0, wsrc in ((gb, wg), (gb + 2, wu)):
                last = None
                for th in range(2):
                    for fc in range(NCH):
                        last = te.matmul(
                            k.bank(bank0 + th),
                            wsrc[:, wbuf * NCH * 128 + fc * 128:
                                 wbuf * NCH * 128 + (fc + 1) * 128],
                            xp[:, fc * 2048 + r * 1024 + th * 512:
                               fc * 2048 + r * 1024 + (th + 1) * 512],
                            start=(fc == 0), stop=(fc == NCH - 1))
                pgu.inc(last)
            snap[("pgu", git)] = pgu.v
            pgu.wait(se, v=snap[("pgu", git)])
            hsl = slice(ip * 1024, (ip + 1) * 1024)
            if ("chm", git - 1) in snap:
                k.sem("chain_ve").wait(se, v=snap[("chm", git - 1)])
            ahg.inc(se.activation(scr[:, 0:1024], k.bank(gb, nb=2), AF.Silu))
            snap[("ahg", git)] = ahg.v
            ahg.wait(ve)
            k.chain("ve", ve.tensor_tensor(scr[:, 1024:2048], scr[:, 0:1024],
                                           k.bank(gb + 2, nb=2), ALU.mult))
            snap[("chm", git)] = k.sem("chain_ve").v
            vh.inc(ve.tensor_tensor(hbuf[:, hsl], scr[:, 1024:2048],
                                    scale_sb[:, tsl], ALU.mult))
            snap[("vh", git)] = vh.v
            state["git"] = git + 1
        vh.wait(te)

        def wd_load(oc):
            dwb = oc % 2
            if ("wdw", dwb) in snap:
                pdn.wait(ge, v=snap[("wdw", dwb)])
            wl.inc(ge.dma_start(
                out=wd[:, dwb * 5632: dwb * 5632 + nip * 128]
                .rearrange("p (c f) -> p c f", c=nip),
                in_=dn_ap[:, oc * 128:(oc + 1) * 128]
                .rearrange("(c p) f -> p c f", p=128)), 16)
            snap[("wds", oc)] = wl.v

        wd_load(0)
        for oc in range(NCH):
            dwb = oc % 2
            if oc + 1 < NCH:
                wd_load(oc + 1)
            wl.wait(te, v=snap[("wds", oc)])
            for th in range(2):
                dit = state["dit"]
                bank = 4 + dit % 2
                if dit >= 2:
                    adn.wait(te, v=snap[("adn", dit - 2)])
                last = None
                for ip in range(nip):
                    last = te.matmul(
                        k.bank(bank),
                        wd[:, dwb * 5632 + ip * 128:dwb * 5632 + (ip + 1) * 128],
                        hbuf[:, ip * 1024 + th * 512: ip * 1024 + th * 512 + 512],
                        start=(ip == 0), stop=(ip == nip - 1))
                pdn.inc(last)
                snap[("wdw", dwb)] = pdn.v
                pdn.wait(se, v=pdn.v)
                if dit >= 2:
                    sst.wait(se, v=snap[("dst", dit - 2)])
                    rdm.wait(se, v=snap[("drm", dit - 2)])
                adn.inc(se.copy(evf[:, (dit % 2) * 512:(dit % 2 + 1) * 512],
                                k.bank(bank)))
                snap[("adn", dit)] = adn.v
                reg = (oc, r * 2 + th)
                dst = g1p_ap[oc * 128:(oc + 1) * 128,
                             r * 1024 + th * 512: r * 1024 + (th + 1) * 512]
                srcv = evf[:, (dit % 2) * 512:(dit % 2 + 1) * 512]
                if accum:
                    adn.wait(ge, v=snap[("adn", dit)])
                    if ("g1w", reg) in snap:
                        rdm.wait(ge, v=snap[("g1w", reg)])
                    rdm.inc(ge.dma_start(out=dst, in_=srcv, accum_op=ALU.add), 16)
                    snap[("g1w", reg)] = rdm.v
                    snap[("drm", dit)] = rdm.v
                    snap[("dst", dit)] = sst.v
                else:
                    adn.wait(sy, v=snap[("adn", dit)])
                    sst.inc(sy.dma_start(out=dst, in_=srcv), 16)
                    snap[("dst", dit)] = sst.v
                    snap[("drm", dit)] = rdm.v
                state["dit"] = dit + 1
        snap[("hwar", ekey + 1)] = pdn.v

    snap[("hwar", 0)] = 0
    ek = 0
    for r in range(2):
        mlp_half(wgu_ap, 5632, 44, wsd_ap, sgB, r, False, ek)
        ek += 1
    sst.wait(ge)
    for e in range(8):
        vh.wait(te)
        aev.wait(te)
        for j in range(4):
            pbj.wait(sy)
            lfm.inc(sy.dma_start(out=srb, in_=w1fm_ap[e:e + 1,
                                                      j * 512:(j + 1) * 512]), 16)
            lfm.wait(te)
            pbj.inc(te.matmul(k.bank(3), k.ones_row[:], srb,
                              start=True, stop=True))
            pbj.wait(se)
            aev.inc(se.copy(ceB[:, j * 512:(j + 1) * 512], k.bank(3)))
            aev.wait(te)
        aev.wait(ve)
        for r in range(2):
            mlp_half(wegu_ap[e * 2048:(e + 1) * 2048, :], 1408, 11,
                     wed_ap[e * 1408:(e + 1) * 1408, :], ceB, r, True, ek)
            ek += 1
    rdm.wait(ge)
    sst.wait(sy)
    _les.close()


def _host_perm(parity):
    own = np.arange(parity, NTL, 2)
    other = np.arange(1 - parity, NTL, 2)
    tiles = np.concatenate([own, other])
    return (tiles[:, None] * 128 + np.arange(128)[None, :]).reshape(-1)


def build_program(dbg=(), phases=None):
    k = KB(dbg=dbg, phases=phases)
    k.constf(EPS_LN, rows=1)
    k.constf(EPS_RMS, rows=1)
    k.wait_init()
    x0 = k.din("x0", [D, NT], F32)
    if k.want("ln0"):
        xn = k.dram("xn0", [D, NT], BF16)
        emit_ln(k, "ln0", x0, xn)
        k.barrier()
    if k.want("qkv0"):
        xn = k.dram("xn0", [D, NT], BF16)
        wq = k.din("wq0", [D, 4096], F32)
        wk = k.din("wk0", [D, 4096], F32)
        wv = k.din("wv0", [D, D], F32)
        q0 = k.dram("q0", [4096, NT], BF16)
        k0 = k.dram("k0", [4096, NT], BF16)
        v0 = k.dram("v0", [D, NT], BF16)
        bnd = k.dram("bnd0", [1, 512], F32)
        emit_qkv0(k, "qkv0", xn, wq, wk, wv, q0, k0, v0, bnd)
        k.barrier()
    if k.want("att0"):
        q0 = k.dram("q0", [4096, NT], BF16)
        k0 = k.dram("k0", [4096, NT], BF16)
        v0 = k.dram("v0", [D, NT], BF16)
        bnd = k.dram("bnd0", [1, 512], F32)
        lam = k.din("lam0", [1, 16], F32)
        s0t = k.dram("s0t", [D, NT], BF16)
        factor = k.dram("factor0", [1, NT], F32)
        rms0 = k.dram("rms0", [1, NT], F32)
        emit_att0(k, "att0", q0, k0, v0, bnd, lam, s0t, factor, rms0)
        k.barrier()
    if k.want("wo0"):
        s0t = k.dram("s0t", [D, NT], BF16)
        factor = k.dram("factor0", [1, NT], F32)
        wo = k.din("wo0w", [D, D], F32)
        ns = k.din("ns0", [128, 16], F32)
        s0 = k.dram("s0", [D, NT], BF16)
        emit_wo0(k, "wo0", s0t, factor, wo, ns, s0)
        k.barrier()
    if k.want("ffn0"):
        xn = k.dram("xn0", [D, NT], BF16)
        wT = k.din("w0ffnT", [D, D], F32)
        aL = k.din("a0l", [D, 128], F32)
        bL = k.din("b0l", [128, D], F32)
        rt = k.din("rt0", [D, 8], F32)
        rep = k.din("rep", [8, 128], F32)
        ident = k.din("ident", [128, 128], F32)
        g0 = k.dram("g0", [D, NT], BF16)
        emit_ffn0(k, "ffn0", xn, wT, aL, bL, rt, rep, ident, g0)
        k.barrier()
    if k.want("cmb0"):
        x0_ = k.din("x0", [D, NT], F32)
        s0 = k.dram("s0", [D, NT], BF16)
        g0 = k.dram("g0", [D, NT], BF16)
        x1 = k.dram("x1", [D, NT], F32)
        emit_combine(k, "cmb0", x0_, s0, g0, x1)
        k.barrier()
    if k.want("ln1"):
        x1 = k.dram("x1", [D, NT], F32)
        xn1 = k.dram("xn1", [D, NT], BF16)
        emit_ln(k, "ln1", x1, xn1)
        k.barrier()
    if k.want("qkv1"):
        xn1 = k.dram("xn1", [D, NT], BF16)
        wq1 = k.din("wq1T", [D, D], F32)
        wk1 = k.din("wk1T", [D, 512], F32)
        wv1 = k.din("wv1T", [D, 512], F32)
        cosT = k.din("cos1", [128, NT], F32)
        sinT = k.din("sin1", [128, NT], F32)
        pswi = k.din("psw", [128, 128], F32)
        q1 = k.dram("q1", [D, NO], BF16)
        k1 = k.dram("k1", [512, NT], BF16)
        v1 = k.dram("v1", [1024, 2048], BF16)
        bnd1 = k.dram("bnd1", [2, 512], F32)
        emit_qkv1(k, "qkv1", xn1, wq1, wk1, wv1, cosT, sinT, pswi,
                  q1, k1, v1, bnd1)
        k.barrier()
    if k.want("att1"):
        q1 = k.dram("q1", [D, NO], BF16)
        k1 = k.dram("k1", [512, NT], BF16)
        v1 = k.dram("v1", [1024, 2048], BF16)
        bnd1 = k.dram("bnd1", [2, 512], F32)
        cm = k.din("cmask", [128, 4 * 8 * 512], F32)
        a1 = k.dram("a1", [D, NO], BF16)
        emit_att1(k, "att1", q1, k1, v1, bnd1, cm, a1)
        k.barrier()
    if k.want("wo1"):
        a1 = k.dram("a1", [D, NO], BF16)
        wo1 = k.din("wo1T", [D, D], F32)
        s1 = k.dram("s1", [D, NO], BF16)
        emit_wo0(k, "wo1", a1, None, wo1, None, s1, n_tok=NO,
                 use_ns=False, use_factor=False)
        k.barrier()
    if k.want("ffn1"):
        xn1 = k.dram("xn1", [D, NT], BF16)
        rt1 = k.din("rt1T", [D, 8], F32)
        wsg1 = k.din("wsg1T", [128, NCH], F32)
        wgu1 = k.din("wgu1", [D, 11264], F32)
        wsd1 = k.din("wsd1", [5632, D], F32)
        wegu1 = k.din("wegu1", [8 * D, 2816], F32)
        wed1 = k.din("wed1", [8 * 1408, D], F32)
        ident = k.din("ident", [128, 128], F32)
        g1p = k.dram("g1p", [D, NO], F32)
        w1fm = k.dram("w1fm", [8, 2048], BF16)
        emit_ffn1(k, "ffn1", xn1, rt1, wsg1, wgu1, wsd1, wegu1, wed1, ident,
                  g1p, w1fm)
        k.barrier()
    if k.want("cmb1"):
        x1 = k.dram("x1", [D, NT], F32)
        s1 = k.dram("s1", [D, NO], BF16)
        g1p = k.dram("g1p", [D, NO], F32)
        x2 = k.dout("out", [D, NO], F32)
        emit_combine(k, "cmb1", x1[:, 0:NO], s1, g1p, x2, n_tok=NO,
                     g_silu=True)
        k.barrier()
    return k


def _stage_weights(inputs):
    f32 = np.float32
    A = np.asarray(inputs["L0_ffn_lora_A"], f32)
    B = np.asarray(inputs["L0_ffn_lora_B"], f32)
    rep = np.zeros((8, 128), f32)
    for e in range(8):
        rep[e, e * 16:(e + 1) * 16] = 1.0
    psw = np.zeros((128, 128), f32)
    psw[np.arange(128), np.arange(128) ^ 1] = 1.0
    return {
        "wq0": np.asarray(inputs["L0_attn_Wq"], f32),
        "wk0": np.asarray(inputs["L0_attn_Wk"], f32),
        "wv0": np.asarray(inputs["L0_attn_Wv"], f32),
        "wo0w": np.asarray(inputs["L0_attn_Wo"], f32),
        "ns0": np.ascontiguousarray(
            np.asarray(inputs["L0_attn_norm_scale"], f32).reshape(16, 128).T),
        "lam0": np.asarray(inputs["L0_attn_lam"], f32).reshape(1, 16),
        "w0ffnT": np.ascontiguousarray(np.asarray(inputs["L0_ffn_weight"], f32).T),
        "a0l": np.ascontiguousarray(np.transpose(A, (2, 0, 1)).reshape(2048, 128)),
        "b0l": np.ascontiguousarray(np.transpose(B, (0, 2, 1)).reshape(128, 2048)),
        "rt0": np.ascontiguousarray(np.asarray(inputs["L0_ffn_router_w"], f32).T),
        "rep": rep,
        "ident": np.eye(128, dtype=f32),
        "psw": psw,
        "wq1T": np.ascontiguousarray(np.asarray(inputs["L1_attn_wq"], f32).T),
        "wk1T": np.ascontiguousarray(np.asarray(inputs["L1_attn_wk"], f32).T),
        "wv1T": np.ascontiguousarray(np.asarray(inputs["L1_attn_wv"], f32).T),
        "wo1T": np.ascontiguousarray(np.asarray(inputs["L1_attn_wo"], f32).T),
        "rt1T": np.ascontiguousarray(np.asarray(inputs["L1_ffn_W_router"], f32).T),
        "wsg1T": np.ascontiguousarray(
            np.asarray(inputs["L1_ffn_W_sgate"], f32).reshape(16, 128).T),
        "wgu1": np.asarray(inputs["L1_ffn_Ws_gate_up"], f32),
        "wsd1": np.asarray(inputs["L1_ffn_Ws_down"], f32),
        "wegu1": np.asarray(inputs["L1_ffn_We_gate_up"], f32).reshape(8 * 2048, 2816),
        "wed1": np.asarray(inputs["L1_ffn_We_down"], f32).reshape(8 * 1408, 2048),
    }


def _stage_percore(inputs, core):
    f32 = np.float32
    p, parity = core // 2, core % 2
    perm = _host_perm(parity)
    x = np.asarray(inputs["L0_attn_x"][p], f32)
    st = {"x0": np.ascontiguousarray(x.T[:, perm])}
    pos = perm.astype(np.float64)
    r = np.arange(128)
    theta = 10000.0 ** (-(2.0 * ((r % 64) // 2)) / 64.0)
    ang = pos[None, :] * theta[:, None]
    st["cos1"] = np.cos(ang).astype(f32)
    sgn = np.where(r % 2 == 0, -1.0, 1.0)
    st["sin1"] = (sgn[:, None] * np.sin(ang)).astype(f32)
    own_t = np.arange(parity, 32, 2)
    oth_t = np.arange(1 - parity, 32, 2)
    cm = np.zeros((4, 8, 128, 512), f32)
    for sb in range(4):
        for m in range(8):
            own = m < 4
            j = 4 * sb + (m if own else m - 4)
            keyg = (own_t if own else oth_t)[j]
            kpos = keyg * 128 + np.arange(128)
            for c in range(4):
                qg = own_t[4 * sb + c]
                qpos = qg * 128 + np.arange(128)
                cm[sb, m, :, c * 128:(c + 1) * 128] = (
                    kpos[:, None] <= qpos[None, :]).astype(f32)
    st["cmask"] = np.ascontiguousarray(
        cm.reshape(32, 128, 512).transpose(1, 0, 2).reshape(128, 32 * 512))
    return st, perm


def kernel(**inputs):
    k = build_program()
    names = set(k.inputs)
    w = _stage_weights(inputs)
    in_maps = []
    perms = []
    for core in range(8):
        st, perm = _stage_percore(inputs, core)
        st.update(w)
        in_maps.append({n: st[n] for n in names})
        perms.append(perm)
    res = run_bass_kernel_spmd(k.nc, in_maps, core_ids=list(range(8)))
    k.close()
    out = np.zeros((4, NT, D), np.float32)
    for core in range(8):
        p = core // 2
        o = np.asarray(res.results[core]["out"], np.float32)
        out[p, perms[core][:NO], :] = o.T
    return out
